# version 02
# baseline (speedup 1.0000x reference)
import numpy as np
import ml_dtypes
from contextlib import ExitStack

import concourse.bass as bass
import concourse.mybir as mybir
from concourse.bass_utils import run_bass_kernel_spmd

F32 = mybir.dt.float32
BF16 = mybir.dt.bfloat16
I8 = mybir.dt.int8
AF = mybir.ActivationFunctionType
ALU = mybir.AluOpType
BF = ml_dtypes.bfloat16

D = 2048
S = 2048
B = 4
HD = 128
NH = 8
NKV = 2
TOK = 1024
WIN = 1024
CAP = 50.0
LN_EPS = 1e-5
QK_EPS = 1e-6
SCALE = 1.0 / float(np.sqrt(HD))
FC = D // 128


def _rope_tables():
    half = HD // 2
    inv = 1.0 / 10000.0 ** (np.arange(half, dtype=np.float32) / half)
    ang = np.outer(np.arange(S, dtype=np.float32), inv)
    c = np.cos(ang).T.astype(np.float32)
    s = np.sin(ang).T.astype(np.float32)
    cos2 = np.concatenate([c, c], axis=0)
    sin2 = np.concatenate([s, s], axis=0)
    return np.ascontiguousarray(cos2), np.ascontiguousarray(sin2)


def _attn_sched(layer):
    tiles = []
    keys = {}
    sched = {}
    for qg in range(4):
        for c in range(16):
            q = qg * 512 + np.arange(512)[None, :]
            k = c * 128 + np.arange(128)[:, None]
            valid = (np.abs(q - k) <= WIN) if layer == 0 else (k <= q)
            if not valid.any():
                continue
            if valid.all():
                sched[(qg, c)] = None
            else:
                t = valid.astype(np.float32)
                key = t.tobytes()
                if key not in keys:
                    keys[key] = len(tiles)
                    tiles.append(t)
                sched[(qg, c)] = keys[key]
    arr = np.stack(tiles).astype(BF)
    return sched, arr



ENGS = ("sync", "tensor", "vector", "scalar", "gpsimd")


class Prog:
    def __init__(self, name):
        self.nc = bass.Bass("TRN2", target_bir_lowering=False, debug=False)
        self.name = name
        self.ops = {e: [] for e in ENGS}
        self.stack = ExitStack()
        self.sems = {}
        self.cnt = {}

    def sem(self, name):
        self.sems[name] = self.stack.enter_context(self.nc.semaphore(name))
        self.cnt[name] = 0
        return name

    def sbuf(self, name, shape, dt):
        return self.stack.enter_context(self.nc.sbuf_tensor(name, shape, dt))

    def psum(self, name, shape, dt):
        return self.stack.enter_context(self.nc.psum_tensor(name, shape, dt))

    def em(self, eng, fn):
        self.ops[eng].append(fn)

    def wait(self, eng, sem, val):
        if val <= 0:
            return
        h = self.sems[sem]
        self.em(eng, lambda e, h=h, v=val: e.wait_ge(h, v))

    def inc(self, sem):
        self.cnt[sem] += 1
        return self.sems[sem], self.cnt[sem]

    def dma(self, eng, out, in_, sem, slow=False):
        h = self.sems[sem]
        self.cnt[sem] += 16
        if slow:
            self.em(eng, lambda e, o=out, i=in_, h=h: e.dma_start(
                out=o, in_=i, allow_slow_non_contiguous=True).then_inc(h, 16))
        else:
            self.em(eng, lambda e, o=out, i=in_, h=h: e.dma_start(out=o, in_=i).then_inc(h, 16))
        return self.cnt[sem]

    def build(self):
        with self.nc.Block() as block:
            def runner(name):
                def body(eng):
                    for fn in self.ops[name]:
                        fn(eng)
                return body
            block.sync(runner("sync"))
            block.tensor(runner("tensor"))
            block.vector(runner("vector"))
            block.scalar(runner("scalar"))
            block.gpsimd(runner("gpsimd"))
        self.stack.close()
        return self.nc



def emit_ln_rows(p, rows, ps_s, ps_ss, ps_bc, a_bc, b_bc, ones_r1, n_feat, wait_mm_val, w=512):
    inv_n = 1.0 / n_feat
    r_v, r_mean, r_t1, r_rstd = (rows[k] for k in ("v", "mean", "t1", "rstd"))
    p.wait("vector", "s_m", wait_mm_val)
    p.em("vector", lambda e: e.tensor_scalar_mul(r_mean[0:1, 0:w], ps_s[0:1, 0:w], inv_n))
    p.em("vector", lambda e: e.tensor_scalar_mul(r_v[0:1, 0:w], ps_ss[0:1, 0:w], inv_n))
    p.em("vector", lambda e: e.tensor_mul(r_t1[0:1, 0:w], r_mean[0:1, 0:w], r_mean[0:1, 0:w]))
    p.em("vector", lambda e: e.tensor_sub(r_v[0:1, 0:w], r_v[0:1, 0:w], r_t1[0:1, 0:w]))
    p.em("vector", lambda e: e.tensor_scalar_add(r_v[0:1, 0:w], r_v[0:1, 0:w], LN_EPS))
    vv = p.inc("s_v")
    p.em("vector", lambda e, s=vv[0]: e.reciprocal(r_t1[0:1, 0:w], r_v[0:1, 0:w]).then_inc(s, 1))
    p.wait("scalar", "s_v", vv[1])
    sc1 = p.inc("s_s")
    p.em("scalar", lambda e, s=sc1[0]: e.activation(out=r_rstd[0:1, 0:w], in_=r_t1[0:1, 0:w],
                                                    func=AF.Sqrt).then_inc(s, 1))
    p.wait("vector", "s_s", sc1[1])
    p.em("vector", lambda e: e.tensor_mul(r_t1[0:1, 0:w], r_rstd[0:1, 0:w], r_rstd[0:1, 0:w]))
    p.em("vector", lambda e: e.scalar_tensor_tensor(r_t1[0:1, 0:w], r_v[0:1, 0:w], -0.5,
                                                    r_t1[0:1, 0:w], ALU.mult, ALU.mult))
    p.em("vector", lambda e: e.tensor_scalar_add(r_t1[0:1, 0:w], r_t1[0:1, 0:w], 1.5))
    vv1 = p.inc("s_v")
    p.em("vector", lambda e, s=vv1[0]: e.tensor_mul(r_rstd[0:1, 0:w], r_rstd[0:1, 0:w],
                                                    r_t1[0:1, 0:w]).then_inc(s, 1))
    p.wait("tensor", "s_v", vv1[1])
    mm1 = p.inc("s_m")
    p.em("tensor", lambda e, s=mm1[0]: e.matmul(ps_bc[:, 0:w], ones_r1[0:1, :], r_rstd[0:1, 0:w],
                                                start=True, stop=True).then_inc(s, 1))
    p.wait("vector", "s_m", mm1[1])
    p.em("vector", lambda e: e.tensor_copy(a_bc[:, 0:w], ps_bc[:, 0:w]))
    vv2 = p.inc("s_v")
    p.em("vector", lambda e, s=vv2[0]: e.tensor_scalar_mul(
        r_mean[0:1, 0:w], r_mean[0:1, 0:w], -1.0).then_inc(s, 1))
    p.wait("tensor", "s_v", vv2[1])
    mm2 = p.inc("s_m")
    p.em("tensor", lambda e, s=mm2[0]: e.matmul(ps_bc[:, 0:w], ones_r1[0:1, :], r_mean[0:1, 0:w],
                                                start=True, stop=True).then_inc(s, 1))
    p.wait("vector", "s_m", mm2[1])
    vv3 = p.inc("s_v")
    p.em("vector", lambda e, s=vv3[0]: e.tensor_copy(b_bc[:, 0:w], ps_bc[:, 0:w]).then_inc(s, 1))
    return vv3[1]



def build_a1(layer):
    sched, mask_arr = _attn_sched(layer)
    nm = mask_arr.shape[0]
    p = Prog(f"a1_l{layer}")
    nc = p.nc

    xT = nc.dram_tensor("xT", [D, S], F32, kind="ExternalInput").ap()
    wq = nc.dram_tensor("wq", [D, NH * HD], F32, kind="ExternalInput").ap()
    wk = nc.dram_tensor("wk", [D, NKV * HD], F32, kind="ExternalInput").ap()
    wv = nc.dram_tensor("wv", [D, NKV * HD], F32, kind="ExternalInput").ap()
    ropec = nc.dram_tensor("ropec", [128, S], F32, kind="ExternalInput").ap()
    ropes = nc.dram_tensor("ropes", [128, S], F32, kind="ExternalInput").ap()
    masks = nc.dram_tensor("masks", [nm, 128, 512], BF16, kind="ExternalInput").ap()
    identf = nc.dram_tensor("identf", [128, 128], F32, kind="ExternalInput").ap()
    if layer == 0:
        qsv = nc.dram_tensor("qs", [HD], F32, kind="ExternalInput").ap()
        ksv = nc.dram_tensor("ks", [HD], F32, kind="ExternalInput").ap()
    ot = nc.dram_tensor("ot", [NH * HD, S], BF16, kind="ExternalOutput").ap()

    xlnb = p.sbuf("xlnb", [128, FC * S], BF16)
    xbuf = p.sbuf("xbuf", [128, FC * 256], F32)
    xbb = p.sbuf("xbb", [128, 512], BF16)
    sqb = p.sbuf("sqb", [128, 512], BF16)
    stage = p.sbuf("stage", [128, 1024], F32)
    wqb = p.sbuf("wqb", [128, 2 * FC * HD], BF16)
    wkb = p.sbuf("wkb", [128, FC * NKV * HD], BF16)
    wvb = p.sbuf("wvb", [128, FC * NKV * HD], BF16)
    Kb = p.sbuf("Kb", [128, NKV * S], BF16)
    Vtok = p.sbuf("Vtok", [128, 16 * NKV * HD], BF16)
    tblc = p.sbuf("tblc", [128, S], F32)
    tbls = p.sbuf("tbls", [128, S], F32)
    maskb = p.sbuf("maskb", [128, nm * 512], BF16)
    idf = p.sbuf("idf", [128, 128], F32)
    kr = p.sbuf("kr", [128, 512], F32)
    k2 = p.sbuf("k2", [128, 512], F32)
    ktmp = p.sbuf("ktmp", [128, 512], F32)
    kf = p.sbuf("kf", [128, 512], F32)
    vf = p.sbuf("vf", [128, 512], F32)
    qb = p.sbuf("qb", [128, 2 * S], BF16)
    ptile = p.sbuf("ptile", [128, 2 * 512], BF16)
    pmk = p.sbuf("pmk", [128, 2 * 512], BF16)
    ttile = p.sbuf("ttile", [128, 512], F32)
    blb = p.sbuf("blb", [128, 512], F32)
    obuf = p.sbuf("obuf", [128, 2 * 512], BF16)
    a_bc = p.sbuf("a_bc", [128, 512], F32)
    b_bc = p.sbuf("b_bc", [128, 512], F32)
    ones_b = p.sbuf("ones_b", [128, 1], BF16)
    ones_r1 = p.sbuf("ones_r1", [1, 128], F32)
    biasb = p.sbuf("biasb", [128, 1], F32)
    rows = {k: p.sbuf("row_" + k, [1, 512], F32)
            for k in ("v", "mean", "t1", "rstd", "l")}
    if layer == 0:
        qsb = p.sbuf("qsb", [128, 1], F32)
        ksb = p.sbuf("ksb", [128, 1], F32)
        nrows = {k: p.sbuf("nrow_" + k, [1, 512], F32) for k in ("v", "t", "d")}

    ps = [p.psum(f"ps{i}", [128, 512], F32) for i in range(6)]

    for s in ("s_x", "s_w", "s_kc", "s_o0", "s_o1", "s_c", "s_m", "s_v", "s_s"):
        p.sem(s)

    p.dma("sync", tblc[:, :], ropec, "s_kc")
    p.dma("sync", tbls[:, :], ropes, "s_kc")
    p.dma("sync", maskb[:].rearrange("p (n m) -> p n m", n=nm),
          masks.rearrange("n p m -> p n m"), "s_kc")
    p.dma("sync", idf[:, :], identf, "s_kc")
    if layer == 0:
        p.dma("sync", qsb[:, :], qsv.rearrange("(p o) -> p o", o=1), "s_kc")
        p.dma("sync", ksb[:, :], ksv.rearrange("(p o) -> p o", o=1), "s_kc")
    w_const = p.cnt["s_kc"]
    p.wait("vector", "s_kc", w_const)
    p.em("vector", lambda e: e.memset(ones_b[:, :], 1.0))
    exp_bias = -CAP if layer == 0 else -100.0 * SCALE
    p.em("vector", lambda e, b=exp_bias: e.memset(biasb[:, :], b))
    vset = p.inc("s_v")
    p.em("vector", lambda e, s=vset[0]: e.memset(ones_r1[:, :], 1.0).then_inc(s, 1))
    p.wait("tensor", "s_v", vset[1])
    p.wait("scalar", "s_v", vset[1])
    p.wait("gpsimd", "s_v", vset[1])

    for fc in range(FC):
        wcur = p.dma("sync", stage[:, 0:256], wk[fc * 128:(fc + 1) * 128, :], "s_w")
        p.wait("gpsimd", "s_w", wcur)
        cc0 = p.inc("s_c")
        p.em("gpsimd", lambda e, fc=fc, s=cc0[0]: e.tensor_copy(
            wkb[:, fc * 256:(fc + 1) * 256], stage[:, 0:256]).then_inc(s, 1))
        p.wait("sync", "s_c", cc0[1])
        wcur = p.dma("sync", stage[:, 256:512], wv[fc * 128:(fc + 1) * 128, :], "s_w")
        p.wait("gpsimd", "s_w", wcur)
        cc1 = p.inc("s_c")
        p.em("gpsimd", lambda e, fc=fc, s=cc1[0]: e.tensor_copy(
            wvb[:, fc * 256:(fc + 1) * 256], stage[:, 256:512]).then_inc(s, 1))
        p.wait("sync", "s_c", cc1[1])
    kv_cast_done = p.cnt["s_c"]

    W0 = 256
    for tt in range(S // W0):
        t0 = tt * W0
        for fc in range(FC):
            p.dma("gpsimd", xbuf[:, fc * W0:(fc + 1) * W0],
                  xT[fc * 128:(fc + 1) * 128, t0:t0 + W0], "s_x")
        x_done = p.cnt["s_x"]
        p.wait("vector", "s_x", x_done)
        for fc in range(FC):
            vv = p.inc("s_v")
            p.em("vector", lambda e, fc=fc, s=vv[0]: e.tensor_copy(
                xbb[:, 0:W0], xbuf[:, fc * W0:(fc + 1) * W0]).then_inc(s, 1))
            p.wait("scalar", "s_v", vv[1])
            sc = p.inc("s_s")
            p.em("scalar", lambda e, s=sc[0]: e.activation(
                out=sqb[:, 0:W0], in_=xbb[:, 0:W0], func=AF.Square).then_inc(s, 1))
            p.wait("tensor", "s_s", sc[1])
            p.em("tensor", lambda e, fc=fc: e.matmul(
                ps[0][0:1, 0:W0], ones_b[:, :], xbb[:, 0:W0],
                start=(fc == 0), stop=(fc == FC - 1)))
            mmv = p.inc("s_m")
            p.em("tensor", lambda e, fc=fc, s=mmv[0]: e.matmul(
                ps[1][0:1, 0:W0], ones_b[:, :], sqb[:, 0:W0],
                start=(fc == 0), stop=(fc == FC - 1)).then_inc(s, 1))
            if fc < FC - 1:
                p.wait("vector", "s_m", p.cnt["s_m"])
        emit_ln_rows(p, rows, ps[0], ps[1], ps[2], a_bc, b_bc, ones_r1, D, p.cnt["s_m"], w=W0)
        for fc in range(FC):
            p.em("vector", lambda e, fc=fc: e.tensor_add(
                xbuf[:, fc * W0:(fc + 1) * W0], xbuf[:, fc * W0:(fc + 1) * W0], b_bc[:, 0:W0]))
            vv = p.inc("s_v")
            p.em("vector", lambda e, fc=fc, t0=t0, s=vv[0]: e.tensor_mul(
                xlnb[:, fc * S + t0: fc * S + t0 + W0],
                xbuf[:, fc * W0:(fc + 1) * W0], a_bc[:, 0:W0]).then_inc(s, 1))
        p.wait("gpsimd", "s_v", p.cnt["s_v"])
    xln_done_v = p.cnt["s_v"]

    def rope(src, t0, dst, col):
        p.em("vector", lambda e: e.tensor_copy(ktmp[0:64, :], src[64:128, :]))
        p.em("vector", lambda e: e.tensor_copy(ktmp[64:128, :], src[0:64, :]))
        p.em("vector", lambda e, t0=t0: e.tensor_mul(
            k2[:, :], ktmp[:, :], tbls[:, t0:t0 + 512]))
        p.em("vector", lambda e, t0=t0: e.tensor_mul(
            ktmp[:, :], src[:, :], tblc[:, t0:t0 + 512]))
        p.em("vector", lambda e: e.tensor_sub(dst[0:64, col:col + 512],
                                              ktmp[0:64, :], k2[0:64, :]))
        vv = p.inc("s_v")
        p.em("vector", lambda e, s=vv[0]: e.tensor_add(
            dst[64:128, col:col + 512], ktmp[64:128, :], k2[64:128, :]).then_inc(s, 1))
        return vv[1]

    def qknorm(src, dst_bf, dst_col, scale_ap):
        vv = p.inc("s_v")
        p.em("vector", lambda e, s=vv[0]: e.tensor_copy(xbb[:, :], src[:, :]).then_inc(s, 1))
        p.wait("scalar", "s_v", vv[1])
        sc = p.inc("s_s")
        p.em("scalar", lambda e, s=sc[0]: e.activation(
            out=sqb[:, :], in_=xbb[:, :], func=AF.Square).then_inc(s, 1))
        p.wait("tensor", "s_s", sc[1])
        mm = p.inc("s_m")
        p.em("tensor", lambda e, s=mm[0]: e.matmul(
            ps[3][0:1, :], ones_b[:, :], sqb[:, :], start=True, stop=True).then_inc(s, 1))
        nv, nt, nd = nrows["v"], nrows["t"], nrows["d"]
        p.wait("vector", "s_m", mm[1])
        p.em("vector", lambda e: e.tensor_copy(nv[0:1, :], ps[3][0:1, :]))
        vv = p.inc("s_v")
        p.em("vector", lambda e, s=vv[0]: e.reciprocal(nt[0:1, :], nv[0:1, :]).then_inc(s, 1))
        p.wait("scalar", "s_v", vv[1])
        sc = p.inc("s_s")
        p.em("scalar", lambda e, s=sc[0]: e.activation(
            out=nd[0:1, :], in_=nt[0:1, :], func=AF.Sqrt).then_inc(s, 1))
        p.wait("vector", "s_s", sc[1])
        p.em("vector", lambda e: e.tensor_mul(nt[0:1, :], nd[0:1, :], nd[0:1, :]))
        p.em("vector", lambda e: e.scalar_tensor_tensor(nt[0:1, :], nv[0:1, :], -0.5,
                                                        nt[0:1, :], ALU.mult, ALU.mult))
        p.em("vector", lambda e: e.tensor_scalar_add(nt[0:1, :], nt[0:1, :], 1.5))
        p.em("vector", lambda e: e.tensor_mul(nd[0:1, :], nd[0:1, :], nt[0:1, :]))
        p.em("vector", lambda e: e.tensor_mul(nd[0:1, :], nv[0:1, :], nd[0:1, :]))
        p.em("vector", lambda e: e.tensor_scalar_add(nd[0:1, :], nd[0:1, :], QK_EPS))
        vv = p.inc("s_v")
        p.em("vector", lambda e, s=vv[0]: e.reciprocal(
            nt[0:1, :], nd[0:1, :]).then_inc(s, 1))
        p.wait("tensor", "s_v", vv[1])
        mm = p.inc("s_m")
        p.em("tensor", lambda e, s=mm[0]: e.matmul(
            ps[2][:, :], ones_r1[0:1, :], nt[0:1, :], start=True, stop=True).then_inc(s, 1))
        p.wait("vector", "s_m", mm[1])
        vv = p.inc("s_v")
        p.em("vector", lambda e, s=vv[0]: e.scalar_tensor_tensor(
            dst_bf[:, dst_col:dst_col + 512], src[:, :], scale_ap,
            ps[2][:, :], ALU.mult, ALU.mult).then_inc(s, 1))
        return vv[1]

    p.wait("tensor", "s_c", kv_cast_done)
    p.wait("tensor", "s_v", xln_done_v)
    for g in range(NKV):
        for tt in range(4):
            t0 = tt * 512
            for fc in range(FC):
                p.em("tensor", lambda e, fc=fc, g=g, t0=t0: e.matmul(
                    ps[4][:, :], wkb[:, fc * 256 + g * 128: fc * 256 + g * 128 + 128],
                    xlnb[:, fc * S + t0: fc * S + t0 + 512],
                    start=(fc == 0), stop=(fc == FC - 1)))
            for fc in range(FC):
                last = fc == FC - 1
                if last:
                    mmv = p.inc("s_m")
                    p.em("tensor", lambda e, fc=fc, g=g, t0=t0, s=mmv[0]: e.matmul(
                        ps[5][:, :], wvb[:, fc * 256 + g * 128: fc * 256 + g * 128 + 128],
                        xlnb[:, fc * S + t0: fc * S + t0 + 512],
                        start=(fc == 0), stop=True).then_inc(s, 1))
                else:
                    p.em("tensor", lambda e, fc=fc, g=g, t0=t0: e.matmul(
                        ps[5][:, :], wvb[:, fc * 256 + g * 128: fc * 256 + g * 128 + 128],
                        xlnb[:, fc * S + t0: fc * S + t0 + 512],
                        start=(fc == 0), stop=False))
            kv_mm = p.cnt["s_m"]
            p.wait("vector", "s_m", kv_mm)
            p.em("vector", lambda e: e.tensor_copy(kr[:, :], ps[4][:, :]))
            vv = p.inc("s_v")
            p.em("vector", lambda e, s=vv[0]: e.tensor_copy(vf[:, :], ps[5][:, :]).then_inc(s, 1))
            evac_v = vv[1]
            if layer == 0:
                rope(kr, t0, kf, 0)
                qknorm(kf, Kb, g * S + t0, ksb[:, 0:1])
            else:
                rope(kr, t0, Kb, g * S + t0)
            p.wait("tensor", "s_v", evac_v)
            for st in range(4):
                mmv = p.inc("s_m")
                p.em("tensor", lambda e, st=st, s=mmv[0]: e.transpose(
                    ps[3][:, 0:128], vf[:, st * 128:(st + 1) * 128], idf[:, :]).then_inc(s, 1))
                p.wait("vector", "s_m", mmv[1])
                ci = tt * 4 + st
                vv = p.inc("s_v")
                p.em("vector", lambda e, ci=ci, g=g, s=vv[0]: e.tensor_copy(
                    Vtok[:, ci * 256 + g * 128: ci * 256 + g * 128 + 128],
                    ps[3][:, 0:128]).then_inc(s, 1))
                p.wait("tensor", "s_v", vv[1])
            p.wait("tensor", "s_v", p.cnt["s_v"])

    sc_tanh = SCALE / CAP
    wq_cast_at = [0, 0]
    qmm_after = [0, 0]
    stage_free_at = kv_cast_done
    for h in range(NH):
        hb = h % 2
        g = h // 4
        p.wait("gpsimd", "s_m", qmm_after[hb])
        for half in range(2):
            p.wait("sync", "s_c", stage_free_at)
            for fi in range(8):
                fc = half * 8 + fi
                p.dma("sync", stage[:, fi * 128:(fi + 1) * 128],
                      wq[fc * 128:(fc + 1) * 128, h * 128:(h + 1) * 128], "s_w")
            p.wait("gpsimd", "s_w", p.cnt["s_w"])
            cc = p.inc("s_c")
            p.em("gpsimd", lambda e, hb=hb, half=half, s=cc[0]: e.tensor_copy(
                wqb[:, hb * 2048 + half * 1024: hb * 2048 + (half + 1) * 1024],
                stage[:, :]).then_inc(s, 1))
            stage_free_at = cc[1]
        p.wait("tensor", "s_c", p.cnt["s_c"])
        for tt in range(4):
            t0 = tt * 512
            bank = 4 + (tt % 2)
            for fc in range(FC):
                last = fc == FC - 1
                if last:
                    mmv = p.inc("s_m")
                    p.em("tensor", lambda e, fc=fc, hb=hb, t0=t0, bank=bank, s=mmv[0]: e.matmul(
                        ps[bank][:, :], wqb[:, hb * 2048 + fc * 128: hb * 2048 + fc * 128 + 128],
                        xlnb[:, fc * S + t0: fc * S + t0 + 512],
                        start=(fc == 0), stop=True).then_inc(s, 1))
                else:
                    p.em("tensor", lambda e, fc=fc, hb=hb, t0=t0, bank=bank: e.matmul(
                        ps[bank][:, :], wqb[:, hb * 2048 + fc * 128: hb * 2048 + fc * 128 + 128],
                        xlnb[:, fc * S + t0: fc * S + t0 + 512],
                        start=(fc == 0), stop=False))
            p.wait("vector", "s_m", p.cnt["s_m"])
            vv = p.inc("s_v")
            p.em("vector", lambda e, bank=bank, s=vv[0]: e.tensor_copy(
                kr[:, :], ps[bank][:, :]).then_inc(s, 1))
            if layer == 0:
                rope(kr, t0, kf, 0)
                qknorm(kf, qb, hb * S + t0, qsb[:, 0:1])
            else:
                rope(kr, t0, qb, hb * S + t0)
            p.wait("tensor", "s_v", vv[1])
        qmm_after[hb] = p.cnt["s_m"]
        q_ready_v = p.cnt["s_v"]
        p.wait("tensor", "s_v", q_ready_v)

        for qg in range(4):
            q0 = qg * 512
            cs = [c for c in range(16) if (qg, c) in sched]
            ncs = len(cs)
            for ci, c in enumerate(cs):
                sb = ci % 2
                mmv = p.inc("s_m")
                p.em("tensor", lambda e, c=c, g=g, hb=hb, q0=q0, sb=sb, s=mmv[0]: e.matmul(
                    ps[4 + sb][:, :], Kb[:, g * S + c * 128: g * S + c * 128 + 128],
                    qb[:, hb * S + q0: hb * S + q0 + 512],
                    start=True, stop=True).then_inc(s, 1))
                p.wait("scalar", "s_m", mmv[1])
                if layer == 0:
                    p.em("scalar", lambda e, sb=sb: e.activation(
                        out=ttile[:, :], in_=ps[4 + sb][:, :],
                        func=AF.Tanh, scale=sc_tanh))
                    sc2 = p.inc("s_s")
                    p.em("scalar", lambda e, sb=sb, s=sc2[0]: e.activation(
                        out=ptile[:, sb * 512:(sb + 1) * 512], in_=ttile[:, :],
                        func=AF.Exp, scale=CAP, bias=biasb[:, 0:1]).then_inc(s, 1))
                else:
                    sc2 = p.inc("s_s")
                    p.em("scalar", lambda e, sb=sb, s=sc2[0]: e.activation(
                        out=ptile[:, sb * 512:(sb + 1) * 512], in_=ps[4 + sb][:, :],
                        func=AF.Exp, scale=SCALE, bias=biasb[:, 0:1]).then_inc(s, 1))
                mi = sched[(qg, c)]
                if mi is not None:
                    p.wait("vector", "s_m", p.cnt["s_m"])
                    p.wait("vector", "s_s", sc2[1])
                    vv = p.inc("s_v")
                    p.em("vector", lambda e, sb=sb, mi=mi, s=vv[0]: e.tensor_mul(
                        pmk[:, sb * 512:(sb + 1) * 512],
                        ptile[:, sb * 512:(sb + 1) * 512],
                        maskb[:, mi * 512:(mi + 1) * 512]).then_inc(s, 1))
                    p.wait("tensor", "s_v", vv[1])
                    src = pmk
                else:
                    p.wait("tensor", "s_s", sc2[1])
                    src = ptile
                p.em("tensor", lambda e, sb=sb, ci=ci, ncs=ncs, src=src: e.matmul(
                    ps[2][0:1, :], ones_b[:, :], src[:, sb * 512:(sb + 1) * 512],
                    start=(ci == 0), stop=(ci == ncs - 1)))
                mmv = p.inc("s_m")
                p.em("tensor", lambda e, sb=sb, ci=ci, ncs=ncs, c=c, g=g, src=src, s=mmv[0]:
                     e.matmul(ps[3][:, :], Vtok[:, c * 256 + g * 128: c * 256 + g * 128 + 128],
                              src[:, sb * 512:(sb + 1) * 512],
                              start=(ci == 0), stop=(ci == ncs - 1)).then_inc(s, 1))
                if ci + 2 < ncs:
                    p.wait("scalar", "s_m", p.cnt["s_m"])
            attn_mm = p.cnt["s_m"]
            p.wait("vector", "s_m", attn_mm)
            vv = p.inc("s_v")
            p.em("vector", lambda e, s=vv[0]: e.reciprocal(
                rows["l"][0:1, :], ps[2][0:1, :]).then_inc(s, 1))
            p.wait("tensor", "s_v", vv[1])
            mmv = p.inc("s_m")
            p.em("tensor", lambda e, s=mmv[0]: e.matmul(
                ps[2][:, :], ones_r1[0:1, :], rows["l"][0:1, :],
                start=True, stop=True).then_inc(s, 1))
            p.wait("vector", "s_m", mmv[1])
            p.em("vector", lambda e: e.tensor_copy(blb[:, :], ps[2][:, :]))
            ob = qg % 2
            if p.cnt[f"s_o{ob}"] > 0:
                p.wait("vector", f"s_o{ob}", p.cnt[f"s_o{ob}"])
            vv = p.inc("s_v")
            p.em("vector", lambda e, ob=ob, s=vv[0]: e.tensor_mul(
                obuf[:, ob * 512:(ob + 1) * 512], ps[3][:, :], blb[:, :]).then_inc(s, 1))
            p.wait("gpsimd", "s_v", vv[1])
            p.dma("gpsimd", ot[h * 128:(h + 1) * 128, q0:q0 + 512],
                  obuf[:, ob * 512:(ob + 1) * 512], f"s_o{ob}")
            p.wait("tensor", "s_v", p.cnt["s_v"])
    p.wait("gpsimd", "s_o0", p.cnt["s_o0"])
    p.wait("gpsimd", "s_o1", p.cnt["s_o1"])
    return p.build()



def build_a2():
    p = Prog("a2")
    nc = p.nc
    otf = nc.dram_tensor("otf", [D, TOK], BF16, kind="ExternalInput").ap()
    wo = nc.dram_tensor("wo", [D, D], F32, kind="ExternalInput").ap()
    xown = nc.dram_tensor("xown", [D, TOK], F32, kind="ExternalInput").ap()
    xo = nc.dram_tensor("xo", [D, TOK], F32, kind="ExternalOutput").ap()

    otb = p.sbuf("otb", [128, FC * TOK], BF16)
    stage = p.sbuf("stage", [128, 2 * 512], F32)
    wob = p.sbuf("wob", [128, 4 * FC * 512], BF16)
    sub = p.sbuf("sub", [128, FC * 512], F32)
    subb = p.sbuf("subb", [128, 512], BF16)
    sqb = p.sbuf("sqb", [128, 512], BF16)
    xin = p.sbuf("xin", [128, 2 * 512], F32)
    xout = p.sbuf("xout", [128, 2 * 512], F32)
    t1 = p.sbuf("t1", [128, 512], F32)
    a_bc = p.sbuf("a_bc", [128, 512], F32)
    b_bc = p.sbuf("b_bc", [128, 512], F32)
    ones_b = p.sbuf("ones_b", [128, 1], BF16)
    ones_r1 = p.sbuf("ones_r1", [1, 128], F32)
    rows = {k: p.sbuf("row_" + k, [1, 512], F32)
            for k in ("v", "mean", "t1", "rstd")}
    ps = [p.psum(f"ps{i}", [128, 512], F32) for i in range(6)]

    for s in ("s_ot", "s_p0", "s_p1", "s_x0", "s_x1", "s_o0", "s_o1",
              "s_c", "s_m", "s_v", "s_s"):
        p.sem(s)

    p.em("vector", lambda e: e.memset(ones_b[:, :], 1.0))
    vset = p.inc("s_v")
    p.em("vector", lambda e, s=vset[0]: e.memset(ones_r1[:, :], 1.0).then_inc(s, 1))
    p.wait("tensor", "s_v", vset[1])

    for fc in range(FC):
        p.dma("gpsimd", otb[:, fc * TOK:(fc + 1) * TOK],
              otf[fc * 128:(fc + 1) * 128, :], "s_ot")
    p.wait("tensor", "s_ot", p.cnt["s_ot"])

    cast_at = [0, 0]
    panel_ready = []
    for pi in range(4):
        for fc in range(FC):
            sbf = fc % 2
            p.wait("sync", "s_c", cast_at[sbf])
            wcur = p.dma("sync", stage[:, sbf * 512:(sbf + 1) * 512],
                         wo[fc * 128:(fc + 1) * 128, pi * 512:(pi + 1) * 512], f"s_p{sbf}")
            p.wait("gpsimd", f"s_p{sbf}", wcur)
            cc = p.inc("s_c")
            p.em("gpsimd", lambda e, fc=fc, pi=pi, sbf=sbf, s=cc[0]: e.tensor_copy(
                wob[:, pi * FC * 512 + fc * 512: pi * FC * 512 + (fc + 1) * 512],
                stage[:, sbf * 512:(sbf + 1) * 512]).then_inc(s, 1))
            cast_at[sbf] = cc[1]
        panel_ready.append(p.cnt["s_c"])

    for tt in range(2):
        t0 = tt * 512
        for pi in range(4):
            p.wait("tensor", "s_c", panel_ready[pi])
            for el in range(4):
                ec = pi * 4 + el
                bank = ec % 2
                for hdc in range(FC):
                    last = hdc == FC - 1
                    if last:
                        mmv = p.inc("s_m")
                        p.em("tensor", lambda e, hdc=hdc, pi=pi, el=el, t0=t0, bank=bank, s=mmv[0]:
                             e.matmul(ps[bank][:, :],
                                      wob[:, pi * FC * 512 + hdc * 512 + el * 128:
                                          pi * FC * 512 + hdc * 512 + el * 128 + 128],
                                      otb[:, hdc * TOK + t0: hdc * TOK + t0 + 512],
                                      start=(hdc == 0), stop=True).then_inc(s, 1))
                    else:
                        p.em("tensor", lambda e, hdc=hdc, pi=pi, el=el, t0=t0, bank=bank:
                             e.matmul(ps[bank][:, :],
                                      wob[:, pi * FC * 512 + hdc * 512 + el * 128:
                                          pi * FC * 512 + hdc * 512 + el * 128 + 128],
                                      otb[:, hdc * TOK + t0: hdc * TOK + t0 + 512],
                                      start=(hdc == 0), stop=False))
                p.wait("vector", "s_m", p.cnt["s_m"])
                p.em("vector", lambda e, ec=ec, bank=bank: e.tensor_copy(
                    sub[:, ec * 512:(ec + 1) * 512], ps[bank][:, :]))
                vv = p.inc("s_v")
                p.em("vector", lambda e, bank=bank, s=vv[0]: e.tensor_copy(
                    subb[:, :], ps[bank][:, :]).then_inc(s, 1))
                p.wait("scalar", "s_v", vv[1])
                sc = p.inc("s_s")
                p.em("scalar", lambda e, s=sc[0]: e.activation(
                    out=sqb[:, :], in_=subb[:, :], func=AF.Square).then_inc(s, 1))
                p.wait("tensor", "s_s", sc[1])
                p.em("tensor", lambda e, ec=ec: e.matmul(
                    ps[4][0:1, :], ones_b[:, :], subb[:, :],
                    start=(ec == 0), stop=(ec == FC - 1)))
                mmv = p.inc("s_m")
                p.em("tensor", lambda e, ec=ec, s=mmv[0]: e.matmul(
                    ps[5][0:1, :], ones_b[:, :], sqb[:, :],
                    start=(ec == 0), stop=(ec == FC - 1)).then_inc(s, 1))
                p.wait("vector", "s_m", p.cnt["s_m"])
        emit_ln_rows(p, rows, ps[4], ps[5], ps[2], a_bc, b_bc, ones_r1, D, p.cnt["s_m"])
        for fc in range(FC):
            xb_ = fc % 2
            xcur = p.dma("gpsimd", xin[:, xb_ * 512:(xb_ + 1) * 512],
                         xown[fc * 128:(fc + 1) * 128, t0:t0 + 512], f"s_x{xb_}")
            p.wait("vector", f"s_x{xb_}", xcur)
            p.em("vector", lambda e, fc=fc: e.tensor_add(
                t1[:, :], sub[:, fc * 512:(fc + 1) * 512], b_bc[:, :]))
            p.em("vector", lambda e: e.tensor_mul(t1[:, :], t1[:, :], a_bc[:, :]))
            if p.cnt[f"s_o{xb_}"] > 0:
                p.wait("vector", f"s_o{xb_}", p.cnt[f"s_o{xb_}"])
            vv = p.inc("s_v")
            p.em("vector", lambda e, xb_=xb_, s=vv[0]: e.tensor_add(
                xout[:, xb_ * 512:(xb_ + 1) * 512], t1[:, :],
                xin[:, xb_ * 512:(xb_ + 1) * 512]).then_inc(s, 1))
            p.wait("gpsimd", "s_v", vv[1])
            p.dma("gpsimd", xo[fc * 128:(fc + 1) * 128, t0:t0 + 512],
                  xout[:, xb_ * 512:(xb_ + 1) * 512], f"s_o{xb_}")
        p.wait("tensor", "s_v", p.cnt["s_v"])
    p.wait("gpsimd", "s_o0", p.cnt["s_o0"])
    p.wait("gpsimd", "s_o1", p.cnt["s_o1"])
    return p.build()



def build_ffn(layer):
    I = 8192 if layer == 0 else 5632
    WDT = I8 if layer == 0 else F32
    NI = I // 128
    NP = I // 512
    p = Prog(f"ffn_l{layer}")
    nc = p.nc
    xown = nc.dram_tensor("xown", [D, TOK], F32, kind="ExternalInput").ap()
    wg = nc.dram_tensor("wg", [D, I], WDT, kind="ExternalInput").ap()
    wu = nc.dram_tensor("wu", [D, I], WDT, kind="ExternalInput").ap()
    wd = nc.dram_tensor("wd", [I, D], WDT, kind="ExternalInput").ap()
    sg = nc.dram_tensor("sg", [128, NI], F32, kind="ExternalInput").ap()
    su = nc.dram_tensor("su", [128, NI], F32, kind="ExternalInput").ap()
    sd = nc.dram_tensor("sd", [128, FC], F32, kind="ExternalInput").ap()
    xo = nc.dram_tensor("xo", [D, TOK], F32, kind="ExternalOutput").ap()

    xlnb = p.sbuf("xlnb", [128, FC * TOK], BF16)
    fuse = p.sbuf("fuse", [128, NI * 512], BF16)
    stageg = p.sbuf("stageg", [128, 2 * 512], WDT)
    stageu = p.sbuf("stageu", [128, 2 * 512], WDT)
    staged = p.sbuf("staged", [128, 2 * 2048], WDT)
    wgb = p.sbuf("wgb", [128, FC * 512], BF16)
    wub = p.sbuf("wub", [128, FC * 512], BF16)
    wdb = p.sbuf("wdb", [128, 2 * 2048], BF16)
    sub = p.sbuf("sub", [128, FC * 512], F32)
    subb = p.sbuf("subb", [128, 512], BF16)
    sqb = p.sbuf("sqb", [128, 512], BF16)
    xbb = p.sbuf("xbb", [128, 512], BF16)
    xin = p.sbuf("xin", [128, 2 * 512], F32)
    xout = p.sbuf("xout", [128, 2 * 512], F32)
    gs = p.sbuf("gs", [128, 512], F32)
    us = p.sbuf("us", [128, 512], F32)
    t1 = p.sbuf("t1", [128, 512], F32)
    a_bc = p.sbuf("a_bc", [128, 512], F32)
    b_bc = p.sbuf("b_bc", [128, 512], F32)
    sgb = p.sbuf("sgb", [128, NI], F32)
    sub_s = p.sbuf("sub_s", [128, NI], F32)
    sdb = p.sbuf("sdb", [128, FC], F32)
    ones_b = p.sbuf("ones_b", [128, 1], BF16)
    ones_r1 = p.sbuf("ones_r1", [1, 128], F32)
    rows = {k: p.sbuf("row_" + k, [1, 512], F32)
            for k in ("v", "mean", "t1", "rstd")}
    ps = [p.psum(f"ps{i}", [128, 512], F32) for i in range(8)]

    for s in ("s_kc", "s_g0", "s_g1", "s_d0", "s_d1", "s_x0", "s_x1",
              "s_o0", "s_o1", "s_c", "s_m", "s_v", "s_s"):
        p.sem(s)

    p.em("vector", lambda e: e.memset(ones_b[:, :], 1.0))
    vset = p.inc("s_v")
    p.em("vector", lambda e, s=vset[0]: e.memset(ones_r1[:, :], 1.0).then_inc(s, 1))
    p.wait("tensor", "s_v", vset[1])

    p.dma("sync", sgb[:, :], sg, "s_kc")
    p.dma("sync", sub_s[:, :], su, "s_kc")
    sc_loaded = p.dma("sync", sdb[:, :], sd, "s_kc")
    p.wait("vector", "s_kc", sc_loaded)
    p.wait("scalar", "s_kc", sc_loaded)

    xin_free = [0, 0]

    def ln_pass(t0, out_to_xln):
        for fc in range(FC):
            xb_ = fc % 2
            p.wait("gpsimd", "s_v", xin_free[xb_])
            xcur = p.dma("gpsimd", xin[:, xb_ * 512:(xb_ + 1) * 512],
                         xown[fc * 128:(fc + 1) * 128, t0:t0 + 512], f"s_x{xb_}")
            p.wait("vector", f"s_x{xb_}", xcur)
            vv = p.inc("s_v")
            p.em("vector", lambda e, xb_=xb_, s=vv[0]: e.tensor_copy(
                xbb[:, :], xin[:, xb_ * 512:(xb_ + 1) * 512]).then_inc(s, 1))
            xin_free[xb_] = vv[1]
            p.wait("scalar", "s_v", vv[1])
            sc = p.inc("s_s")
            p.em("scalar", lambda e, s=sc[0]: e.activation(
                out=sqb[:, :], in_=xbb[:, :], func=AF.Square).then_inc(s, 1))
            p.wait("tensor", "s_s", sc[1])
            p.em("tensor", lambda e, fc=fc: e.matmul(
                ps[0][0:1, :], ones_b[:, :], xbb[:, :],
                start=(fc == 0), stop=(fc == FC - 1)))
            mmv = p.inc("s_m")
            p.em("tensor", lambda e, fc=fc, s=mmv[0]: e.matmul(
                ps[1][0:1, :], ones_b[:, :], sqb[:, :],
                start=(fc == 0), stop=(fc == FC - 1)).then_inc(s, 1))
            if fc < FC - 1:
                p.wait("vector", "s_m", p.cnt["s_m"])
        emit_ln_rows(p, rows, ps[0], ps[1], ps[2], a_bc, b_bc, ones_r1, D, p.cnt["s_m"])
        for fc in range(FC):
            xb_ = fc % 2
            p.wait("gpsimd", "s_v", xin_free[xb_])
            xcur = p.dma("gpsimd", xin[:, xb_ * 512:(xb_ + 1) * 512],
                         xown[fc * 128:(fc + 1) * 128, t0:t0 + 512], f"s_x{xb_}")
            p.wait("vector", f"s_x{xb_}", xcur)
            p.em("vector", lambda e, xb_=xb_: e.tensor_add(
                t1[:, :], xin[:, xb_ * 512:(xb_ + 1) * 512], b_bc[:, :]))
            vv = p.inc("s_v")
            p.em("vector", lambda e, fc=fc, t0=t0, s=vv[0]: e.tensor_mul(
                xlnb[:, fc * TOK + t0: fc * TOK + t0 + 512], t1[:, :],
                a_bc[:, :]).then_inc(s, 1))
            xin_free[xb_] = vv[1]

    for tt in range(2):
        ln_pass(tt * 512, True)
    xln_done = p.cnt["s_v"]
    p.wait("tensor", "s_v", xln_done)

    stg_cast_at = [0, 0]
    std_cast_at = [0, 0]
    mm_after_panel = 0
    for tt in range(2):
        t0 = tt * 512
        for pi in range(NP):
            for fc in range(FC):
                sbf = fc % 2
                p.wait("sync", "s_c", stg_cast_at[sbf])
                p.dma("sync", stageg[:, sbf * 512:(sbf + 1) * 512],
                      wg[fc * 128:(fc + 1) * 128, pi * 512:(pi + 1) * 512], f"s_g{sbf}")
                wcur = p.dma("sync", stageu[:, sbf * 512:(sbf + 1) * 512],
                             wu[fc * 128:(fc + 1) * 128, pi * 512:(pi + 1) * 512], f"s_g{sbf}")
                p.wait("gpsimd", f"s_g{sbf}", wcur)
                if fc == 0:
                    p.wait("gpsimd", "s_m", mm_after_panel)
                p.em("gpsimd", lambda e, fc=fc, sbf=sbf: e.tensor_copy(
                    wgb[:, fc * 512:(fc + 1) * 512], stageg[:, sbf * 512:(sbf + 1) * 512]))
                cc = p.inc("s_c")
                p.em("gpsimd", lambda e, fc=fc, sbf=sbf, s=cc[0]: e.tensor_copy(
                    wub[:, fc * 512:(fc + 1) * 512],
                    stageu[:, sbf * 512:(sbf + 1) * 512]).then_inc(s, 1))
                stg_cast_at[sbf] = cc[1]
            p.wait("tensor", "s_c", p.cnt["s_c"])
            for il in range(4):
                ic = pi * 4 + il
                for fc in range(FC):
                    p.em("tensor", lambda e, fc=fc, il=il, t0=t0: e.matmul(
                        ps[0][:, :], wgb[:, fc * 512 + il * 128: fc * 512 + il * 128 + 128],
                        xlnb[:, fc * TOK + t0: fc * TOK + t0 + 512],
                        start=(fc == 0), stop=(fc == FC - 1)))
                for fc in range(FC):
                    last = fc == FC - 1
                    if last:
                        mmv = p.inc("s_m")
                        p.em("tensor", lambda e, fc=fc, il=il, t0=t0, s=mmv[0]: e.matmul(
                            ps[1][:, :], wub[:, fc * 512 + il * 128: fc * 512 + il * 128 + 128],
                            xlnb[:, fc * TOK + t0: fc * TOK + t0 + 512],
                            start=(fc == 0), stop=True).then_inc(s, 1))
                    else:
                        p.em("tensor", lambda e, fc=fc, il=il, t0=t0: e.matmul(
                            ps[1][:, :], wub[:, fc * 512 + il * 128: fc * 512 + il * 128 + 128],
                            xlnb[:, fc * TOK + t0: fc * TOK + t0 + 512],
                            start=(fc == 0), stop=False))
                gu_mm = p.cnt["s_m"]
                p.wait("scalar", "s_m", gu_mm)
                sc = p.inc("s_s")
                p.em("scalar", lambda e, ic=ic, s=sc[0]: e.activation(
                    out=gs[:, :], in_=ps[0][:, :], func=AF.Silu,
                    scale=sgb[:, ic:ic + 1]).then_inc(s, 1))
                p.wait("vector", "s_m", gu_mm)
                p.em("vector", lambda e, ic=ic: e.tensor_scalar(
                    us[:, :], ps[1][:, :], sub_s[:, ic:ic + 1], None, ALU.mult))
                p.wait("vector", "s_s", sc[1])
                vv = p.inc("s_v")
                p.em("vector", lambda e, ic=ic, s=vv[0]: e.tensor_mul(
                    fuse[:, ic * 512:(ic + 1) * 512], gs[:, :], us[:, :]).then_inc(s, 1))
                p.wait("tensor", "s_v", vv[1])
            mm_after_panel = p.cnt["s_m"]
        fuse_done = p.cnt["s_v"]
        p.wait("tensor", "s_v", fuse_done)

        down_mm_at = {}
        for hg in range(2):
            for ic in range(NI):
                dbuf = ic % 2
                p.wait("sync", "s_c", std_cast_at[dbuf])
                wcur = p.dma("sync", staged[:, dbuf * 2048:(dbuf + 1) * 2048],
                             wd[ic * 128:(ic + 1) * 128, :], f"s_d{dbuf}")
                p.wait("gpsimd", f"s_d{dbuf}", wcur)
                key = (hg, ic - 2)
                if key in down_mm_at:
                    p.wait("gpsimd", "s_m", down_mm_at[key])
                elif hg == 1 and ic < 2:
                    p.wait("gpsimd", "s_m", down_mm_at[(0, NI - 2 + ic)])
                cc = p.inc("s_c")
                p.em("gpsimd", lambda e, dbuf=dbuf, s=cc[0]: e.tensor_copy(
                    wdb[:, dbuf * 2048:(dbuf + 1) * 2048],
                    staged[:, dbuf * 2048:(dbuf + 1) * 2048]).then_inc(s, 1))
                std_cast_at[dbuf] = cc[1]
                p.wait("tensor", "s_c", cc[1])
                for hl in range(8):
                    hc = hg * 8 + hl
                    last = ic == NI - 1
                    if hl == 7:
                        mmv = p.inc("s_m")
                        p.em("tensor", lambda e, dbuf=dbuf, hc=hc, hl=hl, ic=ic, last=last, s=mmv[0]:
                             e.matmul(ps[hl][:, :],
                                      wdb[:, dbuf * 2048 + hc * 128: dbuf * 2048 + hc * 128 + 128],
                                      fuse[:, ic * 512:(ic + 1) * 512],
                                      start=(ic == 0), stop=last).then_inc(s, 1))
                    else:
                        p.em("tensor", lambda e, dbuf=dbuf, hc=hc, hl=hl, ic=ic, last=last:
                             e.matmul(ps[hl][:, :],
                                      wdb[:, dbuf * 2048 + hc * 128: dbuf * 2048 + hc * 128 + 128],
                                      fuse[:, ic * 512:(ic + 1) * 512],
                                      start=(ic == 0), stop=last))
                down_mm_at[(hg, ic)] = p.cnt["s_m"]
            p.wait("scalar", "s_m", p.cnt["s_m"])
            for hl in range(8):
                hc = hg * 8 + hl
                sc = p.inc("s_s")
                p.em("scalar", lambda e, hl=hl, hc=hc, s=sc[0]: e.activation(
                    out=sub[:, hc * 512:(hc + 1) * 512], in_=ps[hl][:, :],
                    func=AF.Copy, scale=sdb[:, hc:hc + 1]).then_inc(s, 1))
            p.wait("tensor", "s_s", p.cnt["s_s"])
        evac_done = p.cnt["s_s"]

        p.wait("vector", "s_s", evac_done)
        for hc in range(FC):
            vv = p.inc("s_v")
            p.em("vector", lambda e, hc=hc, s=vv[0]: e.tensor_copy(
                subb[:, :], sub[:, hc * 512:(hc + 1) * 512]).then_inc(s, 1))
            p.wait("scalar", "s_v", vv[1])
            sc = p.inc("s_s")
            p.em("scalar", lambda e, s=sc[0]: e.activation(
                out=sqb[:, :], in_=subb[:, :], func=AF.Square).then_inc(s, 1))
            p.wait("tensor", "s_s", sc[1])
            p.em("tensor", lambda e, hc=hc: e.matmul(
                ps[0][0:1, :], ones_b[:, :], subb[:, :],
                start=(hc == 0), stop=(hc == FC - 1)))
            mmv = p.inc("s_m")
            p.em("tensor", lambda e, hc=hc, s=mmv[0]: e.matmul(
                ps[1][0:1, :], ones_b[:, :], sqb[:, :],
                start=(hc == 0), stop=(hc == FC - 1)).then_inc(s, 1))
            p.wait("vector", "s_m", p.cnt["s_m"])
        emit_ln_rows(p, rows, ps[0], ps[1], ps[2], a_bc, b_bc, ones_r1, D, p.cnt["s_m"])
        for fc in range(FC):
            xb_ = fc % 2
            p.wait("gpsimd", "s_v", xin_free[xb_])
            xcur = p.dma("gpsimd", xin[:, xb_ * 512:(xb_ + 1) * 512],
                         xown[fc * 128:(fc + 1) * 128, t0:t0 + 512], f"s_x{xb_}")
            p.wait("vector", f"s_x{xb_}", xcur)
            p.em("vector", lambda e, fc=fc: e.tensor_add(
                t1[:, :], sub[:, fc * 512:(fc + 1) * 512], b_bc[:, :]))
            p.em("vector", lambda e: e.tensor_mul(t1[:, :], t1[:, :], a_bc[:, :]))
            if p.cnt[f"s_o{xb_}"] > 0:
                p.wait("vector", f"s_o{xb_}", p.cnt[f"s_o{xb_}"])
            vv = p.inc("s_v")
            p.em("vector", lambda e, xb_=xb_, s=vv[0]: e.tensor_add(
                xout[:, xb_ * 512:(xb_ + 1) * 512], t1[:, :],
                xin[:, xb_ * 512:(xb_ + 1) * 512]).then_inc(s, 1))
            xin_free[xb_] = vv[1]
            p.wait("gpsimd", "s_v", vv[1])
            p.dma("gpsimd", xo[fc * 128:(fc + 1) * 128, t0:t0 + 512],
                  xout[:, xb_ * 512:(xb_ + 1) * 512], f"s_o{xb_}")
        p.wait("tensor", "s_v", p.cnt["s_v"])
    p.wait("gpsimd", "s_o0", p.cnt["s_o0"])
    p.wait("gpsimd", "s_o1", p.cnt["s_o1"])
    return p.build()



def _run(nc, in_maps):
    return run_bass_kernel_spmd(nc, in_maps, core_ids=list(range(8))).results


def _scale_2d(v, nchunk):
    return np.ascontiguousarray(v.reshape(nchunk, 128).T)


def run_a1(layer, xT, wq_, wk_, wv_, qs_, ks_):
    ropec, ropes = _rope_tables()
    identf = np.eye(128, dtype=np.float32)
    _, masks_ = _attn_sched(layer)
    nc1 = build_a1(layer)
    in_maps = []
    for c in range(8):
        b, r = c // 2, c % 2
        m = {"xT": xT[b],
             "wq": np.ascontiguousarray(wq_[:, r * 1024:(r + 1) * 1024]),
             "wk": np.ascontiguousarray(wk_[:, r * 256:(r + 1) * 256]),
             "wv": np.ascontiguousarray(wv_[:, r * 256:(r + 1) * 256]),
             "ropec": ropec, "ropes": ropes, "masks": masks_, "identf": identf}
        if layer == 0:
            m["qs"] = qs_
            m["ks"] = ks_
        in_maps.append(m)
    return [r_["ot"] for r_ in _run(nc1, in_maps)]


def run_a2(xT, ots, wo_):
    nc2 = build_a2()
    in_maps = []
    for c in range(8):
        b, r = c // 2, c % 2
        otf = np.concatenate([ots[2 * b][:, r * 1024:(r + 1) * 1024],
                              ots[2 * b + 1][:, r * 1024:(r + 1) * 1024]], axis=0)
        in_maps.append({"otf": np.ascontiguousarray(otf), "wo": wo_,
                        "xown": np.ascontiguousarray(xT[b][:, r * 1024:(r + 1) * 1024])})
    xos = [r_["xo"] for r_ in _run(nc2, in_maps)]
    return [np.concatenate([xos[2 * b], xos[2 * b + 1]], axis=1) for b in range(B)]


def attn_block(layer, xT, wq_, wk_, wv_, wo_, qs_, ks_):
    ots = run_a1(layer, xT, wq_, wk_, wv_, qs_, ks_)
    return run_a2(xT, ots, wo_)


def ffn_block(layer, xT, wg_, wu_, wd_, sg_, su_, sd_):
    nc = build_ffn(layer)
    I = wg_.shape[1]
    in_maps = []
    for c in range(8):
        b, r = c // 2, c % 2
        in_maps.append({"xown": np.ascontiguousarray(xT[b][:, r * 1024:(r + 1) * 1024]),
                        "wg": wg_, "wu": wu_, "wd": wd_,
                        "sg": _scale_2d(sg_, I // 128),
                        "su": _scale_2d(su_, I // 128),
                        "sd": _scale_2d(sd_, FC)})
    xos = [r_["xo"] for r_ in _run(nc, in_maps)]
    return [np.concatenate([xos[2 * b], xos[2 * b + 1]], axis=1) for b in range(B)]


def kernel(L0_attn_hidden_states, L0_attn_wq, L0_attn_wk, L0_attn_wv, L0_attn_wo,
           L0_attn_query_scale, L0_attn_key_scale,
           L0_ffn_w_gate, L0_ffn_w_up, L0_ffn_w_down,
           L0_ffn_s_gate, L0_ffn_s_up, L0_ffn_s_down,
           L1_attn_Wq, L1_attn_Wk, L1_attn_Wv, L1_attn_Wo,
           L1_ffn_gate_w, L1_ffn_up_w, L1_ffn_down_w,
           L1_ffn_gate_s, L1_ffn_up_s, L1_ffn_down_s):
    xT = [np.ascontiguousarray(L0_attn_hidden_states[b].T) for b in range(B)]
    xT = attn_block(0, xT, L0_attn_wq, L0_attn_wk, L0_attn_wv, L0_attn_wo,
                    L0_attn_query_scale, L0_attn_key_scale)
    xT = ffn_block(0, xT, L0_ffn_w_gate, L0_ffn_w_up, L0_ffn_w_down,
                   L0_ffn_s_gate, L0_ffn_s_up, L0_ffn_s_down)
    xT = attn_block(1, xT, L1_attn_Wq, L1_attn_Wk, L1_attn_Wv, L1_attn_Wo,
                    None, None)
    xT = ffn_block(1, xT,
                   np.ascontiguousarray(L1_ffn_gate_w.T),
                   np.ascontiguousarray(L1_ffn_up_w.T),
                   np.ascontiguousarray(L1_ffn_down_w.T),
                   L1_ffn_gate_s, L1_ffn_up_s, L1_ffn_down_s)
    return np.stack([xT[b].T for b in range(B)]).astype(np.float32)


# version 03
# speedup vs baseline: 1.0810x; 1.0810x over previous
import numpy as np
import ml_dtypes
from contextlib import ExitStack

import concourse.bass as bass
import concourse.mybir as mybir
from concourse.bass_utils import run_bass_kernel_spmd

F32 = mybir.dt.float32
BF16 = mybir.dt.bfloat16
I8 = mybir.dt.int8
AF = mybir.ActivationFunctionType
ALU = mybir.AluOpType
BF = ml_dtypes.bfloat16

D = 2048
S = 2048
B = 4
HD = 128
NH = 8
NKV = 2
TOK = 1024
WIN = 1024
CAP = 50.0
LN_EPS = 1e-5
QK_EPS = 1e-6
SCALE = 1.0 / float(np.sqrt(HD))
FC = D // 128


def _rope_tables():
    half = HD // 2
    inv = 1.0 / 10000.0 ** (np.arange(half, dtype=np.float32) / half)
    ang = np.outer(np.arange(S, dtype=np.float32), inv)
    c = np.cos(ang).T.astype(np.float32)
    s = np.sin(ang).T.astype(np.float32)
    cos2 = np.concatenate([c, c], axis=0)
    sin2 = np.concatenate([s, s], axis=0)
    return np.ascontiguousarray(cos2), np.ascontiguousarray(sin2)


def _attn_sched(layer):
    tiles = []
    keys = {}
    sched = {}
    for qg in range(4):
        for c in range(16):
            q = qg * 512 + np.arange(512)[None, :]
            k = c * 128 + np.arange(128)[:, None]
            valid = (np.abs(q - k) <= WIN) if layer == 0 else (k <= q)
            if not valid.any():
                continue
            if valid.all():
                sched[(qg, c)] = None
            else:
                t = valid.astype(np.float32)
                key = t.tobytes()
                if key not in keys:
                    keys[key] = len(tiles)
                    tiles.append(t)
                sched[(qg, c)] = keys[key]
    arr = np.stack(tiles).astype(BF)
    return sched, arr



ENGS = ("sync", "tensor", "vector", "scalar", "gpsimd")


class Prog:
    def __init__(self, name):
        self.nc = bass.Bass("TRN2", target_bir_lowering=False, debug=False)
        self.name = name
        self.ops = {e: [] for e in ENGS}
        self.stack = ExitStack()
        self.sems = {}
        self.cnt = {}

    def sem(self, name):
        self.sems[name] = self.stack.enter_context(self.nc.semaphore(name))
        self.cnt[name] = 0
        return name

    def sbuf(self, name, shape, dt):
        return self.stack.enter_context(self.nc.sbuf_tensor(name, shape, dt))

    def psum(self, name, shape, dt):
        return self.stack.enter_context(self.nc.psum_tensor(name, shape, dt))

    def em(self, eng, fn):
        self.ops[eng].append(fn)

    def wait(self, eng, sem, val):
        if val <= 0:
            return
        h = self.sems[sem]
        self.em(eng, lambda e, h=h, v=val: e.wait_ge(h, v))

    def inc(self, sem):
        self.cnt[sem] += 1
        return self.sems[sem], self.cnt[sem]

    def dma(self, eng, out, in_, sem, slow=False):
        h = self.sems[sem]
        self.cnt[sem] += 16
        if slow:
            self.em(eng, lambda e, o=out, i=in_, h=h: e.dma_start(
                out=o, in_=i, allow_slow_non_contiguous=True).then_inc(h, 16))
        else:
            self.em(eng, lambda e, o=out, i=in_, h=h: e.dma_start(out=o, in_=i).then_inc(h, 16))
        return self.cnt[sem]

    def build(self):
        with self.nc.Block() as block:
            def runner(name):
                def body(eng):
                    for fn in self.ops[name]:
                        fn(eng)
                return body
            block.sync(runner("sync"))
            block.tensor(runner("tensor"))
            block.vector(runner("vector"))
            block.scalar(runner("scalar"))
            block.gpsimd(runner("gpsimd"))
        self.stack.close()
        return self.nc



def emit_ln_rows(p, rows, ps_s, ps_ss, ps_bc, a_bc, b_bc, ones_r1, n_feat, wait_mm_val, w=512):
    inv_n = 1.0 / n_feat
    r_v, r_mean, r_t1, r_rstd = (rows[k] for k in ("v", "mean", "t1", "rstd"))
    p.wait("vector", "s_m", wait_mm_val)
    p.em("vector", lambda e: e.tensor_scalar_mul(r_mean[0:1, 0:w], ps_s[0:1, 0:w], inv_n))
    p.em("vector", lambda e: e.tensor_scalar_mul(r_v[0:1, 0:w], ps_ss[0:1, 0:w], inv_n))
    p.em("vector", lambda e: e.tensor_mul(r_t1[0:1, 0:w], r_mean[0:1, 0:w], r_mean[0:1, 0:w]))
    p.em("vector", lambda e: e.tensor_sub(r_v[0:1, 0:w], r_v[0:1, 0:w], r_t1[0:1, 0:w]))
    p.em("vector", lambda e: e.tensor_scalar_add(r_v[0:1, 0:w], r_v[0:1, 0:w], LN_EPS))
    vv = p.inc("s_v")
    p.em("vector", lambda e, s=vv[0]: e.reciprocal(r_t1[0:1, 0:w], r_v[0:1, 0:w]).then_inc(s, 1))
    p.wait("scalar", "s_v", vv[1])
    sc1 = p.inc("s_s")
    p.em("scalar", lambda e, s=sc1[0]: e.activation(out=r_rstd[0:1, 0:w], in_=r_t1[0:1, 0:w],
                                                    func=AF.Sqrt).then_inc(s, 1))
    p.wait("vector", "s_s", sc1[1])
    p.em("vector", lambda e: e.tensor_mul(r_t1[0:1, 0:w], r_rstd[0:1, 0:w], r_rstd[0:1, 0:w]))
    p.em("vector", lambda e: e.scalar_tensor_tensor(r_t1[0:1, 0:w], r_v[0:1, 0:w], -0.5,
                                                    r_t1[0:1, 0:w], ALU.mult, ALU.mult))
    p.em("vector", lambda e: e.tensor_scalar_add(r_t1[0:1, 0:w], r_t1[0:1, 0:w], 1.5))
    vv1 = p.inc("s_v")
    p.em("vector", lambda e, s=vv1[0]: e.tensor_mul(r_rstd[0:1, 0:w], r_rstd[0:1, 0:w],
                                                    r_t1[0:1, 0:w]).then_inc(s, 1))
    p.wait("tensor", "s_v", vv1[1])
    mm1 = p.inc("s_m")
    p.em("tensor", lambda e, s=mm1[0]: e.matmul(ps_bc[:, 0:w], ones_r1[0:1, :], r_rstd[0:1, 0:w],
                                                start=True, stop=True).then_inc(s, 1))
    p.wait("vector", "s_m", mm1[1])
    p.em("vector", lambda e: e.tensor_copy(a_bc[:, 0:w], ps_bc[:, 0:w]))
    vv2 = p.inc("s_v")
    p.em("vector", lambda e, s=vv2[0]: e.tensor_scalar_mul(
        r_mean[0:1, 0:w], r_mean[0:1, 0:w], -1.0).then_inc(s, 1))
    p.wait("tensor", "s_v", vv2[1])
    mm2 = p.inc("s_m")
    p.em("tensor", lambda e, s=mm2[0]: e.matmul(ps_bc[:, 0:w], ones_r1[0:1, :], r_mean[0:1, 0:w],
                                                start=True, stop=True).then_inc(s, 1))
    p.wait("vector", "s_m", mm2[1])
    vv3 = p.inc("s_v")
    p.em("vector", lambda e, s=vv3[0]: e.tensor_copy(b_bc[:, 0:w], ps_bc[:, 0:w]).then_inc(s, 1))
    return vv3[1]



def build_a1(layer):
    sched, mask_arr = _attn_sched(layer)
    nm = mask_arr.shape[0]
    p = Prog(f"a1_l{layer}")
    nc = p.nc

    xT = nc.dram_tensor("xT", [D, S], F32, kind="ExternalInput").ap()
    wq = nc.dram_tensor("wq", [D, NH * HD], F32, kind="ExternalInput").ap()
    wk = nc.dram_tensor("wk", [D, NKV * HD], F32, kind="ExternalInput").ap()
    wv = nc.dram_tensor("wv", [D, NKV * HD], F32, kind="ExternalInput").ap()
    ropec = nc.dram_tensor("ropec", [128, S], F32, kind="ExternalInput").ap()
    ropes = nc.dram_tensor("ropes", [128, S], F32, kind="ExternalInput").ap()
    masks = nc.dram_tensor("masks", [nm, 128, 512], BF16, kind="ExternalInput").ap()
    identf = nc.dram_tensor("identf", [128, 128], F32, kind="ExternalInput").ap()
    if layer == 0:
        qsv = nc.dram_tensor("qs", [HD], F32, kind="ExternalInput").ap()
        ksv = nc.dram_tensor("ks", [HD], F32, kind="ExternalInput").ap()
    ot = nc.dram_tensor("ot", [NH * HD, S], BF16, kind="ExternalOutput").ap()

    xlnb = p.sbuf("xlnb", [128, FC * S], BF16)
    xbuf = p.sbuf("xbuf", [128, FC * 256], F32)
    xbb = p.sbuf("xbb", [128, 512], BF16)
    sqb = p.sbuf("sqb", [128, 512], BF16)
    stage = p.sbuf("stage", [128, 1024], F32)
    wqb = p.sbuf("wqb", [128, 2 * FC * HD], BF16)
    wkb = p.sbuf("wkb", [128, FC * NKV * HD], BF16)
    wvb = p.sbuf("wvb", [128, FC * NKV * HD], BF16)
    Kb = p.sbuf("Kb", [128, NKV * S], BF16)
    Vtok = p.sbuf("Vtok", [128, 16 * NKV * HD], BF16)
    tblc = p.sbuf("tblc", [128, S], F32)
    tbls = p.sbuf("tbls", [128, S], F32)
    maskb = p.sbuf("maskb", [128, nm * 512], BF16)
    idf = p.sbuf("idf", [128, 128], F32)
    kr = p.sbuf("kr", [128, 512], F32)
    k2 = p.sbuf("k2", [128, 512], F32)
    ktmp = p.sbuf("ktmp", [128, 512], F32)
    kf = p.sbuf("kf", [128, 512], F32)
    vf = p.sbuf("vf", [128, 512], F32)
    qb = p.sbuf("qb", [128, 2 * S], BF16)
    ptile = p.sbuf("ptile", [128, 2 * 512], BF16)
    pmk = p.sbuf("pmk", [128, 2 * 512], BF16)
    ttile = p.sbuf("ttile", [128, 512], F32)
    blb = p.sbuf("blb", [128, 512], F32)
    obuf = p.sbuf("obuf", [128, 2 * 512], BF16)
    a_bc = p.sbuf("a_bc", [128, 512], F32)
    b_bc = p.sbuf("b_bc", [128, 512], F32)
    ones_b = p.sbuf("ones_b", [128, 1], BF16)
    ones_r1 = p.sbuf("ones_r1", [1, 128], F32)
    biasb = p.sbuf("biasb", [128, 1], F32)
    rows = {k: p.sbuf("row_" + k, [1, 512], F32)
            for k in ("v", "mean", "t1", "rstd", "l")}
    if layer == 0:
        qsb = p.sbuf("qsb", [128, 1], F32)
        ksb = p.sbuf("ksb", [128, 1], F32)
        nrows = {k: p.sbuf("nrow_" + k, [1, 512], F32) for k in ("v", "t", "d")}

    ps = [p.psum(f"ps{i}", [128, 512], F32) for i in range(8)]

    for s in ("s_x", "s_w", "s_kc", "s_o0", "s_o1", "s_c", "s_m", "s_v", "s_s"):
        p.sem(s)

    p.dma("sync", tblc[:, :], ropec, "s_kc")
    p.dma("sync", tbls[:, :], ropes, "s_kc")
    p.dma("sync", maskb[:].rearrange("p (n m) -> p n m", n=nm),
          masks.rearrange("n p m -> p n m"), "s_kc")
    p.dma("sync", idf[:, :], identf, "s_kc")
    if layer == 0:
        p.dma("sync", qsb[:, :], qsv.rearrange("(p o) -> p o", o=1), "s_kc")
        p.dma("sync", ksb[:, :], ksv.rearrange("(p o) -> p o", o=1), "s_kc")
    w_const = p.cnt["s_kc"]
    p.wait("vector", "s_kc", w_const)
    p.em("vector", lambda e: e.memset(ones_b[:, :], 1.0))
    exp_bias = -CAP if layer == 0 else -100.0 * SCALE
    p.em("vector", lambda e, b=exp_bias: e.memset(biasb[:, :], b))
    vset = p.inc("s_v")
    p.em("vector", lambda e, s=vset[0]: e.memset(ones_r1[:, :], 1.0).then_inc(s, 1))
    p.wait("tensor", "s_v", vset[1])
    p.wait("scalar", "s_v", vset[1])
    p.wait("gpsimd", "s_v", vset[1])

    for fc in range(FC):
        wcur = p.dma("sync", stage[:, 0:256], wk[fc * 128:(fc + 1) * 128, :], "s_w")
        p.wait("gpsimd", "s_w", wcur)
        cc0 = p.inc("s_c")
        p.em("gpsimd", lambda e, fc=fc, s=cc0[0]: e.tensor_copy(
            wkb[:, fc * 256:(fc + 1) * 256], stage[:, 0:256]).then_inc(s, 1))
        p.wait("sync", "s_c", cc0[1])
        wcur = p.dma("sync", stage[:, 256:512], wv[fc * 128:(fc + 1) * 128, :], "s_w")
        p.wait("gpsimd", "s_w", wcur)
        cc1 = p.inc("s_c")
        p.em("gpsimd", lambda e, fc=fc, s=cc1[0]: e.tensor_copy(
            wvb[:, fc * 256:(fc + 1) * 256], stage[:, 256:512]).then_inc(s, 1))
        p.wait("sync", "s_c", cc1[1])
    kv_cast_done = p.cnt["s_c"]

    W0 = 256
    for tt in range(S // W0):
        t0 = tt * W0
        for fc in range(FC):
            p.dma("gpsimd", xbuf[:, fc * W0:(fc + 1) * W0],
                  xT[fc * 128:(fc + 1) * 128, t0:t0 + W0], "s_x")
        x_done = p.cnt["s_x"]
        p.wait("vector", "s_x", x_done)
        for fc in range(FC):
            vv = p.inc("s_v")
            p.em("vector", lambda e, fc=fc, s=vv[0]: e.tensor_copy(
                xbb[:, 0:W0], xbuf[:, fc * W0:(fc + 1) * W0]).then_inc(s, 1))
            p.wait("scalar", "s_v", vv[1])
            sc = p.inc("s_s")
            p.em("scalar", lambda e, s=sc[0]: e.activation(
                out=sqb[:, 0:W0], in_=xbb[:, 0:W0], func=AF.Square).then_inc(s, 1))
            p.wait("tensor", "s_s", sc[1])
            p.em("tensor", lambda e, fc=fc: e.matmul(
                ps[0][0:1, 0:W0], ones_b[:, :], xbb[:, 0:W0],
                start=(fc == 0), stop=(fc == FC - 1)))
            mmv = p.inc("s_m")
            p.em("tensor", lambda e, fc=fc, s=mmv[0]: e.matmul(
                ps[1][0:1, 0:W0], ones_b[:, :], sqb[:, 0:W0],
                start=(fc == 0), stop=(fc == FC - 1)).then_inc(s, 1))
            if fc < FC - 1:
                p.wait("vector", "s_m", p.cnt["s_m"])
        emit_ln_rows(p, rows, ps[0], ps[1], ps[2], a_bc, b_bc, ones_r1, D, p.cnt["s_m"], w=W0)
        for fc in range(FC):
            p.em("vector", lambda e, fc=fc: e.tensor_add(
                xbuf[:, fc * W0:(fc + 1) * W0], xbuf[:, fc * W0:(fc + 1) * W0], b_bc[:, 0:W0]))
            vv = p.inc("s_v")
            p.em("vector", lambda e, fc=fc, t0=t0, s=vv[0]: e.tensor_mul(
                xlnb[:, fc * S + t0: fc * S + t0 + W0],
                xbuf[:, fc * W0:(fc + 1) * W0], a_bc[:, 0:W0]).then_inc(s, 1))
        p.wait("gpsimd", "s_v", p.cnt["s_v"])
    xln_done_v = p.cnt["s_v"]

    def rope(src, t0, dst, col):
        p.em("vector", lambda e: e.tensor_copy(ktmp[0:64, :], src[64:128, :]))
        p.em("vector", lambda e: e.tensor_copy(ktmp[64:128, :], src[0:64, :]))
        p.em("vector", lambda e, t0=t0: e.tensor_mul(
            k2[:, :], ktmp[:, :], tbls[:, t0:t0 + 512]))
        p.em("vector", lambda e, t0=t0: e.tensor_mul(
            ktmp[:, :], src[:, :], tblc[:, t0:t0 + 512]))
        p.em("vector", lambda e: e.tensor_sub(dst[0:64, col:col + 512],
                                              ktmp[0:64, :], k2[0:64, :]))
        vv = p.inc("s_v")
        p.em("vector", lambda e, s=vv[0]: e.tensor_add(
            dst[64:128, col:col + 512], ktmp[64:128, :], k2[64:128, :]).then_inc(s, 1))
        return vv[1]

    qkg = {"s": 0, "m": 0}

    def qknorm(src, dst_bf, dst_col, scale_ap, ps_ss, ps_bc):
        p.wait("vector", "s_s", qkg["s"])
        p.wait("vector", "s_m", qkg["m"])
        vv = p.inc("s_v")
        p.em("vector", lambda e, s=vv[0]: e.tensor_copy(xbb[:, :], src[:, :]).then_inc(s, 1))
        p.wait("scalar", "s_v", vv[1])
        sc = p.inc("s_s")
        p.em("scalar", lambda e, s=sc[0]: e.activation(
            out=sqb[:, :], in_=xbb[:, :], func=AF.Square).then_inc(s, 1))
        qkg["s"] = sc[1]
        p.wait("tensor", "s_s", sc[1])
        mm = p.inc("s_m")
        p.em("tensor", lambda e, ps_ss=ps_ss, s=mm[0]: e.matmul(
            ps_ss[0:1, :], ones_b[:, :], sqb[:, :], start=True, stop=True).then_inc(s, 1))
        nv, nt, nd = nrows["v"], nrows["t"], nrows["d"]
        p.wait("vector", "s_m", mm[1])
        p.em("vector", lambda e, ps_ss=ps_ss: e.tensor_copy(nv[0:1, :], ps_ss[0:1, :]))
        vv = p.inc("s_v")
        p.em("vector", lambda e, s=vv[0]: e.reciprocal(nt[0:1, :], nv[0:1, :]).then_inc(s, 1))
        p.wait("scalar", "s_v", vv[1])
        sc = p.inc("s_s")
        p.em("scalar", lambda e, s=sc[0]: e.activation(
            out=nd[0:1, :], in_=nt[0:1, :], func=AF.Sqrt).then_inc(s, 1))
        p.wait("vector", "s_s", sc[1])
        p.em("vector", lambda e: e.tensor_mul(nt[0:1, :], nd[0:1, :], nd[0:1, :]))
        p.em("vector", lambda e: e.scalar_tensor_tensor(nt[0:1, :], nv[0:1, :], -0.5,
                                                        nt[0:1, :], ALU.mult, ALU.mult))
        p.em("vector", lambda e: e.tensor_scalar_add(nt[0:1, :], nt[0:1, :], 1.5))
        p.em("vector", lambda e: e.tensor_mul(nd[0:1, :], nd[0:1, :], nt[0:1, :]))
        p.em("vector", lambda e: e.tensor_mul(nd[0:1, :], nv[0:1, :], nd[0:1, :]))
        p.em("vector", lambda e: e.tensor_scalar_add(nd[0:1, :], nd[0:1, :], QK_EPS))
        vv = p.inc("s_v")
        p.em("vector", lambda e, s=vv[0]: e.reciprocal(
            nt[0:1, :], nd[0:1, :]).then_inc(s, 1))
        p.wait("tensor", "s_v", vv[1])
        mm = p.inc("s_m")
        p.em("tensor", lambda e, ps_bc=ps_bc, s=mm[0]: e.matmul(
            ps_bc[:, :], ones_r1[0:1, :], nt[0:1, :], start=True, stop=True).then_inc(s, 1))
        qkg["m"] = mm[1]
        p.wait("vector", "s_m", mm[1])
        vv = p.inc("s_v")
        p.em("vector", lambda e, ps_bc=ps_bc, s=vv[0]: e.scalar_tensor_tensor(
            dst_bf[:, dst_col:dst_col + 512], src[:, :], scale_ap,
            ps_bc[:, :], ALU.mult, ALU.mult).then_inc(s, 1))
        return vv[1]

    p.wait("tensor", "s_c", kv_cast_done)
    p.wait("tensor", "s_v", xln_done_v)
    for g in range(NKV):
        for tt in range(4):
            t0 = tt * 512
            for fc in range(FC):
                p.em("tensor", lambda e, fc=fc, g=g, t0=t0: e.matmul(
                    ps[4][:, :], wkb[:, fc * 256 + g * 128: fc * 256 + g * 128 + 128],
                    xlnb[:, fc * S + t0: fc * S + t0 + 512],
                    start=(fc == 0), stop=(fc == FC - 1)))
            for fc in range(FC):
                last = fc == FC - 1
                if last:
                    mmv = p.inc("s_m")
                    p.em("tensor", lambda e, fc=fc, g=g, t0=t0, s=mmv[0]: e.matmul(
                        ps[5][:, :], wvb[:, fc * 256 + g * 128: fc * 256 + g * 128 + 128],
                        xlnb[:, fc * S + t0: fc * S + t0 + 512],
                        start=(fc == 0), stop=True).then_inc(s, 1))
                else:
                    p.em("tensor", lambda e, fc=fc, g=g, t0=t0: e.matmul(
                        ps[5][:, :], wvb[:, fc * 256 + g * 128: fc * 256 + g * 128 + 128],
                        xlnb[:, fc * S + t0: fc * S + t0 + 512],
                        start=(fc == 0), stop=False))
            kv_mm = p.cnt["s_m"]
            p.wait("vector", "s_m", kv_mm)
            p.em("vector", lambda e: e.tensor_copy(kr[:, :], ps[4][:, :]))
            vv = p.inc("s_v")
            p.em("vector", lambda e, s=vv[0]: e.tensor_copy(vf[:, :], ps[5][:, :]).then_inc(s, 1))
            evac_v = vv[1]
            if layer == 0:
                rope(kr, t0, kf, 0)
                qknorm(kf, Kb, g * S + t0, ksb[:, 0:1], ps[6], ps[7])
            else:
                rope(kr, t0, Kb, g * S + t0)
            p.wait("tensor", "s_v", evac_v)
            for st in range(4):
                mmv = p.inc("s_m")
                p.em("tensor", lambda e, st=st, s=mmv[0]: e.transpose(
                    ps[3][:, 0:128], vf[:, st * 128:(st + 1) * 128], idf[:, :]).then_inc(s, 1))
                p.wait("vector", "s_m", mmv[1])
                ci = tt * 4 + st
                vv = p.inc("s_v")
                p.em("vector", lambda e, ci=ci, g=g, s=vv[0]: e.tensor_copy(
                    Vtok[:, ci * 256 + g * 128: ci * 256 + g * 128 + 128],
                    ps[3][:, 0:128]).then_inc(s, 1))
                p.wait("tensor", "s_v", vv[1])
            p.wait("tensor", "s_v", p.cnt["s_v"])

    sc_tanh = SCALE / CAP
    qmm_after = [0, 0]
    qb_release = [0, 0]
    wq_cast_done = [0, 0]
    stg = {"free": kv_cast_done}

    def emit_wq_load(h):
        hb = h % 2
        p.wait("gpsimd", "s_m", qmm_after[hb])
        for half in range(2):
            p.wait("sync", "s_c", stg["free"])
            for fi in range(8):
                fc = half * 8 + fi
                p.dma("sync", stage[:, fi * 128:(fi + 1) * 128],
                      wq[fc * 128:(fc + 1) * 128, h * 128:(h + 1) * 128], "s_w")
            p.wait("gpsimd", "s_w", p.cnt["s_w"])
            cc = p.inc("s_c")
            p.em("gpsimd", lambda e, hb=hb, half=half, s=cc[0]: e.tensor_copy(
                wqb[:, hb * 2048 + half * 1024: hb * 2048 + (half + 1) * 1024],
                stage[:, :]).then_inc(s, 1))
            stg["free"] = cc[1]
        wq_cast_done[hb] = p.cnt["s_c"]

    def emit_qprep_tt(h, tt):
        hb = h % 2
        t0 = tt * 512
        bank = 4 + (tt % 2)
        if tt == 0:
            p.wait("tensor", "s_c", wq_cast_done[hb])
            p.wait("vector", "s_m", qb_release[hb])
        for fc in range(FC):
            last = fc == FC - 1
            if last:
                mmv = p.inc("s_m")
                p.em("tensor", lambda e, fc=fc, hb=hb, t0=t0, bank=bank, s=mmv[0]: e.matmul(
                    ps[bank][:, :], wqb[:, hb * 2048 + fc * 128: hb * 2048 + fc * 128 + 128],
                    xlnb[:, fc * S + t0: fc * S + t0 + 512],
                    start=(fc == 0), stop=True).then_inc(s, 1))
            else:
                p.em("tensor", lambda e, fc=fc, hb=hb, t0=t0, bank=bank: e.matmul(
                    ps[bank][:, :], wqb[:, hb * 2048 + fc * 128: hb * 2048 + fc * 128 + 128],
                    xlnb[:, fc * S + t0: fc * S + t0 + 512],
                    start=(fc == 0), stop=False))
        p.wait("vector", "s_m", p.cnt["s_m"])
        vv = p.inc("s_v")
        p.em("vector", lambda e, bank=bank, s=vv[0]: e.tensor_copy(
            kr[:, :], ps[bank][:, :]).then_inc(s, 1))
        if layer == 0:
            rope(kr, t0, kf, 0)
            qknorm(kf, qb, hb * S + t0, qsb[:, 0:1], ps[6], ps[7])
        else:
            rope(kr, t0, qb, hb * S + t0)
        p.wait("tensor", "s_v", vv[1])
        if tt == 3:
            qmm_after[hb] = p.cnt["s_m"]

    def emit_attn_qg(h, qg):
        hb = h % 2
        g = h // 4
        q0 = qg * 512
        cs = [c for c in range(16) if (qg, c) in sched]
        ncs = len(cs)
        st_mm = {}

        def emit_st(ci):
            c = cs[ci]
            sb = ci % 2
            mmv = p.inc("s_m")
            p.em("tensor", lambda e, c=c, g=g, hb=hb, q0=q0, sb=sb, s=mmv[0]: e.matmul(
                ps[sb][:, :], Kb[:, g * S + c * 128: g * S + c * 128 + 128],
                qb[:, hb * S + q0: hb * S + q0 + 512],
                start=True, stop=True).then_inc(s, 1))
            st_mm[ci] = mmv[1]

        emit_st(0)
        for ci, c in enumerate(cs):
            sb = ci % 2
            if ci + 1 < ncs:
                emit_st(ci + 1)
            p.wait("scalar", "s_m", st_mm[ci])
            if layer == 0:
                p.em("scalar", lambda e, sb=sb: e.activation(
                    out=ttile[:, :], in_=ps[sb][:, :], func=AF.Tanh, scale=sc_tanh))
                sc2 = p.inc("s_s")
                p.em("scalar", lambda e, sb=sb, s=sc2[0]: e.activation(
                    out=ptile[:, sb * 512:(sb + 1) * 512], in_=ttile[:, :],
                    func=AF.Exp, scale=CAP, bias=biasb[:, 0:1]).then_inc(s, 1))
            else:
                sc2 = p.inc("s_s")
                p.em("scalar", lambda e, sb=sb, s=sc2[0]: e.activation(
                    out=ptile[:, sb * 512:(sb + 1) * 512], in_=ps[sb][:, :],
                    func=AF.Exp, scale=SCALE, bias=biasb[:, 0:1]).then_inc(s, 1))
            mi = sched[(qg, c)]
            if mi is not None:
                p.wait("vector", "s_m", p.cnt["s_m"])
                p.wait("vector", "s_s", sc2[1])
                vv = p.inc("s_v")
                p.em("vector", lambda e, sb=sb, mi=mi, s=vv[0]: e.tensor_mul(
                    pmk[:, sb * 512:(sb + 1) * 512],
                    ptile[:, sb * 512:(sb + 1) * 512],
                    maskb[:, mi * 512:(mi + 1) * 512]).then_inc(s, 1))
                p.wait("tensor", "s_v", vv[1])
                src = pmk
            else:
                p.wait("tensor", "s_s", sc2[1])
                src = ptile
            p.em("tensor", lambda e, sb=sb, ci=ci, ncs=ncs, src=src: e.matmul(
                ps[2][0:1, :], ones_b[:, :], src[:, sb * 512:(sb + 1) * 512],
                start=(ci == 0), stop=(ci == ncs - 1)))
            mmv = p.inc("s_m")
            p.em("tensor", lambda e, sb=sb, ci=ci, ncs=ncs, c=c, g=g, src=src, s=mmv[0]:
                 e.matmul(ps[3][:, :], Vtok[:, c * 256 + g * 128: c * 256 + g * 128 + 128],
                          src[:, sb * 512:(sb + 1) * 512],
                          start=(ci == 0), stop=(ci == ncs - 1)).then_inc(s, 1))
            if ci + 2 < ncs:
                p.wait("scalar", "s_m", p.cnt["s_m"])
        attn_mm = p.cnt["s_m"]
        p.wait("vector", "s_m", attn_mm)
        vv = p.inc("s_v")
        p.em("vector", lambda e, s=vv[0]: e.reciprocal(
            rows["l"][0:1, :], ps[2][0:1, :]).then_inc(s, 1))
        p.wait("tensor", "s_v", vv[1])
        mmv = p.inc("s_m")
        p.em("tensor", lambda e, s=mmv[0]: e.matmul(
            ps[2][:, :], ones_r1[0:1, :], rows["l"][0:1, :],
            start=True, stop=True).then_inc(s, 1))
        p.wait("vector", "s_m", mmv[1])
        p.em("vector", lambda e: e.tensor_copy(blb[:, :], ps[2][:, :]))
        ob = qg % 2
        if p.cnt[f"s_o{ob}"] > 0:
            p.wait("vector", f"s_o{ob}", p.cnt[f"s_o{ob}"])
        vv = p.inc("s_v")
        p.em("vector", lambda e, ob=ob, s=vv[0]: e.tensor_mul(
            obuf[:, ob * 512:(ob + 1) * 512], ps[3][:, :], blb[:, :]).then_inc(s, 1))
        p.wait("gpsimd", "s_v", vv[1])
        p.dma("gpsimd", ot[h * 128:(h + 1) * 128, q0:q0 + 512],
              obuf[:, ob * 512:(ob + 1) * 512], f"s_o{ob}")
        p.wait("tensor", "s_v", p.cnt["s_v"])

    emit_wq_load(0)
    for tt in range(4):
        emit_qprep_tt(0, tt)
    for h in range(NH):
        if h + 1 < NH:
            emit_wq_load(h + 1)
        for qg in range(4):
            emit_attn_qg(h, qg)
            if h + 1 < NH:
                emit_qprep_tt(h + 1, qg)
        qb_release[h % 2] = p.cnt["s_m"]
    p.wait("gpsimd", "s_o0", p.cnt["s_o0"])
    p.wait("gpsimd", "s_o1", p.cnt["s_o1"])
    return p.build()



def build_a2():
    p = Prog("a2")
    nc = p.nc
    otf = nc.dram_tensor("otf", [D, TOK], BF16, kind="ExternalInput").ap()
    wo = nc.dram_tensor("wo", [D, D], F32, kind="ExternalInput").ap()
    xown = nc.dram_tensor("xown", [D, TOK], F32, kind="ExternalInput").ap()
    xo = nc.dram_tensor("xo", [D, TOK], F32, kind="ExternalOutput").ap()

    otb = p.sbuf("otb", [128, FC * TOK], BF16)
    stage = p.sbuf("stage", [128, 2 * 512], F32)
    wob = p.sbuf("wob", [128, 4 * FC * 512], BF16)
    sub = p.sbuf("sub", [128, FC * 512], F32)
    subb = p.sbuf("subb", [128, 512], BF16)
    sqb = p.sbuf("sqb", [128, 512], BF16)
    xin = p.sbuf("xin", [128, 2 * 512], F32)
    xout = p.sbuf("xout", [128, 2 * 512], F32)
    t1 = p.sbuf("t1", [128, 512], F32)
    a_bc = p.sbuf("a_bc", [128, 512], F32)
    b_bc = p.sbuf("b_bc", [128, 512], F32)
    ones_b = p.sbuf("ones_b", [128, 1], BF16)
    ones_r1 = p.sbuf("ones_r1", [1, 128], F32)
    rows = {k: p.sbuf("row_" + k, [1, 512], F32)
            for k in ("v", "mean", "t1", "rstd")}
    ps = [p.psum(f"ps{i}", [128, 512], F32) for i in range(6)]

    for s in ("s_ot", "s_p0", "s_p1", "s_x0", "s_x1", "s_o0", "s_o1",
              "s_c", "s_m", "s_v", "s_s"):
        p.sem(s)

    p.em("vector", lambda e: e.memset(ones_b[:, :], 1.0))
    vset = p.inc("s_v")
    p.em("vector", lambda e, s=vset[0]: e.memset(ones_r1[:, :], 1.0).then_inc(s, 1))
    p.wait("tensor", "s_v", vset[1])

    for fc in range(FC):
        p.dma("gpsimd", otb[:, fc * TOK:(fc + 1) * TOK],
              otf[fc * 128:(fc + 1) * 128, :], "s_ot")
    p.wait("tensor", "s_ot", p.cnt["s_ot"])

    cast_at = [0, 0]
    panel_ready = []
    for pi in range(4):
        for fc in range(FC):
            sbf = fc % 2
            p.wait("sync", "s_c", cast_at[sbf])
            wcur = p.dma("sync", stage[:, sbf * 512:(sbf + 1) * 512],
                         wo[fc * 128:(fc + 1) * 128, pi * 512:(pi + 1) * 512], f"s_p{sbf}")
            p.wait("gpsimd", f"s_p{sbf}", wcur)
            cc = p.inc("s_c")
            p.em("gpsimd", lambda e, fc=fc, pi=pi, sbf=sbf, s=cc[0]: e.tensor_copy(
                wob[:, pi * FC * 512 + fc * 512: pi * FC * 512 + (fc + 1) * 512],
                stage[:, sbf * 512:(sbf + 1) * 512]).then_inc(s, 1))
            cast_at[sbf] = cc[1]
        panel_ready.append(p.cnt["s_c"])

    for tt in range(2):
        t0 = tt * 512
        for pi in range(4):
            p.wait("tensor", "s_c", panel_ready[pi])
            for el in range(4):
                ec = pi * 4 + el
                bank = ec % 2
                for hdc in range(FC):
                    last = hdc == FC - 1
                    if last:
                        mmv = p.inc("s_m")
                        p.em("tensor", lambda e, hdc=hdc, pi=pi, el=el, t0=t0, bank=bank, s=mmv[0]:
                             e.matmul(ps[bank][:, :],
                                      wob[:, pi * FC * 512 + hdc * 512 + el * 128:
                                          pi * FC * 512 + hdc * 512 + el * 128 + 128],
                                      otb[:, hdc * TOK + t0: hdc * TOK + t0 + 512],
                                      start=(hdc == 0), stop=True).then_inc(s, 1))
                    else:
                        p.em("tensor", lambda e, hdc=hdc, pi=pi, el=el, t0=t0, bank=bank:
                             e.matmul(ps[bank][:, :],
                                      wob[:, pi * FC * 512 + hdc * 512 + el * 128:
                                          pi * FC * 512 + hdc * 512 + el * 128 + 128],
                                      otb[:, hdc * TOK + t0: hdc * TOK + t0 + 512],
                                      start=(hdc == 0), stop=False))
                p.wait("vector", "s_m", p.cnt["s_m"])
                p.em("vector", lambda e, ec=ec, bank=bank: e.tensor_copy(
                    sub[:, ec * 512:(ec + 1) * 512], ps[bank][:, :]))
                vv = p.inc("s_v")
                p.em("vector", lambda e, bank=bank, s=vv[0]: e.tensor_copy(
                    subb[:, :], ps[bank][:, :]).then_inc(s, 1))
                p.wait("scalar", "s_v", vv[1])
                sc = p.inc("s_s")
                p.em("scalar", lambda e, s=sc[0]: e.activation(
                    out=sqb[:, :], in_=subb[:, :], func=AF.Square).then_inc(s, 1))
                p.wait("tensor", "s_s", sc[1])
                p.em("tensor", lambda e, ec=ec: e.matmul(
                    ps[4][0:1, :], ones_b[:, :], subb[:, :],
                    start=(ec == 0), stop=(ec == FC - 1)))
                mmv = p.inc("s_m")
                p.em("tensor", lambda e, ec=ec, s=mmv[0]: e.matmul(
                    ps[5][0:1, :], ones_b[:, :], sqb[:, :],
                    start=(ec == 0), stop=(ec == FC - 1)).then_inc(s, 1))
                p.wait("vector", "s_m", p.cnt["s_m"])
        emit_ln_rows(p, rows, ps[4], ps[5], ps[2], a_bc, b_bc, ones_r1, D, p.cnt["s_m"])
        for fc in range(FC):
            xb_ = fc % 2
            xcur = p.dma("gpsimd", xin[:, xb_ * 512:(xb_ + 1) * 512],
                         xown[fc * 128:(fc + 1) * 128, t0:t0 + 512], f"s_x{xb_}")
            p.wait("vector", f"s_x{xb_}", xcur)
            p.em("vector", lambda e, fc=fc: e.tensor_add(
                t1[:, :], sub[:, fc * 512:(fc + 1) * 512], b_bc[:, :]))
            p.em("vector", lambda e: e.tensor_mul(t1[:, :], t1[:, :], a_bc[:, :]))
            if p.cnt[f"s_o{xb_}"] > 0:
                p.wait("vector", f"s_o{xb_}", p.cnt[f"s_o{xb_}"])
            vv = p.inc("s_v")
            p.em("vector", lambda e, xb_=xb_, s=vv[0]: e.tensor_add(
                xout[:, xb_ * 512:(xb_ + 1) * 512], t1[:, :],
                xin[:, xb_ * 512:(xb_ + 1) * 512]).then_inc(s, 1))
            p.wait("gpsimd", "s_v", vv[1])
            p.dma("gpsimd", xo[fc * 128:(fc + 1) * 128, t0:t0 + 512],
                  xout[:, xb_ * 512:(xb_ + 1) * 512], f"s_o{xb_}")
        p.wait("tensor", "s_v", p.cnt["s_v"])
    p.wait("gpsimd", "s_o0", p.cnt["s_o0"])
    p.wait("gpsimd", "s_o1", p.cnt["s_o1"])
    return p.build()



def build_ffn(layer):
    I = 8192 if layer == 0 else 5632
    WDT = I8 if layer == 0 else F32
    NI = I // 128
    NP = I // 256
    p = Prog(f"ffn_l{layer}")
    nc = p.nc
    xown = nc.dram_tensor("xown", [D, TOK], F32, kind="ExternalInput").ap()
    wg = nc.dram_tensor("wg", [D, I], WDT, kind="ExternalInput").ap()
    wu = nc.dram_tensor("wu", [D, I], WDT, kind="ExternalInput").ap()
    wd = nc.dram_tensor("wd", [I, D], WDT, kind="ExternalInput").ap()
    sg = nc.dram_tensor("sg", [128, NI], F32, kind="ExternalInput").ap()
    su = nc.dram_tensor("su", [128, NI], F32, kind="ExternalInput").ap()
    sd = nc.dram_tensor("sd", [128, FC], F32, kind="ExternalInput").ap()
    xo = nc.dram_tensor("xo", [D, TOK], F32, kind="ExternalOutput").ap()

    xlnb = p.sbuf("xlnb", [128, FC * TOK], BF16)
    fuse = p.sbuf("fuse", [128, NI * 512], BF16)
    stageg = p.sbuf("stageg", [128, 2 * 1024], WDT)
    stageu = p.sbuf("stageu", [128, 2 * 1024], WDT)
    staged = p.sbuf("staged", [128, 2 * 2048], WDT)
    wgb = p.sbuf("wgb", [128, 2 * FC * 256], BF16)
    wub = p.sbuf("wub", [128, 2 * FC * 256], BF16)
    wdb = p.sbuf("wdb", [128, 2 * 2048], BF16)
    sub = p.sbuf("sub", [128, FC * 512], F32)
    sqb = p.sbuf("sqb", [128, 2 * 512], BF16)
    xbb = p.sbuf("xbb", [128, 2 * 512], BF16)
    xin = p.sbuf("xin", [128, 2 * 512], F32)
    gs = p.sbuf("gs", [128, 2 * 512], F32)
    us = p.sbuf("us", [128, 512], F32)
    t1 = us
    a_bc = p.sbuf("a_bc", [128, 512], F32)
    b_bc = p.sbuf("b_bc", [128, 512], F32)
    sgb = p.sbuf("sgb", [128, NI], F32)
    sub_s = p.sbuf("sub_s", [128, NI], F32)
    sdb = p.sbuf("sdb", [128, FC], F32)
    ones_b = p.sbuf("ones_b", [128, 1], BF16)
    ones_r1 = p.sbuf("ones_r1", [1, 128], F32)
    rows = {k: p.sbuf("row_" + k, [1, 512], F32)
            for k in ("v", "mean", "t1", "rstd")}
    ps = [p.psum(f"ps{i}", [128, 512], F32) for i in range(8)]

    for s in ("s_kc", "s_g0", "s_g1", "s_d0", "s_d1", "s_x0", "s_x1",
              "s_o0", "s_o1", "s_c", "s_m", "s_v", "s_s"):
        p.sem(s)

    p.em("vector", lambda e: e.memset(ones_b[:, :], 1.0))
    vset = p.inc("s_v")
    p.em("vector", lambda e, s=vset[0]: e.memset(ones_r1[:, :], 1.0).then_inc(s, 1))
    p.wait("tensor", "s_v", vset[1])

    p.dma("sync", sgb[:, :], sg, "s_kc")
    p.dma("sync", sub_s[:, :], su, "s_kc")
    sc_loaded = p.dma("sync", sdb[:, :], sd, "s_kc")
    p.wait("vector", "s_kc", sc_loaded)
    p.wait("scalar", "s_kc", sc_loaded)

    xin_free = [0, 0]
    stats_mm_at = [0, 0]

    def stats_tile(fc, src_builder, w=512):
        pb = fc % 2
        p.wait("vector", "s_m", stats_mm_at[pb])
        vv = p.inc("s_v")
        p.em("vector", lambda e, pb=pb, s=vv[0]: src_builder(
            e, xbb[:, pb * 512:pb * 512 + w]).then_inc(s, 1))
        p.wait("scalar", "s_v", vv[1])
        sc = p.inc("s_s")
        p.em("scalar", lambda e, pb=pb, s=sc[0]: e.activation(
            out=sqb[:, pb * 512:pb * 512 + w], in_=xbb[:, pb * 512:pb * 512 + w],
            func=AF.Square).then_inc(s, 1))
        p.wait("tensor", "s_s", sc[1])
        p.em("tensor", lambda e, fc=fc, pb=pb: e.matmul(
            ps[4][0:1, 0:w], ones_b[:, :], xbb[:, pb * 512:pb * 512 + w],
            start=(fc == 0), stop=(fc == FC - 1)))
        mmv = p.inc("s_m")
        p.em("tensor", lambda e, fc=fc, pb=pb, s=mmv[0]: e.matmul(
            ps[5][0:1, 0:w], ones_b[:, :], sqb[:, pb * 512:pb * 512 + w],
            start=(fc == 0), stop=(fc == FC - 1)).then_inc(s, 1))
        stats_mm_at[pb] = mmv[1]

    def ln_pass(t0):
        for fc in range(FC):
            xb_ = fc % 2
            p.wait("gpsimd", "s_v", xin_free[xb_])
            xcur = p.dma("gpsimd", xin[:, xb_ * 512:(xb_ + 1) * 512],
                         xown[fc * 128:(fc + 1) * 128, t0:t0 + 512], f"s_x{xb_}")
            p.wait("vector", f"s_x{xb_}", xcur)

            def mk(e, out, xb_=xb_):
                return e.tensor_copy(out, xin[:, xb_ * 512:(xb_ + 1) * 512])
            stats_tile(fc, mk)
            xin_free[xb_] = p.cnt["s_v"]
        emit_ln_rows(p, rows, ps[4], ps[5], ps[6], a_bc, b_bc, ones_r1, D, p.cnt["s_m"])
        for fc in range(FC):
            xb_ = fc % 2
            p.wait("gpsimd", "s_v", xin_free[xb_])
            xcur = p.dma("gpsimd", xin[:, xb_ * 512:(xb_ + 1) * 512],
                         xown[fc * 128:(fc + 1) * 128, t0:t0 + 512], f"s_x{xb_}")
            p.wait("vector", f"s_x{xb_}", xcur)
            p.em("vector", lambda e, xb_=xb_: e.tensor_add(
                t1[:, :], xin[:, xb_ * 512:(xb_ + 1) * 512], b_bc[:, :]))
            vv = p.inc("s_v")
            p.em("vector", lambda e, fc=fc, t0=t0, s=vv[0]: e.tensor_mul(
                xlnb[:, fc * TOK + t0: fc * TOK + t0 + 512], t1[:, :],
                a_bc[:, :]).then_inc(s, 1))
            xin_free[xb_] = vv[1]

    for tt in range(2):
        ln_pass(tt * 512)
    xln_done = p.cnt["s_v"]
    p.wait("tensor", "s_v", xln_done)

    stg_cast_at = [0, 0]
    stg_grp_at = [0, 0]
    panel_mm_at = [0, 0]
    panel_cast_done = [0, 0]
    fuse_done_at = {}
    std_cast_at = [0, 0]
    down_mm_at = {}

    for tt in range(2):
        t0 = tt * 512
        for pi in range(NP):
            pb = pi % 2
            def emit_cast(gr, wait_val):
                p.wait("gpsimd", f"s_g{pb}", wait_val)
                if gr == 0:
                    p.wait("gpsimd", "s_m", panel_mm_at[pb])
                sl = gr % 2
                p.em("gpsimd", lambda e, gr=gr, pb=pb, sl=sl: e.tensor_copy(
                    wgb[:, pb * FC * 256 + gr * 1024: pb * FC * 256 + (gr + 1) * 1024],
                    stageg[:, sl * 1024:(sl + 1) * 1024]))
                cc = p.inc("s_c")
                p.em("gpsimd", lambda e, gr=gr, pb=pb, sl=sl, s=cc[0]: e.tensor_copy(
                    wub[:, pb * FC * 256 + gr * 1024: pb * FC * 256 + (gr + 1) * 1024],
                    stageu[:, sl * 1024:(sl + 1) * 1024]).then_inc(s, 1))
                stg_cast_at[pb] = cc[1]
                stg_grp_at[sl] = cc[1]
            for gr in range(4):
                sl = gr % 2
                p.wait("sync", "s_c", stg_grp_at[sl])
                p.dma("sync",
                      stageg[:, sl * 1024:(sl + 1) * 1024].rearrange("p (j c) -> p j c", j=4),
                      wg[gr * 512:(gr + 1) * 512, pi * 256:(pi + 1) * 256].rearrange(
                          "(j p) c -> p j c", p=128), f"s_g{pb}")
                wcur = p.dma("sync",
                             stageu[:, sl * 1024:(sl + 1) * 1024].rearrange("p (j c) -> p j c", j=4),
                             wu[gr * 512:(gr + 1) * 512, pi * 256:(pi + 1) * 256].rearrange(
                                 "(j p) c -> p j c", p=128), f"s_g{pb}")
                if gr >= 1:
                    emit_cast(gr - 1, wcur)
            emit_cast(3, p.cnt[f"s_g{pb}"])
            panel_cast_done[pb] = p.cnt["s_c"]
            p.wait("tensor", "s_c", panel_cast_done[pb])
            for il in range(2):
                ic = pi * 2 + il
                bk = 2 * (ic % 2)
                if ic >= 2:
                    p.wait("tensor", "s_v", fuse_done_at[ic - 2])
                for fc in range(FC):
                    p.em("tensor", lambda e, fc=fc, il=il, t0=t0, pb=pb, bk=bk: e.matmul(
                        ps[bk][:, :], wgb[:, pb * FC * 256 + fc * 256 + il * 128:
                                          pb * FC * 256 + fc * 256 + il * 128 + 128],
                        xlnb[:, fc * TOK + t0: fc * TOK + t0 + 512],
                        start=(fc == 0), stop=(fc == FC - 1)))
                for fc in range(FC):
                    last = fc == FC - 1
                    if last:
                        mmv = p.inc("s_m")
                        p.em("tensor", lambda e, fc=fc, il=il, t0=t0, pb=pb, bk=bk, s=mmv[0]:
                             e.matmul(ps[bk + 1][:, :],
                                      wub[:, pb * FC * 256 + fc * 256 + il * 128:
                                          pb * FC * 256 + fc * 256 + il * 128 + 128],
                                      xlnb[:, fc * TOK + t0: fc * TOK + t0 + 512],
                                      start=(fc == 0), stop=True).then_inc(s, 1))
                    else:
                        p.em("tensor", lambda e, fc=fc, il=il, t0=t0, pb=pb, bk=bk: e.matmul(
                            ps[bk + 1][:, :],
                            wub[:, pb * FC * 256 + fc * 256 + il * 128:
                                pb * FC * 256 + fc * 256 + il * 128 + 128],
                            xlnb[:, fc * TOK + t0: fc * TOK + t0 + 512],
                            start=(fc == 0), stop=False))
                gu_mm = p.cnt["s_m"]
                sb_ = ic % 2
                p.wait("scalar", "s_m", gu_mm)
                sc = p.inc("s_s")
                p.em("scalar", lambda e, ic=ic, bk=bk, sb_=sb_, s=sc[0]: e.activation(
                    out=gs[:, sb_ * 512:(sb_ + 1) * 512], in_=ps[bk][:, :], func=AF.Silu,
                    scale=sgb[:, ic:ic + 1]).then_inc(s, 1))
                p.wait("vector", "s_m", gu_mm)
                p.em("vector", lambda e, ic=ic, bk=bk: e.tensor_scalar(
                    us[:, :], ps[bk + 1][:, :],
                    sub_s[:, ic:ic + 1], None, ALU.mult))
                p.wait("vector", "s_s", sc[1])
                vv = p.inc("s_v")
                p.em("vector", lambda e, ic=ic, sb_=sb_, s=vv[0]: e.tensor_mul(
                    fuse[:, ic * 512:(ic + 1) * 512], gs[:, sb_ * 512:(sb_ + 1) * 512],
                    us[:, :]).then_inc(s, 1))
                fuse_done_at[ic] = vv[1]
            panel_mm_at[pb] = p.cnt["s_m"]
        fuse_done = p.cnt["s_v"]
        p.wait("tensor", "s_v", fuse_done)

        for hg in range(2):
            for ic in range(NI):
                dbuf = ic % 2
                p.wait("sync", "s_c", std_cast_at[dbuf])
                wcur = p.dma("sync", staged[:, dbuf * 2048:(dbuf + 1) * 2048],
                             wd[ic * 128:(ic + 1) * 128, :], f"s_d{dbuf}")
                p.wait("gpsimd", f"s_d{dbuf}", wcur)
                key = (hg, ic - 2)
                if key in down_mm_at:
                    p.wait("gpsimd", "s_m", down_mm_at[key])
                elif hg == 1 and ic < 2:
                    p.wait("gpsimd", "s_m", down_mm_at[(0, NI - 2 + ic)])
                cc = p.inc("s_c")
                p.em("gpsimd", lambda e, dbuf=dbuf, s=cc[0]: e.tensor_copy(
                    wdb[:, dbuf * 2048:(dbuf + 1) * 2048],
                    staged[:, dbuf * 2048:(dbuf + 1) * 2048]).then_inc(s, 1))
                std_cast_at[dbuf] = cc[1]
                p.wait("tensor", "s_c", cc[1])
                for hl in range(8):
                    hc = hg * 8 + hl
                    last = ic == NI - 1
                    if hl == 7:
                        mmv = p.inc("s_m")
                        p.em("tensor", lambda e, dbuf=dbuf, hc=hc, hl=hl, ic=ic, last=last, s=mmv[0]:
                             e.matmul(ps[hl][:, :],
                                      wdb[:, dbuf * 2048 + hc * 128: dbuf * 2048 + hc * 128 + 128],
                                      fuse[:, ic * 512:(ic + 1) * 512],
                                      start=(ic == 0), stop=last).then_inc(s, 1))
                    else:
                        p.em("tensor", lambda e, dbuf=dbuf, hc=hc, hl=hl, ic=ic, last=last:
                             e.matmul(ps[hl][:, :],
                                      wdb[:, dbuf * 2048 + hc * 128: dbuf * 2048 + hc * 128 + 128],
                                      fuse[:, ic * 512:(ic + 1) * 512],
                                      start=(ic == 0), stop=last))
                down_mm_at[(hg, ic)] = p.cnt["s_m"]
            p.wait("scalar", "s_m", p.cnt["s_m"])
            for hl in range(8):
                hc = hg * 8 + hl
                sc = p.inc("s_s")
                p.em("scalar", lambda e, hl=hl, hc=hc, s=sc[0]: e.activation(
                    out=sub[:, hc * 512:(hc + 1) * 512], in_=ps[hl][:, :],
                    func=AF.Copy, scale=sdb[:, hc:hc + 1]).then_inc(s, 1))
            p.wait("tensor", "s_s", p.cnt["s_s"])
        evac_done = p.cnt["s_s"]

        p.wait("vector", "s_s", evac_done)
        for hc in range(FC):
            def mk(e, out, hc=hc):
                return e.tensor_copy(out, sub[:, hc * 512:(hc + 1) * 512])
            stats_tile(hc, mk)
        emit_ln_rows(p, rows, ps[4], ps[5], ps[6], a_bc, b_bc, ones_r1, D, p.cnt["s_m"])
        for fc in range(FC):
            xb_ = fc % 2
            p.wait("gpsimd", "s_v", xin_free[xb_])
            xcur = p.dma("gpsimd", xin[:, xb_ * 512:(xb_ + 1) * 512],
                         xown[fc * 128:(fc + 1) * 128, t0:t0 + 512], f"s_x{xb_}")
            p.wait("vector", f"s_x{xb_}", xcur)
            p.em("vector", lambda e, fc=fc: e.tensor_add(
                t1[:, :], sub[:, fc * 512:(fc + 1) * 512], b_bc[:, :]))
            p.em("vector", lambda e: e.tensor_mul(t1[:, :], t1[:, :], a_bc[:, :]))
            vv = p.inc("s_v")
            p.em("vector", lambda e, fc=fc, xb_=xb_, s=vv[0]: e.tensor_add(
                sub[:, fc * 512:(fc + 1) * 512], t1[:, :],
                xin[:, xb_ * 512:(xb_ + 1) * 512]).then_inc(s, 1))
            xin_free[xb_] = vv[1]
            p.wait("gpsimd", "s_v", vv[1])
            p.dma("gpsimd", xo[fc * 128:(fc + 1) * 128, t0:t0 + 512],
                  sub[:, fc * 512:(fc + 1) * 512], "s_o0")
        p.wait("tensor", "s_v", p.cnt["s_v"])
        p.wait("scalar", "s_o0", p.cnt["s_o0"])
    p.wait("gpsimd", "s_o0", p.cnt["s_o0"])
    return p.build()



def _run(nc, in_maps):
    return run_bass_kernel_spmd(nc, in_maps, core_ids=list(range(8))).results


def _scale_2d(v, nchunk):
    return np.ascontiguousarray(v.reshape(nchunk, 128).T)


def run_a1(layer, xT, wq_, wk_, wv_, qs_, ks_):
    ropec, ropes = _rope_tables()
    identf = np.eye(128, dtype=np.float32)
    _, masks_ = _attn_sched(layer)
    nc1 = build_a1(layer)
    in_maps = []
    for c in range(8):
        b, r = c // 2, c % 2
        m = {"xT": xT[b],
             "wq": np.ascontiguousarray(wq_[:, r * 1024:(r + 1) * 1024]),
             "wk": np.ascontiguousarray(wk_[:, r * 256:(r + 1) * 256]),
             "wv": np.ascontiguousarray(wv_[:, r * 256:(r + 1) * 256]),
             "ropec": ropec, "ropes": ropes, "masks": masks_, "identf": identf}
        if layer == 0:
            m["qs"] = qs_
            m["ks"] = ks_
        in_maps.append(m)
    return [r_["ot"] for r_ in _run(nc1, in_maps)]


def run_a2(xT, ots, wo_):
    nc2 = build_a2()
    in_maps = []
    for c in range(8):
        b, r = c // 2, c % 2
        otf = np.concatenate([ots[2 * b][:, r * 1024:(r + 1) * 1024],
                              ots[2 * b + 1][:, r * 1024:(r + 1) * 1024]], axis=0)
        in_maps.append({"otf": np.ascontiguousarray(otf), "wo": wo_,
                        "xown": np.ascontiguousarray(xT[b][:, r * 1024:(r + 1) * 1024])})
    xos = [r_["xo"] for r_ in _run(nc2, in_maps)]
    return [np.concatenate([xos[2 * b], xos[2 * b + 1]], axis=1) for b in range(B)]


def attn_block(layer, xT, wq_, wk_, wv_, wo_, qs_, ks_):
    ots = run_a1(layer, xT, wq_, wk_, wv_, qs_, ks_)
    return run_a2(xT, ots, wo_)


def ffn_block(layer, xT, wg_, wu_, wd_, sg_, su_, sd_):
    nc = build_ffn(layer)
    I = wg_.shape[1]
    in_maps = []
    for c in range(8):
        b, r = c // 2, c % 2
        in_maps.append({"xown": np.ascontiguousarray(xT[b][:, r * 1024:(r + 1) * 1024]),
                        "wg": wg_, "wu": wu_, "wd": wd_,
                        "sg": _scale_2d(sg_, I // 128),
                        "su": _scale_2d(su_, I // 128),
                        "sd": _scale_2d(sd_, FC)})
    xos = [r_["xo"] for r_ in _run(nc, in_maps)]
    return [np.concatenate([xos[2 * b], xos[2 * b + 1]], axis=1) for b in range(B)]


def kernel(L0_attn_hidden_states, L0_attn_wq, L0_attn_wk, L0_attn_wv, L0_attn_wo,
           L0_attn_query_scale, L0_attn_key_scale,
           L0_ffn_w_gate, L0_ffn_w_up, L0_ffn_w_down,
           L0_ffn_s_gate, L0_ffn_s_up, L0_ffn_s_down,
           L1_attn_Wq, L1_attn_Wk, L1_attn_Wv, L1_attn_Wo,
           L1_ffn_gate_w, L1_ffn_up_w, L1_ffn_down_w,
           L1_ffn_gate_s, L1_ffn_up_s, L1_ffn_down_s):
    xT = [np.ascontiguousarray(L0_attn_hidden_states[b].T) for b in range(B)]
    xT = attn_block(0, xT, L0_attn_wq, L0_attn_wk, L0_attn_wv, L0_attn_wo,
                    L0_attn_query_scale, L0_attn_key_scale)
    xT = ffn_block(0, xT, L0_ffn_w_gate, L0_ffn_w_up, L0_ffn_w_down,
                   L0_ffn_s_gate, L0_ffn_s_up, L0_ffn_s_down)
    xT = attn_block(1, xT, L1_attn_Wq, L1_attn_Wk, L1_attn_Wv, L1_attn_Wo,
                    None, None)
    xT = ffn_block(1, xT,
                   np.ascontiguousarray(L1_ffn_gate_w.T),
                   np.ascontiguousarray(L1_ffn_up_w.T),
                   np.ascontiguousarray(L1_ffn_down_w.T),
                   L1_ffn_gate_s, L1_ffn_up_s, L1_ffn_down_s)
    return np.stack([xT[b].T for b in range(B)]).astype(np.float32)


# version 04
# speedup vs baseline: 1.4304x; 1.3231x over previous
import numpy as np
import ml_dtypes
from contextlib import ExitStack

import concourse.bass as bass
import concourse.mybir as mybir
from concourse.bass_utils import run_bass_kernel_spmd

F32 = mybir.dt.float32
BF16 = mybir.dt.bfloat16
I8 = mybir.dt.int8
AF = mybir.ActivationFunctionType
ALU = mybir.AluOpType
BF = ml_dtypes.bfloat16

D = 2048
S = 2048
B = 4
HD = 128
NH = 8
NKV = 2
TOK = 1024
WIN = 1024
CAP = 50.0
LN_EPS = 1e-5
QK_EPS = 1e-6
SCALE = 1.0 / float(np.sqrt(HD))
FC = D // 128


def _rope_tables():
    half = HD // 2
    inv = 1.0 / 10000.0 ** (np.arange(half, dtype=np.float32) / half)
    ang = np.outer(np.arange(S, dtype=np.float32), inv)
    c = np.cos(ang).T.astype(np.float32)
    s = np.sin(ang).T.astype(np.float32)
    cos2 = np.concatenate([c, c], axis=0)
    sin2 = np.concatenate([s, s], axis=0)
    return np.ascontiguousarray(cos2), np.ascontiguousarray(sin2)


def _attn_sched(layer):
    tiles = []
    keys = {}
    sched = {}
    for qg in range(4):
        for c in range(16):
            q = qg * 512 + np.arange(512)[None, :]
            k = c * 128 + np.arange(128)[:, None]
            valid = (np.abs(q - k) <= WIN) if layer == 0 else (k <= q)
            if not valid.any():
                continue
            if valid.all():
                sched[(qg, c)] = None
            else:
                t = valid.astype(np.float32)
                key = t.tobytes()
                if key not in keys:
                    keys[key] = len(tiles)
                    tiles.append(t)
                sched[(qg, c)] = keys[key]
    arr = np.stack(tiles).astype(BF)
    return sched, arr



ENGS = ("sync", "tensor", "vector", "scalar", "gpsimd")


class Prog:
    def __init__(self, name):
        self.nc = bass.Bass("TRN2", target_bir_lowering=False, debug=False)
        self.name = name
        self.ops = {e: [] for e in ENGS}
        self.stack = ExitStack()
        self.sems = {}
        self.cnt = {}

    def sem(self, name):
        self.sems[name] = self.stack.enter_context(self.nc.semaphore(name))
        self.cnt[name] = 0
        return name

    def sbuf(self, name, shape, dt):
        return self.stack.enter_context(self.nc.sbuf_tensor(name, shape, dt))

    def psum(self, name, shape, dt):
        return self.stack.enter_context(self.nc.psum_tensor(name, shape, dt))

    def em(self, eng, fn):
        self.ops[eng].append(fn)

    def wait(self, eng, sem, val):
        if val <= 0:
            return
        h = self.sems[sem]
        self.em(eng, lambda e, h=h, v=val: e.wait_ge(h, v))

    def inc(self, sem):
        self.cnt[sem] += 1
        return self.sems[sem], self.cnt[sem]

    def dma(self, eng, out, in_, sem, slow=False):
        h = self.sems[sem]
        self.cnt[sem] += 16
        if slow:
            self.em(eng, lambda e, o=out, i=in_, h=h: e.dma_start(
                out=o, in_=i, allow_slow_non_contiguous=True).then_inc(h, 16))
        else:
            self.em(eng, lambda e, o=out, i=in_, h=h: e.dma_start(out=o, in_=i).then_inc(h, 16))
        return self.cnt[sem]

    def build(self):
        with self.nc.Block() as block:
            def runner(name):
                def body(eng):
                    for fn in self.ops[name]:
                        fn(eng)
                return body
            block.sync(runner("sync"))
            block.tensor(runner("tensor"))
            block.vector(runner("vector"))
            block.scalar(runner("scalar"))
            block.gpsimd(runner("gpsimd"))
        self.stack.close()
        return self.nc



def emit_ln_rows(p, rows, ps_s, ps_ss, ps_bc, a_bc, b_bc, ones_r1, n_feat, wait_mm_val, w=512):
    inv_n = 1.0 / n_feat
    r_v, r_mean, r_t1, r_rstd = (rows[k] for k in ("v", "mean", "t1", "rstd"))
    p.wait("vector", "s_m", wait_mm_val)
    p.em("vector", lambda e: e.tensor_scalar_mul(r_mean[0:1, 0:w], ps_s[0:1, 0:w], inv_n))
    p.em("vector", lambda e: e.tensor_scalar_mul(r_v[0:1, 0:w], ps_ss[0:1, 0:w], inv_n))
    p.em("vector", lambda e: e.tensor_mul(r_t1[0:1, 0:w], r_mean[0:1, 0:w], r_mean[0:1, 0:w]))
    p.em("vector", lambda e: e.tensor_sub(r_v[0:1, 0:w], r_v[0:1, 0:w], r_t1[0:1, 0:w]))
    p.em("vector", lambda e: e.tensor_scalar_add(r_v[0:1, 0:w], r_v[0:1, 0:w], LN_EPS))
    vv = p.inc("s_v")
    p.em("vector", lambda e, s=vv[0]: e.reciprocal(r_t1[0:1, 0:w], r_v[0:1, 0:w]).then_inc(s, 1))
    p.wait("scalar", "s_v", vv[1])
    sc1 = p.inc("s_s")
    p.em("scalar", lambda e, s=sc1[0]: e.activation(out=r_rstd[0:1, 0:w], in_=r_t1[0:1, 0:w],
                                                    func=AF.Sqrt).then_inc(s, 1))
    p.wait("vector", "s_s", sc1[1])
    p.em("vector", lambda e: e.tensor_mul(r_t1[0:1, 0:w], r_rstd[0:1, 0:w], r_rstd[0:1, 0:w]))
    p.em("vector", lambda e: e.scalar_tensor_tensor(r_t1[0:1, 0:w], r_v[0:1, 0:w], -0.5,
                                                    r_t1[0:1, 0:w], ALU.mult, ALU.mult))
    p.em("vector", lambda e: e.tensor_scalar_add(r_t1[0:1, 0:w], r_t1[0:1, 0:w], 1.5))
    vv1 = p.inc("s_v")
    p.em("vector", lambda e, s=vv1[0]: e.tensor_mul(r_rstd[0:1, 0:w], r_rstd[0:1, 0:w],
                                                    r_t1[0:1, 0:w]).then_inc(s, 1))
    p.wait("tensor", "s_v", vv1[1])
    mm1 = p.inc("s_m")
    p.em("tensor", lambda e, s=mm1[0]: e.matmul(ps_bc[:, 0:w], ones_r1[0:1, :], r_rstd[0:1, 0:w],
                                                start=True, stop=True).then_inc(s, 1))
    p.wait("vector", "s_m", mm1[1])
    p.em("vector", lambda e: e.tensor_copy(a_bc[:, 0:w], ps_bc[:, 0:w]))
    vv2 = p.inc("s_v")
    p.em("vector", lambda e, s=vv2[0]: e.tensor_scalar_mul(
        r_mean[0:1, 0:w], r_mean[0:1, 0:w], -1.0).then_inc(s, 1))
    p.wait("tensor", "s_v", vv2[1])
    mm2 = p.inc("s_m")
    p.em("tensor", lambda e, s=mm2[0]: e.matmul(ps_bc[:, 0:w], ones_r1[0:1, :], r_mean[0:1, 0:w],
                                                start=True, stop=True).then_inc(s, 1))
    p.wait("vector", "s_m", mm2[1])
    vv3 = p.inc("s_v")
    p.em("vector", lambda e, s=vv3[0]: e.tensor_copy(b_bc[:, 0:w], ps_bc[:, 0:w]).then_inc(s, 1))
    return vv3[1]



def build_a1(layer):
    sched, mask_arr = _attn_sched(layer)
    nm = mask_arr.shape[0]
    p = Prog(f"a1_l{layer}")
    nc = p.nc

    xT = nc.dram_tensor("xT", [D, S], F32, kind="ExternalInput").ap()
    wq = nc.dram_tensor("wq", [D, NH * HD], F32, kind="ExternalInput").ap()
    wk = nc.dram_tensor("wk", [D, NKV * HD], F32, kind="ExternalInput").ap()
    wv = nc.dram_tensor("wv", [D, NKV * HD], F32, kind="ExternalInput").ap()
    ropec = nc.dram_tensor("ropec", [128, S], F32, kind="ExternalInput").ap()
    ropes = nc.dram_tensor("ropes", [128, S], F32, kind="ExternalInput").ap()
    masks = nc.dram_tensor("masks", [nm, 128, 512], BF16, kind="ExternalInput").ap()
    identf = nc.dram_tensor("identf", [128, 128], F32, kind="ExternalInput").ap()
    if layer == 0:
        qsv = nc.dram_tensor("qs", [HD], F32, kind="ExternalInput").ap()
        ksv = nc.dram_tensor("ks", [HD], F32, kind="ExternalInput").ap()
    ot = nc.dram_tensor("ot", [NH * HD, S], BF16, kind="ExternalOutput").ap()

    xlnb = p.sbuf("xlnb", [128, FC * S], BF16)
    xbuf = p.sbuf("xbuf", [128, FC * 256], F32)
    xbb = p.sbuf("xbb", [128, 512], BF16)
    sqb = p.sbuf("sqb", [128, 512], BF16)
    stage = p.sbuf("stage", [128, 1024], F32)
    wqb = p.sbuf("wqb", [128, 2 * FC * HD], BF16)
    wkb = p.sbuf("wkb", [128, FC * NKV * HD], BF16)
    wvb = p.sbuf("wvb", [128, FC * NKV * HD], BF16)
    Kb = p.sbuf("Kb", [128, NKV * S], BF16)
    Vtok = p.sbuf("Vtok", [128, 16 * NKV * HD], BF16)
    tblc = p.sbuf("tblc", [128, S], F32)
    tbls = p.sbuf("tbls", [128, S], F32)
    maskb = p.sbuf("maskb", [128, nm * 512], BF16)
    idf = p.sbuf("idf", [128, 128], F32)
    kr = p.sbuf("kr", [128, 512], F32)
    k2 = p.sbuf("k2", [128, 512], F32)
    ktmp = p.sbuf("ktmp", [128, 512], F32)
    kf = p.sbuf("kf", [128, 512], F32)
    vf = p.sbuf("vf", [128, 512], F32)
    qb = p.sbuf("qb", [128, 2 * S], BF16)
    ptile = p.sbuf("ptile", [128, 2 * 1024], BF16)
    pmk = p.sbuf("pmk", [128, 1024], BF16)
    ttile = p.sbuf("ttile", [128, 1024], F32)
    blb = p.sbuf("blb", [128, 512], F32)
    obuf = p.sbuf("obuf", [128, 2 * 512], BF16)
    a_bc = p.sbuf("a_bc", [128, 512], F32)
    b_bc = p.sbuf("b_bc", [128, 512], F32)
    ones_b = p.sbuf("ones_b", [128, 1], BF16)
    ones_r1 = p.sbuf("ones_r1", [1, 128], F32)
    biasb = p.sbuf("biasb", [128, 1], F32)
    rows = {k: p.sbuf("row_" + k, [1, 512], F32) for k in ("v", "mean", "rstd")}
    rows["t1"] = a_bc
    rows["l"] = b_bc
    if layer == 0:
        qsb = p.sbuf("qsb", [128, 1], F32)
        ksb = p.sbuf("ksb", [128, 1], F32)
        nrows = {k: p.sbuf("nrow_" + k, [1, 512], F32) for k in ("v", "t", "d")}

    pA = p.psum("pA", [128, 1024], F32)
    pB = p.psum("pB", [128, 1024], F32)
    pl = p.psum("pl", [128, 512], F32)
    pO = p.psum("pO", [128, 512], F32)
    pq = p.psum("pq", [128, 1024], F32)

    for s in ("s_x", "s_w", "s_kc", "s_o0", "s_o1", "s_c", "s_m", "s_v", "s_s"):
        p.sem(s)

    p.dma("sync", tblc[:, :], ropec, "s_kc")
    p.dma("sync", tbls[:, :], ropes, "s_kc")
    p.dma("sync", maskb[:].rearrange("p (n m) -> p n m", n=nm),
          masks.rearrange("n p m -> p n m"), "s_kc")
    p.dma("sync", idf[:, :], identf, "s_kc")
    if layer == 0:
        p.dma("sync", qsb[:, :], qsv.rearrange("(p o) -> p o", o=1), "s_kc")
        p.dma("sync", ksb[:, :], ksv.rearrange("(p o) -> p o", o=1), "s_kc")
    w_const = p.cnt["s_kc"]
    p.wait("vector", "s_kc", w_const)
    p.em("vector", lambda e: e.memset(ones_b[:, :], 1.0))
    exp_bias = -CAP if layer == 0 else -100.0 * SCALE
    p.em("vector", lambda e, b=exp_bias: e.memset(biasb[:, :], b))
    vset = p.inc("s_v")
    p.em("vector", lambda e, s=vset[0]: e.memset(ones_r1[:, :], 1.0).then_inc(s, 1))
    p.wait("tensor", "s_v", vset[1])
    p.wait("scalar", "s_v", vset[1])
    p.wait("gpsimd", "s_v", vset[1])

    for fc in range(FC):
        wcur = p.dma("sync", stage[:, 0:256], wk[fc * 128:(fc + 1) * 128, :], "s_w")
        p.wait("gpsimd", "s_w", wcur)
        cc0 = p.inc("s_c")
        p.em("gpsimd", lambda e, fc=fc, s=cc0[0]: e.tensor_copy(
            wkb[:, fc * 256:(fc + 1) * 256], stage[:, 0:256]).then_inc(s, 1))
        p.wait("sync", "s_c", cc0[1])
        wcur = p.dma("sync", stage[:, 256:512], wv[fc * 128:(fc + 1) * 128, :], "s_w")
        p.wait("gpsimd", "s_w", wcur)
        cc1 = p.inc("s_c")
        p.em("gpsimd", lambda e, fc=fc, s=cc1[0]: e.tensor_copy(
            wvb[:, fc * 256:(fc + 1) * 256], stage[:, 256:512]).then_inc(s, 1))
        p.wait("sync", "s_c", cc1[1])
    kv_cast_done = p.cnt["s_c"]

    W0 = 256
    a1_stats_at = [0, 0]
    for tt in range(S // W0):
        t0 = tt * W0
        for fc in range(FC):
            p.dma("gpsimd", xbuf[:, fc * W0:(fc + 1) * W0],
                  xT[fc * 128:(fc + 1) * 128, t0:t0 + W0], "s_x")
        x_done = p.cnt["s_x"]
        p.wait("vector", "s_x", x_done)
        for fc in range(FC):
            pb2 = fc % 2
            p.wait("vector", "s_m", a1_stats_at[pb2])
            vv = p.inc("s_v")
            p.em("vector", lambda e, fc=fc, pb2=pb2, s=vv[0]: e.tensor_copy(
                xbb[:, pb2 * 256:pb2 * 256 + W0],
                xbuf[:, fc * W0:(fc + 1) * W0]).then_inc(s, 1))
            p.wait("scalar", "s_v", vv[1])
            sc = p.inc("s_s")
            p.em("scalar", lambda e, pb2=pb2, s=sc[0]: e.activation(
                out=sqb[:, pb2 * 256:pb2 * 256 + W0],
                in_=xbb[:, pb2 * 256:pb2 * 256 + W0], func=AF.Square).then_inc(s, 1))
            p.wait("tensor", "s_s", sc[1])
            p.em("tensor", lambda e, fc=fc, pb2=pb2: e.matmul(
                pl[0:1, 0:W0], ones_b[:, :], xbb[:, pb2 * 256:pb2 * 256 + W0],
                start=(fc == 0), stop=(fc == FC - 1)))
            mmv = p.inc("s_m")
            p.em("tensor", lambda e, fc=fc, pb2=pb2, s=mmv[0]: e.matmul(
                pO[0:1, 0:W0], ones_b[:, :], sqb[:, pb2 * 256:pb2 * 256 + W0],
                start=(fc == 0), stop=(fc == FC - 1)).then_inc(s, 1))
            a1_stats_at[pb2] = mmv[1]
        emit_ln_rows(p, rows, pl, pO, pA, a_bc, b_bc, ones_r1, D, p.cnt["s_m"], w=W0)
        for fc in range(FC):
            p.em("vector", lambda e, fc=fc: e.tensor_add(
                xbuf[:, fc * W0:(fc + 1) * W0], xbuf[:, fc * W0:(fc + 1) * W0], b_bc[:, 0:W0]))
            vv = p.inc("s_v")
            p.em("vector", lambda e, fc=fc, t0=t0, s=vv[0]: e.tensor_mul(
                xlnb[:, fc * S + t0: fc * S + t0 + W0],
                xbuf[:, fc * W0:(fc + 1) * W0], a_bc[:, 0:W0]).then_inc(s, 1))
        p.wait("gpsimd", "s_v", p.cnt["s_v"])
    xln_done_v = p.cnt["s_v"]

    def rope(src, t0, dst, col):
        p.em("vector", lambda e: e.tensor_copy(ktmp[0:64, :], src[64:128, :]))
        p.em("vector", lambda e: e.tensor_copy(ktmp[64:128, :], src[0:64, :]))
        p.em("vector", lambda e, t0=t0: e.tensor_mul(
            k2[:, :], ktmp[:, :], tbls[:, t0:t0 + 512]))
        p.em("vector", lambda e, t0=t0: e.tensor_mul(
            ktmp[:, :], src[:, :], tblc[:, t0:t0 + 512]))
        p.em("vector", lambda e: e.tensor_sub(dst[0:64, col:col + 512],
                                              ktmp[0:64, :], k2[0:64, :]))
        vv = p.inc("s_v")
        p.em("vector", lambda e, s=vv[0]: e.tensor_add(
            dst[64:128, col:col + 512], ktmp[64:128, :], k2[64:128, :]).then_inc(s, 1))
        return vv[1]

    qkg = {"s": 0, "m": 0}

    def qknorm(src, dst_bf, dst_col, scale_ap, ps_ss, ps_bc):
        p.wait("vector", "s_s", qkg["s"])
        p.wait("vector", "s_m", qkg["m"])
        vv = p.inc("s_v")
        p.em("vector", lambda e, s=vv[0]: e.tensor_copy(xbb[:, :], src[:, :]).then_inc(s, 1))
        p.wait("scalar", "s_v", vv[1])
        sc = p.inc("s_s")
        p.em("scalar", lambda e, s=sc[0]: e.activation(
            out=sqb[:, :], in_=xbb[:, :], func=AF.Square).then_inc(s, 1))
        qkg["s"] = sc[1]
        p.wait("tensor", "s_s", sc[1])
        mm = p.inc("s_m")
        p.em("tensor", lambda e, ps_ss=ps_ss, s=mm[0]: e.matmul(
            ps_ss[0:1, :], ones_b[:, :], sqb[:, :], start=True, stop=True).then_inc(s, 1))
        nv, nt, nd = nrows["v"], nrows["t"], nrows["d"]
        p.wait("vector", "s_m", mm[1])
        p.em("vector", lambda e, ps_ss=ps_ss: e.tensor_copy(nv[0:1, :], ps_ss[0:1, :]))
        vv = p.inc("s_v")
        p.em("vector", lambda e, s=vv[0]: e.reciprocal(nt[0:1, :], nv[0:1, :]).then_inc(s, 1))
        p.wait("scalar", "s_v", vv[1])
        sc = p.inc("s_s")
        p.em("scalar", lambda e, s=sc[0]: e.activation(
            out=nd[0:1, :], in_=nt[0:1, :], func=AF.Sqrt).then_inc(s, 1))
        p.wait("vector", "s_s", sc[1])
        p.em("vector", lambda e: e.tensor_mul(nt[0:1, :], nd[0:1, :], nd[0:1, :]))
        p.em("vector", lambda e: e.scalar_tensor_tensor(nt[0:1, :], nv[0:1, :], -0.5,
                                                        nt[0:1, :], ALU.mult, ALU.mult))
        p.em("vector", lambda e: e.tensor_scalar_add(nt[0:1, :], nt[0:1, :], 1.5))
        p.em("vector", lambda e: e.tensor_mul(nd[0:1, :], nd[0:1, :], nt[0:1, :]))
        p.em("vector", lambda e: e.tensor_mul(nd[0:1, :], nv[0:1, :], nd[0:1, :]))
        p.em("vector", lambda e: e.tensor_scalar_add(nd[0:1, :], nd[0:1, :], QK_EPS))
        vv = p.inc("s_v")
        p.em("vector", lambda e, s=vv[0]: e.reciprocal(
            nt[0:1, :], nd[0:1, :]).then_inc(s, 1))
        p.wait("tensor", "s_v", vv[1])
        mm = p.inc("s_m")
        p.em("tensor", lambda e, ps_bc=ps_bc, s=mm[0]: e.matmul(
            ps_bc[:, :], ones_r1[0:1, :], nt[0:1, :], start=True, stop=True).then_inc(s, 1))
        qkg["m"] = mm[1]
        p.wait("vector", "s_m", mm[1])
        vv = p.inc("s_v")
        p.em("vector", lambda e, ps_bc=ps_bc, s=vv[0]: e.scalar_tensor_tensor(
            dst_bf[:, dst_col:dst_col + 512], src[:, :], scale_ap,
            ps_bc[:, :], ALU.mult, ALU.mult).then_inc(s, 1))
        return vv[1]

    p.wait("tensor", "s_c", kv_cast_done)
    p.wait("tensor", "s_v", xln_done_v)
    for g in range(NKV):
        for tt in range(4):
            t0 = tt * 512
            for fc in range(FC):
                p.em("tensor", lambda e, fc=fc, g=g, t0=t0: e.matmul(
                    pq[:, 0:512], wkb[:, fc * 256 + g * 128: fc * 256 + g * 128 + 128],
                    xlnb[:, fc * S + t0: fc * S + t0 + 512],
                    start=(fc == 0), stop=(fc == FC - 1)))
            for fc in range(FC):
                last = fc == FC - 1
                if last:
                    mmv = p.inc("s_m")
                    p.em("tensor", lambda e, fc=fc, g=g, t0=t0, s=mmv[0]: e.matmul(
                        pq[:, 512:1024], wvb[:, fc * 256 + g * 128: fc * 256 + g * 128 + 128],
                        xlnb[:, fc * S + t0: fc * S + t0 + 512],
                        start=(fc == 0), stop=True).then_inc(s, 1))
                else:
                    p.em("tensor", lambda e, fc=fc, g=g, t0=t0: e.matmul(
                        pq[:, 512:1024], wvb[:, fc * 256 + g * 128: fc * 256 + g * 128 + 128],
                        xlnb[:, fc * S + t0: fc * S + t0 + 512],
                        start=(fc == 0), stop=False))
            kv_mm = p.cnt["s_m"]
            p.wait("vector", "s_m", kv_mm)
            p.em("vector", lambda e: e.tensor_copy(kr[:, :], pq[:, 0:512]))
            vv = p.inc("s_v")
            p.em("vector", lambda e, s=vv[0]: e.tensor_copy(vf[:, :], pq[:, 512:1024]).then_inc(s, 1))
            evac_v = vv[1]
            if layer == 0:
                rope(kr, t0, kf, 0)
                qknorm(kf, Kb, g * S + t0, ksb[:, 0:1], pl, pO)
            else:
                rope(kr, t0, Kb, g * S + t0)
            p.wait("tensor", "s_v", evac_v)
            for st in range(4):
                mmv = p.inc("s_m")
                p.em("tensor", lambda e, st=st, s=mmv[0]: e.transpose(
                    pA[:, 0:128], vf[:, st * 128:(st + 1) * 128], idf[:, :]).then_inc(s, 1))
                p.wait("vector", "s_m", mmv[1])
                ci = tt * 4 + st
                vv = p.inc("s_v")
                p.em("vector", lambda e, ci=ci, g=g, s=vv[0]: e.tensor_copy(
                    Vtok[:, ci * 256 + g * 128: ci * 256 + g * 128 + 128],
                    pA[:, 0:128]).then_inc(s, 1))
                p.wait("tensor", "s_v", vv[1])
            p.wait("tensor", "s_v", p.cnt["s_v"])

    sc_tanh = SCALE / CAP
    qmm_after = [0, 0]
    qb_release = [0, 0]
    wq_cast_done = [0, 0]
    stg = {"free": kv_cast_done}
    prep_guard = [0]

    def emit_wq_load(h):
        hb = h % 2
        p.wait("gpsimd", "s_m", qmm_after[hb])
        for half in range(2):
            p.wait("sync", "s_c", stg["free"])
            for fi in range(8):
                fc = half * 8 + fi
                p.dma("sync", stage[:, fi * 128:(fi + 1) * 128],
                      wq[fc * 128:(fc + 1) * 128, h * 128:(h + 1) * 128], "s_w")
            p.wait("gpsimd", "s_w", p.cnt["s_w"])
            cc = p.inc("s_c")
            p.em("gpsimd", lambda e, hb=hb, half=half, s=cc[0]: e.tensor_copy(
                wqb[:, hb * 2048 + half * 1024: hb * 2048 + (half + 1) * 1024],
                stage[:, :]).then_inc(s, 1))
            stg["free"] = cc[1]
        wq_cast_done[hb] = p.cnt["s_c"]

    def emit_qprep_tt(h, tt):
        hb = h % 2
        t0 = tt * 512
        bq = (tt % 2) * 512
        if tt == 0:
            p.wait("tensor", "s_c", wq_cast_done[hb])
            p.wait("vector", "s_m", qb_release[hb])
        for fc in range(FC):
            last = fc == FC - 1
            if last:
                mmv = p.inc("s_m")
                p.em("tensor", lambda e, fc=fc, hb=hb, t0=t0, bq=bq, s=mmv[0]: e.matmul(
                    pq[:, bq:bq + 512], wqb[:, hb * 2048 + fc * 128: hb * 2048 + fc * 128 + 128],
                    xlnb[:, fc * S + t0: fc * S + t0 + 512],
                    start=(fc == 0), stop=True).then_inc(s, 1))
            else:
                p.em("tensor", lambda e, fc=fc, hb=hb, t0=t0, bq=bq: e.matmul(
                    pq[:, bq:bq + 512], wqb[:, hb * 2048 + fc * 128: hb * 2048 + fc * 128 + 128],
                    xlnb[:, fc * S + t0: fc * S + t0 + 512],
                    start=(fc == 0), stop=False))
        p.wait("vector", "s_m", p.cnt["s_m"])
        vv = p.inc("s_v")
        p.em("vector", lambda e, bq=bq, s=vv[0]: e.tensor_copy(
            kr[:, :], pq[:, bq:bq + 512]).then_inc(s, 1))
        if layer == 0:
            rope(kr, t0, kf, 0)
            prep_guard[0] = qknorm(kf, qb, hb * S + t0, qsb[:, 0:1], pl, pO)
        else:
            rope(kr, t0, qb, hb * S + t0)
        p.wait("tensor", "s_v", vv[1])
        if tt == 3:
            qmm_after[hb] = p.cnt["s_m"]

    def emit_attn_qg(h, qg):
        hb = h % 2
        g = h // 4
        q0 = qg * 512
        cs = [c for c in range(16) if (qg, c) in sched]
        ncs = len(cs)
        npair = (ncs + 1) // 2
        pair_mm = {}

        p.wait("tensor", "s_v", prep_guard[0])

        def emit_st_pair(j):
            pair = pA if j % 2 == 0 else pB
            lastk = 1 if 2 * j + 1 < ncs else 0
            for k in range(lastk + 1):
                ci = 2 * j + k
                c = cs[ci]
                if k == lastk:
                    mmv = p.inc("s_m")
                    p.em("tensor", lambda e, c=c, g=g, hb=hb, q0=q0, k=k, pair=pair, s=mmv[0]:
                         e.matmul(pair[:, k * 512:(k + 1) * 512],
                                  Kb[:, g * S + c * 128: g * S + c * 128 + 128],
                                  qb[:, hb * S + q0: hb * S + q0 + 512],
                                  start=True, stop=True).then_inc(s, 1))
                    pair_mm[j] = mmv[1]
                else:
                    p.em("tensor", lambda e, c=c, g=g, hb=hb, q0=q0, k=k, pair=pair:
                         e.matmul(pair[:, k * 512:(k + 1) * 512],
                                  Kb[:, g * S + c * 128: g * S + c * 128 + 128],
                                  qb[:, hb * S + q0: hb * S + q0 + 512],
                                  start=True, stop=True))

        emit_st_pair(0)
        if npair > 1:
            emit_st_pair(1)
        for j in range(npair):
            pair = pA if j % 2 == 0 else pB
            pt0 = (j % 2) * 1024
            w = 1024 if 2 * j + 1 < ncs else 512
            p.wait("scalar", "s_m", pair_mm[j])
            if layer == 0:
                p.em("scalar", lambda e, pair=pair, w=w: e.activation(
                    out=ttile[:, 0:w], in_=pair[:, 0:w], func=AF.Tanh, scale=sc_tanh))
                sc2 = p.inc("s_s")
                p.em("scalar", lambda e, pt0=pt0, w=w, s=sc2[0]: e.activation(
                    out=ptile[:, pt0:pt0 + w], in_=ttile[:, 0:w],
                    func=AF.Exp, scale=CAP, bias=biasb[:, 0:1]).then_inc(s, 1))
            else:
                sc2 = p.inc("s_s")
                p.em("scalar", lambda e, pair=pair, pt0=pt0, w=w, s=sc2[0]: e.activation(
                    out=ptile[:, pt0:pt0 + w], in_=pair[:, 0:w],
                    func=AF.Exp, scale=SCALE, bias=biasb[:, 0:1]).then_inc(s, 1))
            for k in range(2):
                ci = 2 * j + k
                if ci >= ncs:
                    break
                c = cs[ci]
                mi = sched[(qg, c)]
                if mi is not None:
                    p.wait("vector", "s_m", p.cnt["s_m"])
                    p.wait("vector", "s_s", sc2[1])
                    vv = p.inc("s_v")
                    p.em("vector", lambda e, pt0=pt0, k=k, mi=mi, s=vv[0]: e.tensor_mul(
                        pmk[:, k * 512:(k + 1) * 512],
                        ptile[:, pt0 + k * 512:pt0 + (k + 1) * 512],
                        maskb[:, mi * 512:(mi + 1) * 512]).then_inc(s, 1))
                    p.wait("tensor", "s_v", vv[1])
                    srcap = pmk[:, k * 512:(k + 1) * 512]
                else:
                    p.wait("tensor", "s_s", sc2[1])
                    srcap = ptile[:, pt0 + k * 512:pt0 + (k + 1) * 512]
                p.em("tensor", lambda e, ci=ci, ncs=ncs, srcap=srcap: e.matmul(
                    pl[0:1, :], ones_b[:, :], srcap,
                    start=(ci == 0), stop=(ci == ncs - 1)))
                mmv = p.inc("s_m")
                p.em("tensor", lambda e, ci=ci, ncs=ncs, c=c, g=g, srcap=srcap, s=mmv[0]:
                     e.matmul(pO[:, :], Vtok[:, c * 256 + g * 128: c * 256 + g * 128 + 128],
                              srcap, start=(ci == 0), stop=(ci == ncs - 1)).then_inc(s, 1))
            if j + 2 < npair:
                emit_st_pair(j + 2)
                p.wait("scalar", "s_m", p.cnt["s_m"])
        attn_mm = p.cnt["s_m"]
        p.wait("vector", "s_m", attn_mm)
        vv = p.inc("s_v")
        p.em("vector", lambda e, s=vv[0]: e.reciprocal(
            rows["l"][0:1, :], pl[0:1, :]).then_inc(s, 1))
        p.wait("tensor", "s_v", vv[1])
        mmv = p.inc("s_m")
        p.em("tensor", lambda e, s=mmv[0]: e.matmul(
            pl[:, :], ones_r1[0:1, :], rows["l"][0:1, :],
            start=True, stop=True).then_inc(s, 1))
        p.wait("vector", "s_m", mmv[1])
        p.em("vector", lambda e: e.tensor_copy(blb[:, :], pl[:, :]))
        ob = qg % 2
        if p.cnt[f"s_o{ob}"] > 0:
            p.wait("vector", f"s_o{ob}", p.cnt[f"s_o{ob}"])
        vv = p.inc("s_v")
        p.em("vector", lambda e, ob=ob, s=vv[0]: e.tensor_mul(
            obuf[:, ob * 512:(ob + 1) * 512], pO[:, :], blb[:, :]).then_inc(s, 1))
        p.wait("gpsimd", "s_v", vv[1])
        p.dma("gpsimd", ot[h * 128:(h + 1) * 128, q0:q0 + 512],
              obuf[:, ob * 512:(ob + 1) * 512], f"s_o{ob}")
        p.wait("tensor", "s_v", p.cnt["s_v"])

    emit_wq_load(0)
    for tt in range(4):
        emit_qprep_tt(0, tt)
    for h in range(NH):
        if h + 1 < NH:
            emit_wq_load(h + 1)
        for qg in range(4):
            emit_attn_qg(h, qg)
            if h + 1 < NH:
                emit_qprep_tt(h + 1, qg)
        qb_release[h % 2] = p.cnt["s_m"]
    p.wait("gpsimd", "s_o0", p.cnt["s_o0"])
    p.wait("gpsimd", "s_o1", p.cnt["s_o1"])
    return p.build()



def build_a2():
    p = Prog("a2")
    nc = p.nc
    otf = nc.dram_tensor("otf", [D, TOK], BF16, kind="ExternalInput").ap()
    wo = nc.dram_tensor("wo", [D, D], F32, kind="ExternalInput").ap()
    xown = nc.dram_tensor("xown", [D, TOK], F32, kind="ExternalInput").ap()
    xo = nc.dram_tensor("xo", [D, TOK], F32, kind="ExternalOutput").ap()

    otb = p.sbuf("otb", [128, FC * TOK], BF16)
    stage = p.sbuf("stage", [128, 2 * 512], F32)
    wob = p.sbuf("wob", [128, 4 * FC * 512], BF16)
    sub = p.sbuf("sub", [128, FC * 512], F32)
    subb = p.sbuf("subb", [128, 512], BF16)
    sqb = p.sbuf("sqb", [128, 512], BF16)
    xin = p.sbuf("xin", [128, 2 * 512], F32)
    xout = p.sbuf("xout", [128, 2 * 512], F32)
    t1 = p.sbuf("t1", [128, 512], F32)
    a_bc = p.sbuf("a_bc", [128, 512], F32)
    b_bc = p.sbuf("b_bc", [128, 512], F32)
    ones_b = p.sbuf("ones_b", [128, 1], BF16)
    ones_r1 = p.sbuf("ones_r1", [1, 128], F32)
    rows = {k: p.sbuf("row_" + k, [1, 512], F32)
            for k in ("v", "mean", "t1", "rstd")}
    ps = [p.psum(f"ps{i}", [128, 512], F32) for i in range(6)]

    for s in ("s_ot", "s_p0", "s_p1", "s_x0", "s_x1", "s_o0", "s_o1",
              "s_c", "s_m", "s_v", "s_s"):
        p.sem(s)

    p.em("vector", lambda e: e.memset(ones_b[:, :], 1.0))
    vset = p.inc("s_v")
    p.em("vector", lambda e, s=vset[0]: e.memset(ones_r1[:, :], 1.0).then_inc(s, 1))
    p.wait("tensor", "s_v", vset[1])

    for fc in range(FC):
        p.dma("gpsimd", otb[:, fc * TOK:(fc + 1) * TOK],
              otf[fc * 128:(fc + 1) * 128, :], "s_ot")
    p.wait("tensor", "s_ot", p.cnt["s_ot"])

    cast_at = [0, 0]
    panel_ready = []
    for pi in range(4):
        for fc in range(FC):
            sbf = fc % 2
            p.wait("sync", "s_c", cast_at[sbf])
            wcur = p.dma("sync", stage[:, sbf * 512:(sbf + 1) * 512],
                         wo[fc * 128:(fc + 1) * 128, pi * 512:(pi + 1) * 512], f"s_p{sbf}")
            p.wait("gpsimd", f"s_p{sbf}", wcur)
            cc = p.inc("s_c")
            p.em("gpsimd", lambda e, fc=fc, pi=pi, sbf=sbf, s=cc[0]: e.tensor_copy(
                wob[:, pi * FC * 512 + fc * 512: pi * FC * 512 + (fc + 1) * 512],
                stage[:, sbf * 512:(sbf + 1) * 512]).then_inc(s, 1))
            cast_at[sbf] = cc[1]
        panel_ready.append(p.cnt["s_c"])

    for tt in range(2):
        t0 = tt * 512
        for pi in range(4):
            p.wait("tensor", "s_c", panel_ready[pi])
            for el in range(4):
                ec = pi * 4 + el
                bank = ec % 2
                for hdc in range(FC):
                    last = hdc == FC - 1
                    if last:
                        mmv = p.inc("s_m")
                        p.em("tensor", lambda e, hdc=hdc, pi=pi, el=el, t0=t0, bank=bank, s=mmv[0]:
                             e.matmul(ps[bank][:, :],
                                      wob[:, pi * FC * 512 + hdc * 512 + el * 128:
                                          pi * FC * 512 + hdc * 512 + el * 128 + 128],
                                      otb[:, hdc * TOK + t0: hdc * TOK + t0 + 512],
                                      start=(hdc == 0), stop=True).then_inc(s, 1))
                    else:
                        p.em("tensor", lambda e, hdc=hdc, pi=pi, el=el, t0=t0, bank=bank:
                             e.matmul(ps[bank][:, :],
                                      wob[:, pi * FC * 512 + hdc * 512 + el * 128:
                                          pi * FC * 512 + hdc * 512 + el * 128 + 128],
                                      otb[:, hdc * TOK + t0: hdc * TOK + t0 + 512],
                                      start=(hdc == 0), stop=False))
                p.wait("vector", "s_m", p.cnt["s_m"])
                p.em("vector", lambda e, ec=ec, bank=bank: e.tensor_copy(
                    sub[:, ec * 512:(ec + 1) * 512], ps[bank][:, :]))
                vv = p.inc("s_v")
                p.em("vector", lambda e, bank=bank, s=vv[0]: e.tensor_copy(
                    subb[:, :], ps[bank][:, :]).then_inc(s, 1))
                p.wait("scalar", "s_v", vv[1])
                sc = p.inc("s_s")
                p.em("scalar", lambda e, s=sc[0]: e.activation(
                    out=sqb[:, :], in_=subb[:, :], func=AF.Square).then_inc(s, 1))
                p.wait("tensor", "s_s", sc[1])
                p.em("tensor", lambda e, ec=ec: e.matmul(
                    ps[4][0:1, :], ones_b[:, :], subb[:, :],
                    start=(ec == 0), stop=(ec == FC - 1)))
                mmv = p.inc("s_m")
                p.em("tensor", lambda e, ec=ec, s=mmv[0]: e.matmul(
                    ps[5][0:1, :], ones_b[:, :], sqb[:, :],
                    start=(ec == 0), stop=(ec == FC - 1)).then_inc(s, 1))
                p.wait("vector", "s_m", p.cnt["s_m"])
        emit_ln_rows(p, rows, ps[4], ps[5], ps[2], a_bc, b_bc, ones_r1, D, p.cnt["s_m"])
        for fc in range(FC):
            xb_ = fc % 2
            xcur = p.dma("gpsimd", xin[:, xb_ * 512:(xb_ + 1) * 512],
                         xown[fc * 128:(fc + 1) * 128, t0:t0 + 512], f"s_x{xb_}")
            p.wait("vector", f"s_x{xb_}", xcur)
            p.em("vector", lambda e, fc=fc: e.tensor_add(
                t1[:, :], sub[:, fc * 512:(fc + 1) * 512], b_bc[:, :]))
            p.em("vector", lambda e: e.tensor_mul(t1[:, :], t1[:, :], a_bc[:, :]))
            if p.cnt[f"s_o{xb_}"] > 0:
                p.wait("vector", f"s_o{xb_}", p.cnt[f"s_o{xb_}"])
            vv = p.inc("s_v")
            p.em("vector", lambda e, xb_=xb_, s=vv[0]: e.tensor_add(
                xout[:, xb_ * 512:(xb_ + 1) * 512], t1[:, :],
                xin[:, xb_ * 512:(xb_ + 1) * 512]).then_inc(s, 1))
            p.wait("gpsimd", "s_v", vv[1])
            p.dma("gpsimd", xo[fc * 128:(fc + 1) * 128, t0:t0 + 512],
                  xout[:, xb_ * 512:(xb_ + 1) * 512], f"s_o{xb_}")
        p.wait("tensor", "s_v", p.cnt["s_v"])
    p.wait("gpsimd", "s_o0", p.cnt["s_o0"])
    p.wait("gpsimd", "s_o1", p.cnt["s_o1"])
    return p.build()



def build_ffn(layer):
    I = 8192 if layer == 0 else 5632
    WDT = I8 if layer == 0 else F32
    NI = I // 128
    NP = I // 256
    p = Prog(f"ffn_l{layer}")
    nc = p.nc
    xown = nc.dram_tensor("xown", [D, TOK], F32, kind="ExternalInput").ap()
    wg = nc.dram_tensor("wg", [D, I], WDT, kind="ExternalInput").ap()
    wu = nc.dram_tensor("wu", [D, I], WDT, kind="ExternalInput").ap()
    wd = nc.dram_tensor("wd", [I, D], WDT, kind="ExternalInput").ap()
    sg = nc.dram_tensor("sg", [128, NI], F32, kind="ExternalInput").ap()
    su = nc.dram_tensor("su", [128, NI], F32, kind="ExternalInput").ap()
    sd = nc.dram_tensor("sd", [128, FC], F32, kind="ExternalInput").ap()
    xo = nc.dram_tensor("xo", [D, TOK], F32, kind="ExternalOutput").ap()

    xlnb = p.sbuf("xlnb", [128, FC * TOK], BF16)
    fuse = p.sbuf("fuse", [128, NI * 512], BF16)
    GSLOT = 3 if layer == 0 else 2
    DSLOT = 2
    stageg = p.sbuf("stageg", [128, GSLOT * 1024], WDT)
    stageu = p.sbuf("stageu", [128, GSLOT * 1024], WDT)
    staged = p.sbuf("staged", [128, DSLOT * 2048], WDT)
    wgb = p.sbuf("wgb", [128, 2 * FC * 256], BF16)
    wub = p.sbuf("wub", [128, 2 * FC * 256], BF16)
    wdb = p.sbuf("wdb", [128, 2 * 2048], BF16)
    sub = p.sbuf("sub", [128, FC * 512], F32)
    sqb = p.sbuf("sqb", [128, 2 * 512], BF16)
    xbb = p.sbuf("xbb", [128, 2 * 512], BF16)
    xin = p.sbuf("xin", [128, 2 * 512], F32)
    gs = p.sbuf("gs", [128, 2 * 512], F32)
    us = p.sbuf("us", [128, 512], F32)
    t1 = us
    a_bc = p.sbuf("a_bc", [128, 512], F32)
    b_bc = p.sbuf("b_bc", [128, 512], F32)
    sgb = p.sbuf("sgb", [128, NI], F32)
    sub_s = p.sbuf("sub_s", [128, NI], F32)
    sdb = p.sbuf("sdb", [128, FC], F32)
    ones_b = p.sbuf("ones_b", [128, 1], BF16)
    ones_r1 = p.sbuf("ones_r1", [1, 128], F32)
    rows = {k: p.sbuf("row_" + k, [1, 512], F32)
            for k in ("v", "mean", "rstd")}
    rows["t1"] = a_bc
    ps = [p.psum(f"ps{i}", [128, 512], F32) for i in range(8)]

    for s in ("s_kc", "s_g0", "s_g1", "s_g2", "s_g3", "s_d0", "s_d1", "s_d2", "s_d3",
              "s_x0", "s_x1", "s_o0", "s_o1", "s_c", "s_cv", "s_m", "s_v", "s_s"):
        p.sem(s)

    p.em("vector", lambda e: e.memset(ones_b[:, :], 1.0))
    vset = p.inc("s_v")
    p.em("vector", lambda e, s=vset[0]: e.memset(ones_r1[:, :], 1.0).then_inc(s, 1))
    p.wait("tensor", "s_v", vset[1])

    p.dma("sync", sgb[:, :], sg, "s_kc")
    p.dma("sync", sub_s[:, :], su, "s_kc")
    sc_loaded = p.dma("sync", sdb[:, :], sd, "s_kc")
    p.wait("vector", "s_kc", sc_loaded)
    p.wait("scalar", "s_kc", sc_loaded)

    xin_free = [0, 0]
    stats_mm_at = [0, 0]

    def stats_tile(fc, src_builder, w=512):
        pb = fc % 2
        p.wait("vector", "s_m", stats_mm_at[pb])
        vv = p.inc("s_v")
        p.em("vector", lambda e, pb=pb, s=vv[0]: src_builder(
            e, xbb[:, pb * 512:pb * 512 + w]).then_inc(s, 1))
        p.wait("scalar", "s_v", vv[1])
        sc = p.inc("s_s")
        p.em("scalar", lambda e, pb=pb, s=sc[0]: e.activation(
            out=sqb[:, pb * 512:pb * 512 + w], in_=xbb[:, pb * 512:pb * 512 + w],
            func=AF.Square).then_inc(s, 1))
        p.wait("tensor", "s_s", sc[1])
        p.em("tensor", lambda e, fc=fc, pb=pb: e.matmul(
            ps[4][0:1, 0:w], ones_b[:, :], xbb[:, pb * 512:pb * 512 + w],
            start=(fc == 0), stop=(fc == FC - 1)))
        mmv = p.inc("s_m")
        p.em("tensor", lambda e, fc=fc, pb=pb, s=mmv[0]: e.matmul(
            ps[5][0:1, 0:w], ones_b[:, :], sqb[:, pb * 512:pb * 512 + w],
            start=(fc == 0), stop=(fc == FC - 1)).then_inc(s, 1))
        stats_mm_at[pb] = mmv[1]

    def ln_pass(t0):
        for fc in range(FC):
            xb_ = fc % 2
            p.wait("gpsimd", "s_v", xin_free[xb_])
            xcur = p.dma("gpsimd", xin[:, xb_ * 512:(xb_ + 1) * 512],
                         xown[fc * 128:(fc + 1) * 128, t0:t0 + 512], f"s_x{xb_}")
            p.wait("vector", f"s_x{xb_}", xcur)

            def mk(e, out, xb_=xb_):
                return e.tensor_copy(out, xin[:, xb_ * 512:(xb_ + 1) * 512])
            stats_tile(fc, mk)
            xin_free[xb_] = p.cnt["s_v"]
        emit_ln_rows(p, rows, ps[4], ps[5], ps[6], a_bc, b_bc, ones_r1, D, p.cnt["s_m"])
        for fc in range(FC):
            xb_ = fc % 2
            p.wait("gpsimd", "s_v", xin_free[xb_])
            xcur = p.dma("gpsimd", xin[:, xb_ * 512:(xb_ + 1) * 512],
                         xown[fc * 128:(fc + 1) * 128, t0:t0 + 512], f"s_x{xb_}")
            p.wait("vector", f"s_x{xb_}", xcur)
            p.em("vector", lambda e, xb_=xb_: e.tensor_add(
                t1[:, :], xin[:, xb_ * 512:(xb_ + 1) * 512], b_bc[:, :]))
            vv = p.inc("s_v")
            p.em("vector", lambda e, fc=fc, t0=t0, s=vv[0]: e.tensor_mul(
                xlnb[:, fc * TOK + t0: fc * TOK + t0 + 512], t1[:, :],
                a_bc[:, :]).then_inc(s, 1))
            xin_free[xb_] = vv[1]

    for tt in range(2):
        ln_pass(tt * 512)
    xln_done = p.cnt["s_v"]
    p.wait("tensor", "s_v", xln_done)

    stg_grp_at = [(0, 0)] * GSLOT
    panel_mm_at = [0, 0]
    panel_cast_done = [(0, 0), (0, 0)]
    fuse_done_at = {}
    std_cast_at = [None] * DSLOT
    down_mm_at = {}

    for tt in range(2):
        t0 = tt * 512
        for pi in range(NP):
            pb = pi % 2
            def emit_cast(gr, wait_val):
                sl = gr % GSLOT
                p.wait("scalar", f"s_g{sl}", wait_val)
                p.wait("vector", f"s_g{sl}", wait_val)
                if gr == 0:
                    p.wait("scalar", "s_m", panel_mm_at[pb])
                    p.wait("vector", "s_m", panel_mm_at[pb])
                ccg = p.inc("s_c")
                p.em("scalar", lambda e, gr=gr, pb=pb, sl=sl, s=ccg[0]: e.copy(
                    out=wgb[:, pb * FC * 256 + gr * 1024: pb * FC * 256 + (gr + 1) * 1024],
                    in_=stageg[:, sl * 1024:(sl + 1) * 1024]).then_inc(s, 1))
                cc = p.inc("s_cv")
                p.em("vector", lambda e, gr=gr, pb=pb, sl=sl, s=cc[0]: e.tensor_copy(
                    wub[:, pb * FC * 256 + gr * 1024: pb * FC * 256 + (gr + 1) * 1024],
                    stageu[:, sl * 1024:(sl + 1) * 1024]).then_inc(s, 1))
                stg_grp_at[sl] = (ccg[1], cc[1])
            for gr in range(4):
                sl = gr % GSLOT
                p.wait("sync", "s_c", stg_grp_at[sl][0])
                p.wait("sync", "s_cv", stg_grp_at[sl][1])
                p.dma("sync",
                      stageg[:, sl * 1024:(sl + 1) * 1024].rearrange("p (j c) -> p j c", j=4),
                      wg[gr * 512:(gr + 1) * 512, pi * 256:(pi + 1) * 256].rearrange(
                          "(j p) c -> p j c", p=128), f"s_g{sl}")
                wcur = p.dma("sync",
                             stageu[:, sl * 1024:(sl + 1) * 1024].rearrange("p (j c) -> p j c", j=4),
                             wu[gr * 512:(gr + 1) * 512, pi * 256:(pi + 1) * 256].rearrange(
                                 "(j p) c -> p j c", p=128), f"s_g{sl}")
                emit_cast(gr, wcur)
            panel_cast_done[pb] = (p.cnt["s_c"], p.cnt["s_cv"])
            p.wait("tensor", "s_c", panel_cast_done[pb][0])
            p.wait("tensor", "s_cv", panel_cast_done[pb][1])
            for il in range(2):
                ic = pi * 2 + il
                bk = 2 * (ic % 2)
                if ic >= 2:
                    p.wait("tensor", "s_v", fuse_done_at[ic - 2])
                for fc in range(FC):
                    p.em("tensor", lambda e, fc=fc, il=il, t0=t0, pb=pb, bk=bk: e.matmul(
                        ps[bk][:, :], wgb[:, pb * FC * 256 + fc * 256 + il * 128:
                                          pb * FC * 256 + fc * 256 + il * 128 + 128],
                        xlnb[:, fc * TOK + t0: fc * TOK + t0 + 512],
                        start=(fc == 0), stop=(fc == FC - 1)))
                for fc in range(FC):
                    last = fc == FC - 1
                    if last:
                        mmv = p.inc("s_m")
                        p.em("tensor", lambda e, fc=fc, il=il, t0=t0, pb=pb, bk=bk, s=mmv[0]:
                             e.matmul(ps[bk + 1][:, :],
                                      wub[:, pb * FC * 256 + fc * 256 + il * 128:
                                          pb * FC * 256 + fc * 256 + il * 128 + 128],
                                      xlnb[:, fc * TOK + t0: fc * TOK + t0 + 512],
                                      start=(fc == 0), stop=True).then_inc(s, 1))
                    else:
                        p.em("tensor", lambda e, fc=fc, il=il, t0=t0, pb=pb, bk=bk: e.matmul(
                            ps[bk + 1][:, :],
                            wub[:, pb * FC * 256 + fc * 256 + il * 128:
                                pb * FC * 256 + fc * 256 + il * 128 + 128],
                            xlnb[:, fc * TOK + t0: fc * TOK + t0 + 512],
                            start=(fc == 0), stop=False))
                gu_mm = p.cnt["s_m"]
                sb_ = ic % 2
                p.wait("scalar", "s_m", gu_mm)
                sc = p.inc("s_s")
                p.em("scalar", lambda e, ic=ic, bk=bk, sb_=sb_, s=sc[0]: e.activation(
                    out=gs[:, sb_ * 512:(sb_ + 1) * 512], in_=ps[bk][:, :], func=AF.Silu,
                    scale=sgb[:, ic:ic + 1]).then_inc(s, 1))
                p.wait("vector", "s_m", gu_mm)
                p.em("vector", lambda e, ic=ic, bk=bk: e.tensor_scalar(
                    us[:, :], ps[bk + 1][:, :],
                    sub_s[:, ic:ic + 1], None, ALU.mult))
                p.wait("vector", "s_s", sc[1])
                vv = p.inc("s_v")
                p.em("vector", lambda e, ic=ic, sb_=sb_, s=vv[0]: e.tensor_mul(
                    fuse[:, ic * 512:(ic + 1) * 512], gs[:, sb_ * 512:(sb_ + 1) * 512],
                    us[:, :]).then_inc(s, 1))
                fuse_done_at[ic] = vv[1]
            panel_mm_at[pb] = p.cnt["s_m"]
        fuse_done = p.cnt["s_v"]
        p.wait("tensor", "s_v", fuse_done)

        for hg in range(2):
            for ic in range(NI):
                dbuf = ic % DSLOT
                if std_cast_at[dbuf] is not None:
                    p.wait("sync", std_cast_at[dbuf][0], std_cast_at[dbuf][1])
                wcur = p.dma("sync", staged[:, dbuf * 2048:(dbuf + 1) * 2048],
                             wd[ic * 128:(ic + 1) * 128, :], f"s_d{dbuf}")
                ceng = "vector" if ic % 2 == 0 else "scalar"
                p.wait(ceng, f"s_d{dbuf}", wcur)
                key = (hg, ic - DSLOT)
                if key in down_mm_at:
                    p.wait(ceng, "s_m", down_mm_at[key])
                elif hg == 1 and ic < DSLOT:
                    p.wait(ceng, "s_m", down_mm_at[(0, NI - DSLOT + ic)])
                csem = "s_cv" if ceng == "vector" else "s_c"
                cc = p.inc(csem)
                if ceng == "vector":
                    p.em("vector", lambda e, dbuf=dbuf, s=cc[0]: e.tensor_copy(
                        wdb[:, dbuf * 2048:(dbuf + 1) * 2048],
                        staged[:, dbuf * 2048:(dbuf + 1) * 2048]).then_inc(s, 1))
                else:
                    p.em("scalar", lambda e, dbuf=dbuf, s=cc[0]: e.copy(
                        out=wdb[:, dbuf * 2048:(dbuf + 1) * 2048],
                        in_=staged[:, dbuf * 2048:(dbuf + 1) * 2048]).then_inc(s, 1))
                std_cast_at[dbuf] = (csem, cc[1])
                p.wait("tensor", csem, cc[1])
                for hl in range(8):
                    hc = hg * 8 + hl
                    last = ic == NI - 1
                    if hl == 7:
                        mmv = p.inc("s_m")
                        p.em("tensor", lambda e, dbuf=dbuf, hc=hc, hl=hl, ic=ic, last=last, s=mmv[0]:
                             e.matmul(ps[hl][:, :],
                                      wdb[:, dbuf * 2048 + hc * 128: dbuf * 2048 + hc * 128 + 128],
                                      fuse[:, ic * 512:(ic + 1) * 512],
                                      start=(ic == 0), stop=last).then_inc(s, 1))
                    else:
                        p.em("tensor", lambda e, dbuf=dbuf, hc=hc, hl=hl, ic=ic, last=last:
                             e.matmul(ps[hl][:, :],
                                      wdb[:, dbuf * 2048 + hc * 128: dbuf * 2048 + hc * 128 + 128],
                                      fuse[:, ic * 512:(ic + 1) * 512],
                                      start=(ic == 0), stop=last))
                down_mm_at[(hg, ic)] = p.cnt["s_m"]
            p.wait("scalar", "s_m", p.cnt["s_m"])
            for hl in range(8):
                hc = hg * 8 + hl
                sc = p.inc("s_s")
                p.em("scalar", lambda e, hl=hl, hc=hc, s=sc[0]: e.activation(
                    out=sub[:, hc * 512:(hc + 1) * 512], in_=ps[hl][:, :],
                    func=AF.Copy, scale=sdb[:, hc:hc + 1]).then_inc(s, 1))
            p.wait("tensor", "s_s", p.cnt["s_s"])
        evac_done = p.cnt["s_s"]

        p.wait("vector", "s_s", evac_done)
        for hc in range(FC):
            def mk(e, out, hc=hc):
                return e.tensor_copy(out, sub[:, hc * 512:(hc + 1) * 512])
            stats_tile(hc, mk)
        emit_ln_rows(p, rows, ps[4], ps[5], ps[6], a_bc, b_bc, ones_r1, D, p.cnt["s_m"])
        for fc in range(FC):
            xb_ = fc % 2
            p.wait("gpsimd", "s_v", xin_free[xb_])
            xcur = p.dma("gpsimd", xin[:, xb_ * 512:(xb_ + 1) * 512],
                         xown[fc * 128:(fc + 1) * 128, t0:t0 + 512], f"s_x{xb_}")
            p.wait("vector", f"s_x{xb_}", xcur)
            p.em("vector", lambda e, fc=fc: e.tensor_add(
                t1[:, :], sub[:, fc * 512:(fc + 1) * 512], b_bc[:, :]))
            p.em("vector", lambda e: e.tensor_mul(t1[:, :], t1[:, :], a_bc[:, :]))
            vv = p.inc("s_v")
            p.em("vector", lambda e, fc=fc, xb_=xb_, s=vv[0]: e.tensor_add(
                sub[:, fc * 512:(fc + 1) * 512], t1[:, :],
                xin[:, xb_ * 512:(xb_ + 1) * 512]).then_inc(s, 1))
            xin_free[xb_] = vv[1]
            p.wait("gpsimd", "s_v", vv[1])
            p.dma("gpsimd", xo[fc * 128:(fc + 1) * 128, t0:t0 + 512],
                  sub[:, fc * 512:(fc + 1) * 512], "s_o0")
        p.wait("scalar", "s_o0", p.cnt["s_o0"])
    p.wait("gpsimd", "s_o0", p.cnt["s_o0"])
    return p.build()



def _run(nc, in_maps):
    return run_bass_kernel_spmd(nc, in_maps, core_ids=list(range(8))).results


def _scale_2d(v, nchunk):
    return np.ascontiguousarray(v.reshape(nchunk, 128).T)


def run_a1(layer, xT, wq_, wk_, wv_, qs_, ks_):
    ropec, ropes = _rope_tables()
    identf = np.eye(128, dtype=np.float32)
    _, masks_ = _attn_sched(layer)
    nc1 = build_a1(layer)
    in_maps = []
    for c in range(8):
        b, r = c // 2, c % 2
        m = {"xT": xT[b],
             "wq": np.ascontiguousarray(wq_[:, r * 1024:(r + 1) * 1024]),
             "wk": np.ascontiguousarray(wk_[:, r * 256:(r + 1) * 256]),
             "wv": np.ascontiguousarray(wv_[:, r * 256:(r + 1) * 256]),
             "ropec": ropec, "ropes": ropes, "masks": masks_, "identf": identf}
        if layer == 0:
            m["qs"] = qs_
            m["ks"] = ks_
        in_maps.append(m)
    return [r_["ot"] for r_ in _run(nc1, in_maps)]


def run_a2(xT, ots, wo_):
    nc2 = build_a2()
    in_maps = []
    for c in range(8):
        b, r = c // 2, c % 2
        otf = np.concatenate([ots[2 * b][:, r * 1024:(r + 1) * 1024],
                              ots[2 * b + 1][:, r * 1024:(r + 1) * 1024]], axis=0)
        in_maps.append({"otf": np.ascontiguousarray(otf), "wo": wo_,
                        "xown": np.ascontiguousarray(xT[b][:, r * 1024:(r + 1) * 1024])})
    xos = [r_["xo"] for r_ in _run(nc2, in_maps)]
    return [np.concatenate([xos[2 * b], xos[2 * b + 1]], axis=1) for b in range(B)]


def attn_block(layer, xT, wq_, wk_, wv_, wo_, qs_, ks_):
    ots = run_a1(layer, xT, wq_, wk_, wv_, qs_, ks_)
    return run_a2(xT, ots, wo_)


def ffn_block(layer, xT, wg_, wu_, wd_, sg_, su_, sd_):
    nc = build_ffn(layer)
    I = wg_.shape[1]
    in_maps = []
    for c in range(8):
        b, r = c // 2, c % 2
        in_maps.append({"xown": np.ascontiguousarray(xT[b][:, r * 1024:(r + 1) * 1024]),
                        "wg": wg_, "wu": wu_, "wd": wd_,
                        "sg": _scale_2d(sg_, I // 128),
                        "su": _scale_2d(su_, I // 128),
                        "sd": _scale_2d(sd_, FC)})
    xos = [r_["xo"] for r_ in _run(nc, in_maps)]
    return [np.concatenate([xos[2 * b], xos[2 * b + 1]], axis=1) for b in range(B)]


def kernel(L0_attn_hidden_states, L0_attn_wq, L0_attn_wk, L0_attn_wv, L0_attn_wo,
           L0_attn_query_scale, L0_attn_key_scale,
           L0_ffn_w_gate, L0_ffn_w_up, L0_ffn_w_down,
           L0_ffn_s_gate, L0_ffn_s_up, L0_ffn_s_down,
           L1_attn_Wq, L1_attn_Wk, L1_attn_Wv, L1_attn_Wo,
           L1_ffn_gate_w, L1_ffn_up_w, L1_ffn_down_w,
           L1_ffn_gate_s, L1_ffn_up_s, L1_ffn_down_s):
    xT = [np.ascontiguousarray(L0_attn_hidden_states[b].T) for b in range(B)]
    xT = attn_block(0, xT, L0_attn_wq, L0_attn_wk, L0_attn_wv, L0_attn_wo,
                    L0_attn_query_scale, L0_attn_key_scale)
    xT = ffn_block(0, xT, L0_ffn_w_gate, L0_ffn_w_up, L0_ffn_w_down,
                   L0_ffn_s_gate, L0_ffn_s_up, L0_ffn_s_down)
    xT = attn_block(1, xT, L1_attn_Wq, L1_attn_Wk, L1_attn_Wv, L1_attn_Wo,
                    None, None)
    xT = ffn_block(1, xT,
                   np.ascontiguousarray(L1_ffn_gate_w.T),
                   np.ascontiguousarray(L1_ffn_up_w.T),
                   np.ascontiguousarray(L1_ffn_down_w.T),
                   L1_ffn_gate_s, L1_ffn_up_s, L1_ffn_down_s)
    return np.stack([xT[b].T for b in range(B)]).astype(np.float32)


# version 05
# speedup vs baseline: 1.4458x; 1.0108x over previous
import numpy as np
import ml_dtypes
from contextlib import ExitStack

import concourse.bass as bass
import concourse.mybir as mybir
from concourse.bass_utils import run_bass_kernel_spmd

F32 = mybir.dt.float32
BF16 = mybir.dt.bfloat16
I8 = mybir.dt.int8
AF = mybir.ActivationFunctionType
ALU = mybir.AluOpType
BF = ml_dtypes.bfloat16

D = 2048
S = 2048
B = 4
HD = 128
NH = 8
NKV = 2
TOK = 1024
WIN = 1024
CAP = 50.0
LN_EPS = 1e-5
QK_EPS = 1e-6
SCALE = 1.0 / float(np.sqrt(HD))
FC = D // 128


def _rope_tables():
    half = HD // 2
    inv = 1.0 / 10000.0 ** (np.arange(half, dtype=np.float32) / half)
    ang = np.outer(np.arange(S, dtype=np.float32), inv)
    c = np.cos(ang).T.astype(np.float32)
    s = np.sin(ang).T.astype(np.float32)
    cos2 = np.concatenate([c, c], axis=0)
    sin2 = np.concatenate([s, s], axis=0)
    return np.ascontiguousarray(cos2), np.ascontiguousarray(sin2)


def _attn_sched(layer):
    tiles = []
    keys = {}
    sched = {}
    for qg in range(4):
        for c in range(16):
            q = qg * 512 + np.arange(512)[None, :]
            k = c * 128 + np.arange(128)[:, None]
            valid = (np.abs(q - k) <= WIN) if layer == 0 else (k <= q)
            if not valid.any():
                continue
            if valid.all():
                sched[(qg, c)] = None
            else:
                t = valid.astype(np.float32)
                key = t.tobytes()
                if key not in keys:
                    keys[key] = len(tiles)
                    tiles.append(t)
                sched[(qg, c)] = keys[key]
    arr = np.stack(tiles).astype(BF)
    return sched, arr



ENGS = ("sync", "tensor", "vector", "scalar", "gpsimd")


class Prog:
    def __init__(self, name):
        self.nc = bass.Bass("TRN2", target_bir_lowering=False, debug=False)
        self.name = name
        self.ops = {e: [] for e in ENGS}
        self.stack = ExitStack()
        self.scopes = []
        self.sems = {}
        self.cnt = {}

    def sem(self, name):
        self.sems[name] = self.stack.enter_context(self.nc.semaphore(name))
        self.cnt[name] = 0
        return name

    def sbuf(self, name, shape, dt):
        st = self.scopes[-1] if self.scopes else self.stack
        return st.enter_context(self.nc.sbuf_tensor(name, shape, dt))

    def open_scope(self):
        self.scopes.append(ExitStack())

    def close_scope(self):
        self.scopes.pop().close()

    def barrier(self, dma_sems=()):
        vals = [(s, self.cnt[s]) for s in ("s_m", "s_v", "s_s", "s_c")
                if s in self.cnt and self.cnt[s] > 0]
        vals += [(s, self.cnt[s]) for s in dma_sems if self.cnt.get(s, 0) > 0]
        for eng in ENGS:
            for s, v in vals:
                self.wait(eng, s, v)

    def psum(self, name, shape, dt):
        return self.stack.enter_context(self.nc.psum_tensor(name, shape, dt))

    def em(self, eng, fn):
        self.ops[eng].append(fn)

    def wait(self, eng, sem, val):
        if val <= 0:
            return
        h = self.sems[sem]
        self.em(eng, lambda e, h=h, v=val: e.wait_ge(h, v))

    def inc(self, sem):
        self.cnt[sem] += 1
        return self.sems[sem], self.cnt[sem]

    def dma(self, eng, out, in_, sem, slow=False):
        h = self.sems[sem]
        self.cnt[sem] += 16
        if slow:
            self.em(eng, lambda e, o=out, i=in_, h=h: e.dma_start(
                out=o, in_=i, allow_slow_non_contiguous=True).then_inc(h, 16))
        else:
            self.em(eng, lambda e, o=out, i=in_, h=h: e.dma_start(out=o, in_=i).then_inc(h, 16))
        return self.cnt[sem]

    def build(self):
        with self.nc.Block() as block:
            def runner(name):
                def body(eng):
                    for fn in self.ops[name]:
                        fn(eng)
                return body
            block.sync(runner("sync"))
            block.tensor(runner("tensor"))
            block.vector(runner("vector"))
            block.scalar(runner("scalar"))
            block.gpsimd(runner("gpsimd"))
        self.stack.close()
        return self.nc



def emit_ln_rows(p, rows, ps_s, ps_ss, ps_bc, a_bc, b_bc, ones_r1, n_feat, wait_mm_val, w=512):
    inv_n = 1.0 / n_feat
    r_v, r_mean, r_t1, r_rstd = (rows[k] for k in ("v", "mean", "t1", "rstd"))
    p.wait("vector", "s_m", wait_mm_val)
    p.em("vector", lambda e: e.tensor_scalar_mul(r_mean[0:1, 0:w], ps_s[0:1, 0:w], inv_n))
    p.em("vector", lambda e: e.tensor_scalar_mul(r_v[0:1, 0:w], ps_ss[0:1, 0:w], inv_n))
    p.em("vector", lambda e: e.tensor_mul(r_t1[0:1, 0:w], r_mean[0:1, 0:w], r_mean[0:1, 0:w]))
    p.em("vector", lambda e: e.tensor_sub(r_v[0:1, 0:w], r_v[0:1, 0:w], r_t1[0:1, 0:w]))
    p.em("vector", lambda e: e.tensor_scalar_add(r_v[0:1, 0:w], r_v[0:1, 0:w], LN_EPS))
    vv = p.inc("s_v")
    p.em("vector", lambda e, s=vv[0]: e.reciprocal(r_t1[0:1, 0:w], r_v[0:1, 0:w]).then_inc(s, 1))
    p.wait("scalar", "s_v", vv[1])
    sc1 = p.inc("s_s")
    p.em("scalar", lambda e, s=sc1[0]: e.activation(out=r_rstd[0:1, 0:w], in_=r_t1[0:1, 0:w],
                                                    func=AF.Sqrt).then_inc(s, 1))
    p.wait("vector", "s_s", sc1[1])
    p.em("vector", lambda e: e.tensor_mul(r_t1[0:1, 0:w], r_rstd[0:1, 0:w], r_rstd[0:1, 0:w]))
    p.em("vector", lambda e: e.scalar_tensor_tensor(r_t1[0:1, 0:w], r_v[0:1, 0:w], -0.5,
                                                    r_t1[0:1, 0:w], ALU.mult, ALU.mult))
    p.em("vector", lambda e: e.tensor_scalar_add(r_t1[0:1, 0:w], r_t1[0:1, 0:w], 1.5))
    vv1 = p.inc("s_v")
    p.em("vector", lambda e, s=vv1[0]: e.tensor_mul(r_rstd[0:1, 0:w], r_rstd[0:1, 0:w],
                                                    r_t1[0:1, 0:w]).then_inc(s, 1))
    p.wait("tensor", "s_v", vv1[1])
    mm1 = p.inc("s_m")
    p.em("tensor", lambda e, s=mm1[0]: e.matmul(ps_bc[:, 0:w], ones_r1[0:1, :], r_rstd[0:1, 0:w],
                                                start=True, stop=True).then_inc(s, 1))
    p.wait("vector", "s_m", mm1[1])
    p.em("vector", lambda e: e.tensor_copy(a_bc[:, 0:w], ps_bc[:, 0:w]))
    vv2 = p.inc("s_v")
    p.em("vector", lambda e, s=vv2[0]: e.tensor_scalar_mul(
        r_mean[0:1, 0:w], r_mean[0:1, 0:w], -1.0).then_inc(s, 1))
    p.wait("tensor", "s_v", vv2[1])
    mm2 = p.inc("s_m")
    p.em("tensor", lambda e, s=mm2[0]: e.matmul(ps_bc[:, 0:w], ones_r1[0:1, :], r_mean[0:1, 0:w],
                                                start=True, stop=True).then_inc(s, 1))
    p.wait("vector", "s_m", mm2[1])
    vv3 = p.inc("s_v")
    p.em("vector", lambda e, s=vv3[0]: e.tensor_copy(b_bc[:, 0:w], ps_bc[:, 0:w]).then_inc(s, 1))
    return vv3[1]



def build_a1(layer):
    sched, mask_arr = _attn_sched(layer)
    nm = mask_arr.shape[0]
    p = Prog(f"a1_l{layer}")
    nc = p.nc

    xT = nc.dram_tensor("xT", [D, S], F32, kind="ExternalInput").ap()
    wq = nc.dram_tensor("wq", [D, NH * HD], F32, kind="ExternalInput").ap()
    wk = nc.dram_tensor("wk", [D, NKV * HD], F32, kind="ExternalInput").ap()
    wv = nc.dram_tensor("wv", [D, NKV * HD], F32, kind="ExternalInput").ap()
    ropec = nc.dram_tensor("ropec", [128, S], F32, kind="ExternalInput").ap()
    ropes = nc.dram_tensor("ropes", [128, S], F32, kind="ExternalInput").ap()
    masks = nc.dram_tensor("masks", [nm, 128, 512], BF16, kind="ExternalInput").ap()
    identf = nc.dram_tensor("identf", [128, 128], F32, kind="ExternalInput").ap()
    if layer == 0:
        qsv = nc.dram_tensor("qs", [HD], F32, kind="ExternalInput").ap()
        ksv = nc.dram_tensor("ks", [HD], F32, kind="ExternalInput").ap()
    ot = nc.dram_tensor("ot", [NH * HD, S], BF16, kind="ExternalOutput").ap()

    xlnb = p.sbuf("xlnb", [128, FC * S], BF16)
    xbuf = p.sbuf("xbuf", [128, FC * 256], F32)
    xbb = p.sbuf("xbb", [128, 512], BF16)
    sqb = p.sbuf("sqb", [128, 512], BF16)
    stage = p.sbuf("stage", [128, 1024], F32)
    wqb = p.sbuf("wqb", [128, 2 * FC * HD], BF16)
    wkb = p.sbuf("wkb", [128, FC * NKV * HD], BF16)
    wvb = p.sbuf("wvb", [128, FC * NKV * HD], BF16)
    Kb = p.sbuf("Kb", [128, NKV * S], BF16)
    Vtok = p.sbuf("Vtok", [128, 16 * NKV * HD], BF16)
    tblc = p.sbuf("tblc", [128, S], F32)
    tbls = p.sbuf("tbls", [128, S], F32)
    maskb = p.sbuf("maskb", [128, nm * 512], BF16)
    idf = p.sbuf("idf", [128, 128], F32)
    kr = p.sbuf("kr", [128, 512], F32)
    k2 = p.sbuf("k2", [128, 512], F32)
    ktmp = p.sbuf("ktmp", [128, 512], F32)
    kf = p.sbuf("kf", [128, 512], F32)
    vf = p.sbuf("vf", [128, 512], F32)
    qb = p.sbuf("qb", [128, 2 * S], BF16)
    ptile = p.sbuf("ptile", [128, 2 * 1024], BF16)
    pmk = p.sbuf("pmk", [128, 1024], BF16)
    ttile = p.sbuf("ttile", [128, 1024], F32)
    blb = p.sbuf("blb", [128, 512], F32)
    obuf = p.sbuf("obuf", [128, 2 * 512], BF16)
    a_bc = p.sbuf("a_bc", [128, 512], F32)
    b_bc = p.sbuf("b_bc", [128, 512], F32)
    ones_b = p.sbuf("ones_b", [128, 1], BF16)
    ones_r1 = p.sbuf("ones_r1", [1, 128], F32)
    biasb = p.sbuf("biasb", [128, 1], F32)
    rows = {k: p.sbuf("row_" + k, [1, 512], F32) for k in ("v", "mean", "rstd")}
    rows["t1"] = a_bc
    rows["l"] = b_bc
    if layer == 0:
        qsb = p.sbuf("qsb", [128, 1], F32)
        ksb = p.sbuf("ksb", [128, 1], F32)
        nrows = {k: p.sbuf("nrow_" + k, [1, 512], F32) for k in ("v", "t", "d")}

    pA = p.psum("pA", [128, 1024], F32)
    pB = p.psum("pB", [128, 1024], F32)
    pl = p.psum("pl", [128, 512], F32)
    pO = p.psum("pO", [128, 512], F32)
    pq = p.psum("pq", [128, 1024], F32)

    for s in ("s_x", "s_w", "s_kc", "s_o0", "s_o1", "s_c", "s_m", "s_v", "s_s"):
        p.sem(s)

    p.dma("sync", tblc[:, :], ropec, "s_kc")
    p.dma("sync", tbls[:, :], ropes, "s_kc")
    p.dma("sync", maskb[:].rearrange("p (n m) -> p n m", n=nm),
          masks.rearrange("n p m -> p n m"), "s_kc")
    p.dma("sync", idf[:, :], identf, "s_kc")
    if layer == 0:
        p.dma("sync", qsb[:, :], qsv.rearrange("(p o) -> p o", o=1), "s_kc")
        p.dma("sync", ksb[:, :], ksv.rearrange("(p o) -> p o", o=1), "s_kc")
    w_const = p.cnt["s_kc"]
    p.wait("vector", "s_kc", w_const)
    p.em("vector", lambda e: e.memset(ones_b[:, :], 1.0))
    exp_bias = -CAP if layer == 0 else -100.0 * SCALE
    p.em("vector", lambda e, b=exp_bias: e.memset(biasb[:, :], b))
    vset = p.inc("s_v")
    p.em("vector", lambda e, s=vset[0]: e.memset(ones_r1[:, :], 1.0).then_inc(s, 1))
    p.wait("tensor", "s_v", vset[1])
    p.wait("scalar", "s_v", vset[1])
    p.wait("gpsimd", "s_v", vset[1])

    for fc in range(FC):
        wcur = p.dma("sync", stage[:, 0:256], wk[fc * 128:(fc + 1) * 128, :], "s_w")
        p.wait("gpsimd", "s_w", wcur)
        cc0 = p.inc("s_c")
        p.em("gpsimd", lambda e, fc=fc, s=cc0[0]: e.tensor_copy(
            wkb[:, fc * 256:(fc + 1) * 256], stage[:, 0:256]).then_inc(s, 1))
        p.wait("sync", "s_c", cc0[1])
        wcur = p.dma("sync", stage[:, 256:512], wv[fc * 128:(fc + 1) * 128, :], "s_w")
        p.wait("gpsimd", "s_w", wcur)
        cc1 = p.inc("s_c")
        p.em("gpsimd", lambda e, fc=fc, s=cc1[0]: e.tensor_copy(
            wvb[:, fc * 256:(fc + 1) * 256], stage[:, 256:512]).then_inc(s, 1))
        p.wait("sync", "s_c", cc1[1])
    kv_cast_done = p.cnt["s_c"]

    W0 = 256
    a1_stats_at = [0, 0]
    for tt in range(S // W0):
        t0 = tt * W0
        for fc in range(FC):
            p.dma("gpsimd", xbuf[:, fc * W0:(fc + 1) * W0],
                  xT[fc * 128:(fc + 1) * 128, t0:t0 + W0], "s_x")
        x_done = p.cnt["s_x"]
        p.wait("vector", "s_x", x_done)
        for fc in range(FC):
            pb2 = fc % 2
            p.wait("vector", "s_m", a1_stats_at[pb2])
            vv = p.inc("s_v")
            p.em("vector", lambda e, fc=fc, pb2=pb2, s=vv[0]: e.tensor_copy(
                xbb[:, pb2 * 256:pb2 * 256 + W0],
                xbuf[:, fc * W0:(fc + 1) * W0]).then_inc(s, 1))
            p.wait("scalar", "s_v", vv[1])
            sc = p.inc("s_s")
            p.em("scalar", lambda e, pb2=pb2, s=sc[0]: e.activation(
                out=sqb[:, pb2 * 256:pb2 * 256 + W0],
                in_=xbb[:, pb2 * 256:pb2 * 256 + W0], func=AF.Square).then_inc(s, 1))
            p.wait("tensor", "s_s", sc[1])
            p.em("tensor", lambda e, fc=fc, pb2=pb2: e.matmul(
                pl[0:1, 0:W0], ones_b[:, :], xbb[:, pb2 * 256:pb2 * 256 + W0],
                start=(fc == 0), stop=(fc == FC - 1)))
            mmv = p.inc("s_m")
            p.em("tensor", lambda e, fc=fc, pb2=pb2, s=mmv[0]: e.matmul(
                pO[0:1, 0:W0], ones_b[:, :], sqb[:, pb2 * 256:pb2 * 256 + W0],
                start=(fc == 0), stop=(fc == FC - 1)).then_inc(s, 1))
            a1_stats_at[pb2] = mmv[1]
        emit_ln_rows(p, rows, pl, pO, pA, a_bc, b_bc, ones_r1, D, p.cnt["s_m"], w=W0)
        for fc in range(FC):
            p.em("vector", lambda e, fc=fc: e.tensor_add(
                xbuf[:, fc * W0:(fc + 1) * W0], xbuf[:, fc * W0:(fc + 1) * W0], b_bc[:, 0:W0]))
            vv = p.inc("s_v")
            p.em("vector", lambda e, fc=fc, t0=t0, s=vv[0]: e.tensor_mul(
                xlnb[:, fc * S + t0: fc * S + t0 + W0],
                xbuf[:, fc * W0:(fc + 1) * W0], a_bc[:, 0:W0]).then_inc(s, 1))
        p.wait("gpsimd", "s_v", p.cnt["s_v"])
    xln_done_v = p.cnt["s_v"]

    def rope(src, t0, dst, col):
        p.em("vector", lambda e: e.tensor_copy(ktmp[0:64, :], src[64:128, :]))
        p.em("vector", lambda e: e.tensor_copy(ktmp[64:128, :], src[0:64, :]))
        p.em("vector", lambda e, t0=t0: e.tensor_mul(
            k2[:, :], ktmp[:, :], tbls[:, t0:t0 + 512]))
        p.em("vector", lambda e, t0=t0: e.tensor_mul(
            ktmp[:, :], src[:, :], tblc[:, t0:t0 + 512]))
        p.em("vector", lambda e: e.tensor_sub(dst[0:64, col:col + 512],
                                              ktmp[0:64, :], k2[0:64, :]))
        vv = p.inc("s_v")
        p.em("vector", lambda e, s=vv[0]: e.tensor_add(
            dst[64:128, col:col + 512], ktmp[64:128, :], k2[64:128, :]).then_inc(s, 1))
        return vv[1]

    qkg = {"s": 0, "m": 0}

    def qknorm(src, dst_bf, dst_col, scale_ap, ps_ss, ps_bc):
        p.wait("vector", "s_s", qkg["s"])
        p.wait("vector", "s_m", qkg["m"])
        vv = p.inc("s_v")
        p.em("vector", lambda e, s=vv[0]: e.tensor_copy(xbb[:, :], src[:, :]).then_inc(s, 1))
        p.wait("scalar", "s_v", vv[1])
        sc = p.inc("s_s")
        p.em("scalar", lambda e, s=sc[0]: e.activation(
            out=sqb[:, :], in_=xbb[:, :], func=AF.Square).then_inc(s, 1))
        qkg["s"] = sc[1]
        p.wait("tensor", "s_s", sc[1])
        mm = p.inc("s_m")
        p.em("tensor", lambda e, ps_ss=ps_ss, s=mm[0]: e.matmul(
            ps_ss[0:1, :], ones_b[:, :], sqb[:, :], start=True, stop=True).then_inc(s, 1))
        nv, nt, nd = nrows["v"], nrows["t"], nrows["d"]
        p.wait("vector", "s_m", mm[1])
        p.em("vector", lambda e, ps_ss=ps_ss: e.tensor_copy(nv[0:1, :], ps_ss[0:1, :]))
        vv = p.inc("s_v")
        p.em("vector", lambda e, s=vv[0]: e.reciprocal(nt[0:1, :], nv[0:1, :]).then_inc(s, 1))
        p.wait("scalar", "s_v", vv[1])
        sc = p.inc("s_s")
        p.em("scalar", lambda e, s=sc[0]: e.activation(
            out=nd[0:1, :], in_=nt[0:1, :], func=AF.Sqrt).then_inc(s, 1))
        p.wait("vector", "s_s", sc[1])
        p.em("vector", lambda e: e.tensor_mul(nt[0:1, :], nd[0:1, :], nd[0:1, :]))
        p.em("vector", lambda e: e.scalar_tensor_tensor(nt[0:1, :], nv[0:1, :], -0.5,
                                                        nt[0:1, :], ALU.mult, ALU.mult))
        p.em("vector", lambda e: e.tensor_scalar_add(nt[0:1, :], nt[0:1, :], 1.5))
        p.em("vector", lambda e: e.tensor_mul(nd[0:1, :], nd[0:1, :], nt[0:1, :]))
        p.em("vector", lambda e: e.tensor_mul(nd[0:1, :], nv[0:1, :], nd[0:1, :]))
        p.em("vector", lambda e: e.tensor_scalar_add(nd[0:1, :], nd[0:1, :], QK_EPS))
        vv = p.inc("s_v")
        p.em("vector", lambda e, s=vv[0]: e.reciprocal(
            nt[0:1, :], nd[0:1, :]).then_inc(s, 1))
        p.wait("tensor", "s_v", vv[1])
        mm = p.inc("s_m")
        p.em("tensor", lambda e, ps_bc=ps_bc, s=mm[0]: e.matmul(
            ps_bc[:, :], ones_r1[0:1, :], nt[0:1, :], start=True, stop=True).then_inc(s, 1))
        qkg["m"] = mm[1]
        p.wait("vector", "s_m", mm[1])
        vv = p.inc("s_v")
        p.em("vector", lambda e, ps_bc=ps_bc, s=vv[0]: e.scalar_tensor_tensor(
            dst_bf[:, dst_col:dst_col + 512], src[:, :], scale_ap,
            ps_bc[:, :], ALU.mult, ALU.mult).then_inc(s, 1))
        return vv[1]

    p.wait("tensor", "s_c", kv_cast_done)
    p.wait("tensor", "s_v", xln_done_v)
    for g in range(NKV):
        for tt in range(4):
            t0 = tt * 512
            for fc in range(FC):
                p.em("tensor", lambda e, fc=fc, g=g, t0=t0: e.matmul(
                    pq[:, 0:512], wkb[:, fc * 256 + g * 128: fc * 256 + g * 128 + 128],
                    xlnb[:, fc * S + t0: fc * S + t0 + 512],
                    start=(fc == 0), stop=(fc == FC - 1)))
            for fc in range(FC):
                last = fc == FC - 1
                if last:
                    mmv = p.inc("s_m")
                    p.em("tensor", lambda e, fc=fc, g=g, t0=t0, s=mmv[0]: e.matmul(
                        pq[:, 512:1024], wvb[:, fc * 256 + g * 128: fc * 256 + g * 128 + 128],
                        xlnb[:, fc * S + t0: fc * S + t0 + 512],
                        start=(fc == 0), stop=True).then_inc(s, 1))
                else:
                    p.em("tensor", lambda e, fc=fc, g=g, t0=t0: e.matmul(
                        pq[:, 512:1024], wvb[:, fc * 256 + g * 128: fc * 256 + g * 128 + 128],
                        xlnb[:, fc * S + t0: fc * S + t0 + 512],
                        start=(fc == 0), stop=False))
            kv_mm = p.cnt["s_m"]
            p.wait("vector", "s_m", kv_mm)
            p.em("vector", lambda e: e.tensor_copy(kr[:, :], pq[:, 0:512]))
            vv = p.inc("s_v")
            p.em("vector", lambda e, s=vv[0]: e.tensor_copy(vf[:, :], pq[:, 512:1024]).then_inc(s, 1))
            evac_v = vv[1]
            if layer == 0:
                rope(kr, t0, kf, 0)
                qknorm(kf, Kb, g * S + t0, ksb[:, 0:1], pl, pO)
            else:
                rope(kr, t0, Kb, g * S + t0)
            p.wait("tensor", "s_v", evac_v)
            for st in range(4):
                mmv = p.inc("s_m")
                p.em("tensor", lambda e, st=st, s=mmv[0]: e.transpose(
                    pA[:, 0:128], vf[:, st * 128:(st + 1) * 128], idf[:, :]).then_inc(s, 1))
                p.wait("vector", "s_m", mmv[1])
                ci = tt * 4 + st
                vv = p.inc("s_v")
                p.em("vector", lambda e, ci=ci, g=g, s=vv[0]: e.tensor_copy(
                    Vtok[:, ci * 256 + g * 128: ci * 256 + g * 128 + 128],
                    pA[:, 0:128]).then_inc(s, 1))
                p.wait("tensor", "s_v", vv[1])
            p.wait("tensor", "s_v", p.cnt["s_v"])

    sc_tanh = SCALE / CAP
    qmm_after = [0, 0]
    qb_release = [0, 0]
    wq_cast_done = [0, 0]
    stg = {"free": kv_cast_done}
    prep_guard = [0]

    def emit_wq_load(h):
        hb = h % 2
        p.wait("gpsimd", "s_m", qmm_after[hb])
        for half in range(2):
            p.wait("sync", "s_c", stg["free"])
            for fi in range(8):
                fc = half * 8 + fi
                p.dma("sync", stage[:, fi * 128:(fi + 1) * 128],
                      wq[fc * 128:(fc + 1) * 128, h * 128:(h + 1) * 128], "s_w")
            p.wait("gpsimd", "s_w", p.cnt["s_w"])
            cc = p.inc("s_c")
            p.em("gpsimd", lambda e, hb=hb, half=half, s=cc[0]: e.tensor_copy(
                wqb[:, hb * 2048 + half * 1024: hb * 2048 + (half + 1) * 1024],
                stage[:, :]).then_inc(s, 1))
            stg["free"] = cc[1]
        wq_cast_done[hb] = p.cnt["s_c"]

    def emit_qprep_tt(h, tt):
        hb = h % 2
        t0 = tt * 512
        bq = (tt % 2) * 512
        if tt == 0:
            p.wait("tensor", "s_c", wq_cast_done[hb])
            p.wait("vector", "s_m", qb_release[hb])
        for fc in range(FC):
            last = fc == FC - 1
            if last:
                mmv = p.inc("s_m")
                p.em("tensor", lambda e, fc=fc, hb=hb, t0=t0, bq=bq, s=mmv[0]: e.matmul(
                    pq[:, bq:bq + 512], wqb[:, hb * 2048 + fc * 128: hb * 2048 + fc * 128 + 128],
                    xlnb[:, fc * S + t0: fc * S + t0 + 512],
                    start=(fc == 0), stop=True).then_inc(s, 1))
            else:
                p.em("tensor", lambda e, fc=fc, hb=hb, t0=t0, bq=bq: e.matmul(
                    pq[:, bq:bq + 512], wqb[:, hb * 2048 + fc * 128: hb * 2048 + fc * 128 + 128],
                    xlnb[:, fc * S + t0: fc * S + t0 + 512],
                    start=(fc == 0), stop=False))
        p.wait("vector", "s_m", p.cnt["s_m"])
        vv = p.inc("s_v")
        p.em("vector", lambda e, bq=bq, s=vv[0]: e.tensor_copy(
            kr[:, :], pq[:, bq:bq + 512]).then_inc(s, 1))
        if layer == 0:
            rope(kr, t0, kf, 0)
            prep_guard[0] = qknorm(kf, qb, hb * S + t0, qsb[:, 0:1], pl, pO)
        else:
            rope(kr, t0, qb, hb * S + t0)
        p.wait("tensor", "s_v", vv[1])
        if tt == 3:
            qmm_after[hb] = p.cnt["s_m"]

    def emit_attn_qg(h, qg):
        hb = h % 2
        g = h // 4
        q0 = qg * 512
        cs = [c for c in range(16) if (qg, c) in sched]
        ncs = len(cs)
        npair = (ncs + 1) // 2
        pair_mm = {}

        p.wait("tensor", "s_v", prep_guard[0])

        def emit_st_pair(j):
            pair = pA if j % 2 == 0 else pB
            lastk = 1 if 2 * j + 1 < ncs else 0
            for k in range(lastk + 1):
                ci = 2 * j + k
                c = cs[ci]
                if k == lastk:
                    mmv = p.inc("s_m")
                    p.em("tensor", lambda e, c=c, g=g, hb=hb, q0=q0, k=k, pair=pair, s=mmv[0]:
                         e.matmul(pair[:, k * 512:(k + 1) * 512],
                                  Kb[:, g * S + c * 128: g * S + c * 128 + 128],
                                  qb[:, hb * S + q0: hb * S + q0 + 512],
                                  start=True, stop=True).then_inc(s, 1))
                    pair_mm[j] = mmv[1]
                else:
                    p.em("tensor", lambda e, c=c, g=g, hb=hb, q0=q0, k=k, pair=pair:
                         e.matmul(pair[:, k * 512:(k + 1) * 512],
                                  Kb[:, g * S + c * 128: g * S + c * 128 + 128],
                                  qb[:, hb * S + q0: hb * S + q0 + 512],
                                  start=True, stop=True))

        emit_st_pair(0)
        if npair > 1:
            emit_st_pair(1)
        for j in range(npair):
            pair = pA if j % 2 == 0 else pB
            pt0 = (j % 2) * 1024
            w = 1024 if 2 * j + 1 < ncs else 512
            p.wait("scalar", "s_m", pair_mm[j])
            if layer == 0:
                p.em("scalar", lambda e, pair=pair, w=w: e.activation(
                    out=ttile[:, 0:w], in_=pair[:, 0:w], func=AF.Tanh, scale=sc_tanh))
                sc2 = p.inc("s_s")
                p.em("scalar", lambda e, pt0=pt0, w=w, s=sc2[0]: e.activation(
                    out=ptile[:, pt0:pt0 + w], in_=ttile[:, 0:w],
                    func=AF.Exp, scale=CAP, bias=biasb[:, 0:1]).then_inc(s, 1))
            else:
                sc2 = p.inc("s_s")
                p.em("scalar", lambda e, pair=pair, pt0=pt0, w=w, s=sc2[0]: e.activation(
                    out=ptile[:, pt0:pt0 + w], in_=pair[:, 0:w],
                    func=AF.Exp, scale=SCALE, bias=biasb[:, 0:1]).then_inc(s, 1))
            for k in range(2):
                ci = 2 * j + k
                if ci >= ncs:
                    break
                c = cs[ci]
                mi = sched[(qg, c)]
                if mi is not None:
                    p.wait("vector", "s_m", p.cnt["s_m"])
                    p.wait("vector", "s_s", sc2[1])
                    vv = p.inc("s_v")
                    p.em("vector", lambda e, pt0=pt0, k=k, mi=mi, s=vv[0]: e.tensor_mul(
                        pmk[:, k * 512:(k + 1) * 512],
                        ptile[:, pt0 + k * 512:pt0 + (k + 1) * 512],
                        maskb[:, mi * 512:(mi + 1) * 512]).then_inc(s, 1))
                    p.wait("tensor", "s_v", vv[1])
                    srcap = pmk[:, k * 512:(k + 1) * 512]
                else:
                    p.wait("tensor", "s_s", sc2[1])
                    srcap = ptile[:, pt0 + k * 512:pt0 + (k + 1) * 512]
                p.em("tensor", lambda e, ci=ci, ncs=ncs, srcap=srcap: e.matmul(
                    pl[0:1, :], ones_b[:, :], srcap,
                    start=(ci == 0), stop=(ci == ncs - 1)))
                mmv = p.inc("s_m")
                p.em("tensor", lambda e, ci=ci, ncs=ncs, c=c, g=g, srcap=srcap, s=mmv[0]:
                     e.matmul(pO[:, :], Vtok[:, c * 256 + g * 128: c * 256 + g * 128 + 128],
                              srcap, start=(ci == 0), stop=(ci == ncs - 1)).then_inc(s, 1))
            if j + 2 < npair:
                emit_st_pair(j + 2)
                p.wait("scalar", "s_m", p.cnt["s_m"])
        attn_mm = p.cnt["s_m"]
        p.wait("vector", "s_m", attn_mm)
        vv = p.inc("s_v")
        p.em("vector", lambda e, s=vv[0]: e.reciprocal(
            rows["l"][0:1, :], pl[0:1, :]).then_inc(s, 1))
        p.wait("tensor", "s_v", vv[1])
        mmv = p.inc("s_m")
        p.em("tensor", lambda e, s=mmv[0]: e.matmul(
            pl[:, :], ones_r1[0:1, :], rows["l"][0:1, :],
            start=True, stop=True).then_inc(s, 1))
        p.wait("vector", "s_m", mmv[1])
        p.em("vector", lambda e: e.tensor_copy(blb[:, :], pl[:, :]))
        ob = qg % 2
        if p.cnt[f"s_o{ob}"] > 0:
            p.wait("vector", f"s_o{ob}", p.cnt[f"s_o{ob}"])
        vv = p.inc("s_v")
        p.em("vector", lambda e, ob=ob, s=vv[0]: e.tensor_mul(
            obuf[:, ob * 512:(ob + 1) * 512], pO[:, :], blb[:, :]).then_inc(s, 1))
        p.wait("gpsimd", "s_v", vv[1])
        p.dma("gpsimd", ot[h * 128:(h + 1) * 128, q0:q0 + 512],
              obuf[:, ob * 512:(ob + 1) * 512], f"s_o{ob}")
        p.wait("tensor", "s_v", p.cnt["s_v"])

    emit_wq_load(0)
    for tt in range(4):
        emit_qprep_tt(0, tt)
    for h in range(NH):
        if h + 1 < NH:
            emit_wq_load(h + 1)
        for qg in range(4):
            emit_attn_qg(h, qg)
            if h + 1 < NH:
                emit_qprep_tt(h + 1, qg)
        qb_release[h % 2] = p.cnt["s_m"]
    p.wait("gpsimd", "s_o0", p.cnt["s_o0"])
    p.wait("gpsimd", "s_o1", p.cnt["s_o1"])
    return p.build()



def build_a2():
    p = Prog("a2")
    nc = p.nc
    otf = nc.dram_tensor("otf", [D, TOK], BF16, kind="ExternalInput").ap()
    wo = nc.dram_tensor("wo", [D, D], F32, kind="ExternalInput").ap()
    xown = nc.dram_tensor("xown", [D, TOK], F32, kind="ExternalInput").ap()
    xo = nc.dram_tensor("xo", [D, TOK], F32, kind="ExternalOutput").ap()

    otb = p.sbuf("otb", [128, FC * TOK], BF16)
    stage = p.sbuf("stage", [128, 2 * 512], F32)
    wob = p.sbuf("wob", [128, 4 * FC * 512], BF16)
    sub = p.sbuf("sub", [128, FC * 512], F32)
    subb = p.sbuf("subb", [128, 512], BF16)
    sqb = p.sbuf("sqb", [128, 512], BF16)
    xin = p.sbuf("xin", [128, 2 * 512], F32)
    xout = p.sbuf("xout", [128, 2 * 512], F32)
    t1 = p.sbuf("t1", [128, 512], F32)
    a_bc = p.sbuf("a_bc", [128, 512], F32)
    b_bc = p.sbuf("b_bc", [128, 512], F32)
    ones_b = p.sbuf("ones_b", [128, 1], BF16)
    ones_r1 = p.sbuf("ones_r1", [1, 128], F32)
    rows = {k: p.sbuf("row_" + k, [1, 512], F32)
            for k in ("v", "mean", "t1", "rstd")}
    ps = [p.psum(f"ps{i}", [128, 512], F32) for i in range(6)]

    for s in ("s_ot", "s_p0", "s_p1", "s_x0", "s_x1", "s_o0", "s_o1",
              "s_c", "s_m", "s_v", "s_s"):
        p.sem(s)

    p.em("vector", lambda e: e.memset(ones_b[:, :], 1.0))
    vset = p.inc("s_v")
    p.em("vector", lambda e, s=vset[0]: e.memset(ones_r1[:, :], 1.0).then_inc(s, 1))
    p.wait("tensor", "s_v", vset[1])

    for fc in range(FC):
        p.dma("gpsimd", otb[:, fc * TOK:(fc + 1) * TOK],
              otf[fc * 128:(fc + 1) * 128, :], "s_ot")
    p.wait("tensor", "s_ot", p.cnt["s_ot"])

    cast_at = [0, 0]
    panel_ready = []
    for pi in range(4):
        for fc in range(FC):
            sbf = fc % 2
            p.wait("sync", "s_c", cast_at[sbf])
            wcur = p.dma("sync", stage[:, sbf * 512:(sbf + 1) * 512],
                         wo[fc * 128:(fc + 1) * 128, pi * 512:(pi + 1) * 512], f"s_p{sbf}")
            p.wait("gpsimd", f"s_p{sbf}", wcur)
            cc = p.inc("s_c")
            p.em("gpsimd", lambda e, fc=fc, pi=pi, sbf=sbf, s=cc[0]: e.tensor_copy(
                wob[:, pi * FC * 512 + fc * 512: pi * FC * 512 + (fc + 1) * 512],
                stage[:, sbf * 512:(sbf + 1) * 512]).then_inc(s, 1))
            cast_at[sbf] = cc[1]
        panel_ready.append(p.cnt["s_c"])

    for tt in range(2):
        t0 = tt * 512
        for pi in range(4):
            p.wait("tensor", "s_c", panel_ready[pi])
            for el in range(4):
                ec = pi * 4 + el
                bank = ec % 2
                for hdc in range(FC):
                    last = hdc == FC - 1
                    if last:
                        mmv = p.inc("s_m")
                        p.em("tensor", lambda e, hdc=hdc, pi=pi, el=el, t0=t0, bank=bank, s=mmv[0]:
                             e.matmul(ps[bank][:, :],
                                      wob[:, pi * FC * 512 + hdc * 512 + el * 128:
                                          pi * FC * 512 + hdc * 512 + el * 128 + 128],
                                      otb[:, hdc * TOK + t0: hdc * TOK + t0 + 512],
                                      start=(hdc == 0), stop=True).then_inc(s, 1))
                    else:
                        p.em("tensor", lambda e, hdc=hdc, pi=pi, el=el, t0=t0, bank=bank:
                             e.matmul(ps[bank][:, :],
                                      wob[:, pi * FC * 512 + hdc * 512 + el * 128:
                                          pi * FC * 512 + hdc * 512 + el * 128 + 128],
                                      otb[:, hdc * TOK + t0: hdc * TOK + t0 + 512],
                                      start=(hdc == 0), stop=False))
                p.wait("vector", "s_m", p.cnt["s_m"])
                p.em("vector", lambda e, ec=ec, bank=bank: e.tensor_copy(
                    sub[:, ec * 512:(ec + 1) * 512], ps[bank][:, :]))
                vv = p.inc("s_v")
                p.em("vector", lambda e, bank=bank, s=vv[0]: e.tensor_copy(
                    subb[:, :], ps[bank][:, :]).then_inc(s, 1))
                p.wait("scalar", "s_v", vv[1])
                sc = p.inc("s_s")
                p.em("scalar", lambda e, s=sc[0]: e.activation(
                    out=sqb[:, :], in_=subb[:, :], func=AF.Square).then_inc(s, 1))
                p.wait("tensor", "s_s", sc[1])
                p.em("tensor", lambda e, ec=ec: e.matmul(
                    ps[4][0:1, :], ones_b[:, :], subb[:, :],
                    start=(ec == 0), stop=(ec == FC - 1)))
                mmv = p.inc("s_m")
                p.em("tensor", lambda e, ec=ec, s=mmv[0]: e.matmul(
                    ps[5][0:1, :], ones_b[:, :], sqb[:, :],
                    start=(ec == 0), stop=(ec == FC - 1)).then_inc(s, 1))
                p.wait("vector", "s_m", p.cnt["s_m"])
        emit_ln_rows(p, rows, ps[4], ps[5], ps[2], a_bc, b_bc, ones_r1, D, p.cnt["s_m"])
        for fc in range(FC):
            xb_ = fc % 2
            xcur = p.dma("gpsimd", xin[:, xb_ * 512:(xb_ + 1) * 512],
                         xown[fc * 128:(fc + 1) * 128, t0:t0 + 512], f"s_x{xb_}")
            p.wait("vector", f"s_x{xb_}", xcur)
            p.em("vector", lambda e, fc=fc: e.tensor_add(
                t1[:, :], sub[:, fc * 512:(fc + 1) * 512], b_bc[:, :]))
            p.em("vector", lambda e: e.tensor_mul(t1[:, :], t1[:, :], a_bc[:, :]))
            if p.cnt[f"s_o{xb_}"] > 0:
                p.wait("vector", f"s_o{xb_}", p.cnt[f"s_o{xb_}"])
            vv = p.inc("s_v")
            p.em("vector", lambda e, xb_=xb_, s=vv[0]: e.tensor_add(
                xout[:, xb_ * 512:(xb_ + 1) * 512], t1[:, :],
                xin[:, xb_ * 512:(xb_ + 1) * 512]).then_inc(s, 1))
            p.wait("gpsimd", "s_v", vv[1])
            p.dma("gpsimd", xo[fc * 128:(fc + 1) * 128, t0:t0 + 512],
                  xout[:, xb_ * 512:(xb_ + 1) * 512], f"s_o{xb_}")
        p.wait("tensor", "s_v", p.cnt["s_v"])
    p.wait("gpsimd", "s_o0", p.cnt["s_o0"])
    p.wait("gpsimd", "s_o1", p.cnt["s_o1"])
    return p.build()



def build_ffn(layer):
    I = 8192 if layer == 0 else 5632
    WDT = I8 if layer == 0 else F32
    NI = I // 128
    NP = I // 256
    p = Prog(f"ffn_l{layer}")
    nc = p.nc
    xown = nc.dram_tensor("xown", [D, TOK], F32, kind="ExternalInput").ap()
    wg = nc.dram_tensor("wg", [D, I], WDT, kind="ExternalInput").ap()
    wu = nc.dram_tensor("wu", [D, I], WDT, kind="ExternalInput").ap()
    wd = nc.dram_tensor("wd", [I, D], WDT, kind="ExternalInput").ap()
    sg = nc.dram_tensor("sg", [128, NI], F32, kind="ExternalInput").ap()
    su = nc.dram_tensor("su", [128, NI], F32, kind="ExternalInput").ap()
    sd = nc.dram_tensor("sd", [128, FC], F32, kind="ExternalInput").ap()
    xo = nc.dram_tensor("xo", [D, TOK], F32, kind="ExternalOutput").ap()

    xlnb = p.sbuf("xlnb", [128, FC * TOK], BF16)
    fuse = p.sbuf("fuse", [128, NI * 512], BF16)
    GSLOT = 3 if layer == 0 else 2
    DSLOT = 2
    stageg = p.sbuf("stageg", [128, GSLOT * 1024], WDT)
    stageu = p.sbuf("stageu", [128, GSLOT * 1024], WDT)
    staged = p.sbuf("staged", [128, DSLOT * 2048], WDT)
    wgb = p.sbuf("wgb", [128, 2 * FC * 256], BF16)
    wub = p.sbuf("wub", [128, 2 * FC * 256], BF16)
    wdb = p.sbuf("wdb", [128, 2 * 2048], BF16)
    sub = p.sbuf("sub", [128, FC * 512], F32)
    sqb = p.sbuf("sqb", [128, 2 * 512], BF16)
    xbb = p.sbuf("xbb", [128, 2 * 512], BF16)
    xin = p.sbuf("xin", [128, 2 * 512], F32)
    gs = p.sbuf("gs", [128, 2 * 512], F32)
    us = p.sbuf("us", [128, 512], F32)
    t1 = us
    a_bc = p.sbuf("a_bc", [128, 512], F32)
    b_bc = p.sbuf("b_bc", [128, 512], F32)
    sgb = p.sbuf("sgb", [128, NI], F32)
    sub_s = p.sbuf("sub_s", [128, NI], F32)
    sdb = p.sbuf("sdb", [128, FC], F32)
    ones_b = p.sbuf("ones_b", [128, 1], BF16)
    ones_r1 = p.sbuf("ones_r1", [1, 128], F32)
    rows = {k: p.sbuf("row_" + k, [1, 512], F32)
            for k in ("v", "mean", "rstd")}
    rows["t1"] = a_bc
    ps = [p.psum(f"ps{i}", [128, 512], F32) for i in range(8)]

    for s in ("s_kc", "s_g0", "s_g1", "s_g2", "s_g3", "s_d0", "s_d1", "s_d2", "s_d3",
              "s_x0", "s_x1", "s_o0", "s_o1", "s_c", "s_cv", "s_m", "s_v", "s_s"):
        p.sem(s)

    p.em("vector", lambda e: e.memset(ones_b[:, :], 1.0))
    vset = p.inc("s_v")
    p.em("vector", lambda e, s=vset[0]: e.memset(ones_r1[:, :], 1.0).then_inc(s, 1))
    p.wait("tensor", "s_v", vset[1])

    p.dma("sync", sgb[:, :], sg, "s_kc")
    p.dma("sync", sub_s[:, :], su, "s_kc")
    sc_loaded = p.dma("sync", sdb[:, :], sd, "s_kc")
    p.wait("vector", "s_kc", sc_loaded)
    p.wait("scalar", "s_kc", sc_loaded)

    xin_free = [0, 0]
    stats_mm_at = [0, 0]

    def stats_tile(fc, src_builder, w=512):
        pb = fc % 2
        p.wait("vector", "s_m", stats_mm_at[pb])
        vv = p.inc("s_v")
        p.em("vector", lambda e, pb=pb, s=vv[0]: src_builder(
            e, xbb[:, pb * 512:pb * 512 + w]).then_inc(s, 1))
        p.wait("scalar", "s_v", vv[1])
        sc = p.inc("s_s")
        p.em("scalar", lambda e, pb=pb, s=sc[0]: e.activation(
            out=sqb[:, pb * 512:pb * 512 + w], in_=xbb[:, pb * 512:pb * 512 + w],
            func=AF.Square).then_inc(s, 1))
        p.wait("tensor", "s_s", sc[1])
        p.em("tensor", lambda e, fc=fc, pb=pb: e.matmul(
            ps[4][0:1, 0:w], ones_b[:, :], xbb[:, pb * 512:pb * 512 + w],
            start=(fc == 0), stop=(fc == FC - 1)))
        mmv = p.inc("s_m")
        p.em("tensor", lambda e, fc=fc, pb=pb, s=mmv[0]: e.matmul(
            ps[5][0:1, 0:w], ones_b[:, :], sqb[:, pb * 512:pb * 512 + w],
            start=(fc == 0), stop=(fc == FC - 1)).then_inc(s, 1))
        stats_mm_at[pb] = mmv[1]

    def ln_pass(t0):
        for fc in range(FC):
            xb_ = fc % 2
            p.wait("gpsimd", "s_v", xin_free[xb_])
            xcur = p.dma("gpsimd", xin[:, xb_ * 512:(xb_ + 1) * 512],
                         xown[fc * 128:(fc + 1) * 128, t0:t0 + 512], f"s_x{xb_}")
            p.wait("vector", f"s_x{xb_}", xcur)

            def mk(e, out, xb_=xb_):
                return e.tensor_copy(out, xin[:, xb_ * 512:(xb_ + 1) * 512])
            stats_tile(fc, mk)
            xin_free[xb_] = p.cnt["s_v"]
        emit_ln_rows(p, rows, ps[4], ps[5], ps[6], a_bc, b_bc, ones_r1, D, p.cnt["s_m"])
        for fc in range(FC):
            xb_ = fc % 2
            p.wait("gpsimd", "s_v", xin_free[xb_])
            xcur = p.dma("gpsimd", xin[:, xb_ * 512:(xb_ + 1) * 512],
                         xown[fc * 128:(fc + 1) * 128, t0:t0 + 512], f"s_x{xb_}")
            p.wait("vector", f"s_x{xb_}", xcur)
            p.em("vector", lambda e, xb_=xb_: e.tensor_add(
                t1[:, :], xin[:, xb_ * 512:(xb_ + 1) * 512], b_bc[:, :]))
            vv = p.inc("s_v")
            p.em("vector", lambda e, fc=fc, t0=t0, s=vv[0]: e.tensor_mul(
                xlnb[:, fc * TOK + t0: fc * TOK + t0 + 512], t1[:, :],
                a_bc[:, :]).then_inc(s, 1))
            xin_free[xb_] = vv[1]

    for tt in range(2):
        ln_pass(tt * 512)
    xln_done = p.cnt["s_v"]
    p.wait("tensor", "s_v", xln_done)

    stg_grp_at = [(0, 0)] * GSLOT
    panel_mm_at = [0, 0]
    panel_cast_done = [(0, 0), (0, 0)]
    fuse_done_at = {}
    std_cast_at = [None] * DSLOT
    down_mm_at = {}

    for tt in range(2):
        t0 = tt * 512
        for pi in range(NP):
            pb = pi % 2
            def emit_cast(gr, wait_val):
                sl = gr % GSLOT
                p.wait("scalar", f"s_g{sl}", wait_val)
                p.wait("vector", f"s_g{sl}", wait_val)
                if gr == 0:
                    p.wait("scalar", "s_m", panel_mm_at[pb])
                    p.wait("vector", "s_m", panel_mm_at[pb])
                ccg = p.inc("s_c")
                p.em("scalar", lambda e, gr=gr, pb=pb, sl=sl, s=ccg[0]: e.copy(
                    out=wgb[:, pb * FC * 256 + gr * 1024: pb * FC * 256 + (gr + 1) * 1024],
                    in_=stageg[:, sl * 1024:(sl + 1) * 1024]).then_inc(s, 1))
                cc = p.inc("s_cv")
                p.em("vector", lambda e, gr=gr, pb=pb, sl=sl, s=cc[0]: e.tensor_copy(
                    wub[:, pb * FC * 256 + gr * 1024: pb * FC * 256 + (gr + 1) * 1024],
                    stageu[:, sl * 1024:(sl + 1) * 1024]).then_inc(s, 1))
                stg_grp_at[sl] = (ccg[1], cc[1])
            for gr in range(4):
                sl = gr % GSLOT
                p.wait("sync", "s_c", stg_grp_at[sl][0])
                p.wait("sync", "s_cv", stg_grp_at[sl][1])
                p.dma("sync",
                      stageg[:, sl * 1024:(sl + 1) * 1024].rearrange("p (j c) -> p j c", j=4),
                      wg[gr * 512:(gr + 1) * 512, pi * 256:(pi + 1) * 256].rearrange(
                          "(j p) c -> p j c", p=128), f"s_g{sl}")
                wcur = p.dma("sync",
                             stageu[:, sl * 1024:(sl + 1) * 1024].rearrange("p (j c) -> p j c", j=4),
                             wu[gr * 512:(gr + 1) * 512, pi * 256:(pi + 1) * 256].rearrange(
                                 "(j p) c -> p j c", p=128), f"s_g{sl}")
                emit_cast(gr, wcur)
            panel_cast_done[pb] = (p.cnt["s_c"], p.cnt["s_cv"])
            p.wait("tensor", "s_c", panel_cast_done[pb][0])
            p.wait("tensor", "s_cv", panel_cast_done[pb][1])
            for il in range(2):
                ic = pi * 2 + il
                bk = 2 * (ic % 2)
                if ic >= 2:
                    p.wait("tensor", "s_v", fuse_done_at[ic - 2])
                for fc in range(FC):
                    p.em("tensor", lambda e, fc=fc, il=il, t0=t0, pb=pb, bk=bk: e.matmul(
                        ps[bk][:, :], wgb[:, pb * FC * 256 + fc * 256 + il * 128:
                                          pb * FC * 256 + fc * 256 + il * 128 + 128],
                        xlnb[:, fc * TOK + t0: fc * TOK + t0 + 512],
                        start=(fc == 0), stop=(fc == FC - 1)))
                for fc in range(FC):
                    last = fc == FC - 1
                    if last:
                        mmv = p.inc("s_m")
                        p.em("tensor", lambda e, fc=fc, il=il, t0=t0, pb=pb, bk=bk, s=mmv[0]:
                             e.matmul(ps[bk + 1][:, :],
                                      wub[:, pb * FC * 256 + fc * 256 + il * 128:
                                          pb * FC * 256 + fc * 256 + il * 128 + 128],
                                      xlnb[:, fc * TOK + t0: fc * TOK + t0 + 512],
                                      start=(fc == 0), stop=True).then_inc(s, 1))
                    else:
                        p.em("tensor", lambda e, fc=fc, il=il, t0=t0, pb=pb, bk=bk: e.matmul(
                            ps[bk + 1][:, :],
                            wub[:, pb * FC * 256 + fc * 256 + il * 128:
                                pb * FC * 256 + fc * 256 + il * 128 + 128],
                            xlnb[:, fc * TOK + t0: fc * TOK + t0 + 512],
                            start=(fc == 0), stop=False))
                gu_mm = p.cnt["s_m"]
                sb_ = ic % 2
                p.wait("scalar", "s_m", gu_mm)
                sc = p.inc("s_s")
                p.em("scalar", lambda e, ic=ic, bk=bk, sb_=sb_, s=sc[0]: e.activation(
                    out=gs[:, sb_ * 512:(sb_ + 1) * 512], in_=ps[bk][:, :], func=AF.Silu,
                    scale=sgb[:, ic:ic + 1]).then_inc(s, 1))
                p.wait("vector", "s_m", gu_mm)
                p.em("vector", lambda e, ic=ic, bk=bk: e.tensor_scalar(
                    us[:, :], ps[bk + 1][:, :],
                    sub_s[:, ic:ic + 1], None, ALU.mult))
                p.wait("vector", "s_s", sc[1])
                vv = p.inc("s_v")
                p.em("vector", lambda e, ic=ic, sb_=sb_, s=vv[0]: e.tensor_mul(
                    fuse[:, ic * 512:(ic + 1) * 512], gs[:, sb_ * 512:(sb_ + 1) * 512],
                    us[:, :]).then_inc(s, 1))
                fuse_done_at[ic] = vv[1]
            panel_mm_at[pb] = p.cnt["s_m"]
        fuse_done = p.cnt["s_v"]
        p.wait("tensor", "s_v", fuse_done)

        for hg in range(2):
            for ic in range(NI):
                dbuf = ic % DSLOT
                if std_cast_at[dbuf] is not None:
                    p.wait("sync", std_cast_at[dbuf][0], std_cast_at[dbuf][1])
                wcur = p.dma("sync", staged[:, dbuf * 2048:(dbuf + 1) * 2048],
                             wd[ic * 128:(ic + 1) * 128, :], f"s_d{dbuf}")
                ceng = "vector" if ic % 2 == 0 else "scalar"
                p.wait(ceng, f"s_d{dbuf}", wcur)
                key = (hg, ic - DSLOT)
                if key in down_mm_at:
                    p.wait(ceng, "s_m", down_mm_at[key])
                elif hg == 1 and ic < DSLOT:
                    p.wait(ceng, "s_m", down_mm_at[(0, NI - DSLOT + ic)])
                csem = "s_cv" if ceng == "vector" else "s_c"
                cc = p.inc(csem)
                if ceng == "vector":
                    p.em("vector", lambda e, dbuf=dbuf, s=cc[0]: e.tensor_copy(
                        wdb[:, dbuf * 2048:(dbuf + 1) * 2048],
                        staged[:, dbuf * 2048:(dbuf + 1) * 2048]).then_inc(s, 1))
                else:
                    p.em("scalar", lambda e, dbuf=dbuf, s=cc[0]: e.copy(
                        out=wdb[:, dbuf * 2048:(dbuf + 1) * 2048],
                        in_=staged[:, dbuf * 2048:(dbuf + 1) * 2048]).then_inc(s, 1))
                std_cast_at[dbuf] = (csem, cc[1])
                p.wait("tensor", csem, cc[1])
                for hl in range(8):
                    hc = hg * 8 + hl
                    last = ic == NI - 1
                    if hl == 7:
                        mmv = p.inc("s_m")
                        p.em("tensor", lambda e, dbuf=dbuf, hc=hc, hl=hl, ic=ic, last=last, s=mmv[0]:
                             e.matmul(ps[hl][:, :],
                                      wdb[:, dbuf * 2048 + hc * 128: dbuf * 2048 + hc * 128 + 128],
                                      fuse[:, ic * 512:(ic + 1) * 512],
                                      start=(ic == 0), stop=last).then_inc(s, 1))
                    else:
                        p.em("tensor", lambda e, dbuf=dbuf, hc=hc, hl=hl, ic=ic, last=last:
                             e.matmul(ps[hl][:, :],
                                      wdb[:, dbuf * 2048 + hc * 128: dbuf * 2048 + hc * 128 + 128],
                                      fuse[:, ic * 512:(ic + 1) * 512],
                                      start=(ic == 0), stop=last))
                down_mm_at[(hg, ic)] = p.cnt["s_m"]
            p.wait("scalar", "s_m", p.cnt["s_m"])
            for hl in range(8):
                hc = hg * 8 + hl
                sc = p.inc("s_s")
                p.em("scalar", lambda e, hl=hl, hc=hc, s=sc[0]: e.activation(
                    out=sub[:, hc * 512:(hc + 1) * 512], in_=ps[hl][:, :],
                    func=AF.Copy, scale=sdb[:, hc:hc + 1]).then_inc(s, 1))
            p.wait("tensor", "s_s", p.cnt["s_s"])
        evac_done = p.cnt["s_s"]

        p.wait("vector", "s_s", evac_done)
        for hc in range(FC):
            def mk(e, out, hc=hc):
                return e.tensor_copy(out, sub[:, hc * 512:(hc + 1) * 512])
            stats_tile(hc, mk)
        emit_ln_rows(p, rows, ps[4], ps[5], ps[6], a_bc, b_bc, ones_r1, D, p.cnt["s_m"])
        for fc in range(FC):
            xb_ = fc % 2
            p.wait("gpsimd", "s_v", xin_free[xb_])
            xcur = p.dma("gpsimd", xin[:, xb_ * 512:(xb_ + 1) * 512],
                         xown[fc * 128:(fc + 1) * 128, t0:t0 + 512], f"s_x{xb_}")
            p.wait("vector", f"s_x{xb_}", xcur)
            p.em("vector", lambda e, fc=fc: e.tensor_add(
                t1[:, :], sub[:, fc * 512:(fc + 1) * 512], b_bc[:, :]))
            p.em("vector", lambda e: e.tensor_mul(t1[:, :], t1[:, :], a_bc[:, :]))
            vv = p.inc("s_v")
            p.em("vector", lambda e, fc=fc, xb_=xb_, s=vv[0]: e.tensor_add(
                sub[:, fc * 512:(fc + 1) * 512], t1[:, :],
                xin[:, xb_ * 512:(xb_ + 1) * 512]).then_inc(s, 1))
            xin_free[xb_] = vv[1]
            p.wait("gpsimd", "s_v", vv[1])
            p.dma("gpsimd", xo[fc * 128:(fc + 1) * 128, t0:t0 + 512],
                  sub[:, fc * 512:(fc + 1) * 512], "s_o0")
        p.wait("scalar", "s_o0", p.cnt["s_o0"])
    p.wait("gpsimd", "s_o0", p.cnt["s_o0"])
    return p.build()



def build_a2f(layer):
    I = 8192 if layer == 0 else 5632
    WDT = I8 if layer == 0 else F32
    NI = I // 128
    NP = I // 256
    GSLOT = 3 if layer == 0 else 2
    DSLOT = 2
    p = Prog(f"a2f_l{layer}")
    nc = p.nc
    otf = nc.dram_tensor("otf", [D, TOK], BF16, kind="ExternalInput").ap()
    wo = nc.dram_tensor("wo", [D, D], F32, kind="ExternalInput").ap()
    xown = nc.dram_tensor("xown", [D, TOK], F32, kind="ExternalInput").ap()
    wg = nc.dram_tensor("wg", [D, I], WDT, kind="ExternalInput").ap()
    wu = nc.dram_tensor("wu", [D, I], WDT, kind="ExternalInput").ap()
    wd = nc.dram_tensor("wd", [I, D], WDT, kind="ExternalInput").ap()
    sg = nc.dram_tensor("sg", [128, NI], F32, kind="ExternalInput").ap()
    su = nc.dram_tensor("su", [128, NI], F32, kind="ExternalInput").ap()
    sd = nc.dram_tensor("sd", [128, FC], F32, kind="ExternalInput").ap()
    xo = nc.dram_tensor("xo", [D, TOK], F32, kind="ExternalOutput").ap()
    x1 = nc.dram_tensor("x1", [D, TOK], F32).ap()

    xout = p.sbuf("xout", [128, 2 * 512], F32)
    xbb = p.sbuf("xbb", [128, 2 * 512], BF16)
    sqb = p.sbuf("sqb", [128, 2 * 512], BF16)
    xin = p.sbuf("xin", [128, 2 * 512], F32)
    t1 = p.sbuf("t1", [128, 512], F32)
    a_bc = p.sbuf("a_bc", [128, 512], F32)
    b_bc = p.sbuf("b_bc", [128, 512], F32)
    sgb = p.sbuf("sgb", [128, NI], F32)
    sub_s = p.sbuf("sub_s", [128, NI], F32)
    sdb = p.sbuf("sdb", [128, FC], F32)
    ones_b = p.sbuf("ones_b", [128, 1], BF16)
    ones_r1 = p.sbuf("ones_r1", [1, 128], F32)
    rows = {k: p.sbuf("row_" + k, [1, 512], F32) for k in ("v", "mean", "rstd")}
    rows["t1"] = a_bc
    ps = [p.psum(f"ps{i}", [128, 512], F32) for i in range(8)]

    for s in ("s_kc", "s_ot", "s_p0", "s_p1", "s_g0", "s_g1", "s_g2", "s_g3",
              "s_d0", "s_d1", "s_x0", "s_x1", "s_o0", "s_o1",
              "s_c", "s_cv", "s_m", "s_v", "s_s"):
        p.sem(s)

    p.em("vector", lambda e: e.memset(ones_b[:, :], 1.0))
    vset = p.inc("s_v")
    p.em("vector", lambda e, s=vset[0]: e.memset(ones_r1[:, :], 1.0).then_inc(s, 1))
    p.wait("tensor", "s_v", vset[1])

    p.dma("sync", sgb[:, :], sg, "s_kc")
    p.dma("sync", sub_s[:, :], su, "s_kc")
    sc_loaded = p.dma("sync", sdb[:, :], sd, "s_kc")
    p.wait("vector", "s_kc", sc_loaded)
    p.wait("scalar", "s_kc", sc_loaded)

    stats_mm_at = [0, 0]

    def stats_tile(fc, src_builder, w=512):
        pb = fc % 2
        p.wait("vector", "s_m", stats_mm_at[pb])
        vv = p.inc("s_v")
        p.em("vector", lambda e, pb=pb, s=vv[0]: src_builder(
            e, xbb[:, pb * 512:pb * 512 + w]).then_inc(s, 1))
        p.wait("scalar", "s_v", vv[1])
        sc = p.inc("s_s")
        p.em("scalar", lambda e, pb=pb, s=sc[0]: e.activation(
            out=sqb[:, pb * 512:pb * 512 + w], in_=xbb[:, pb * 512:pb * 512 + w],
            func=AF.Square).then_inc(s, 1))
        p.wait("tensor", "s_s", sc[1])
        p.em("tensor", lambda e, fc=fc, pb=pb: e.matmul(
            ps[4][0:1, 0:w], ones_b[:, :], xbb[:, pb * 512:pb * 512 + w],
            start=(fc == 0), stop=(fc == FC - 1)))
        mmv = p.inc("s_m")
        p.em("tensor", lambda e, fc=fc, pb=pb, s=mmv[0]: e.matmul(
            ps[5][0:1, 0:w], ones_b[:, :], sqb[:, pb * 512:pb * 512 + w],
            start=(fc == 0), stop=(fc == FC - 1)).then_inc(s, 1))
        stats_mm_at[pb] = mmv[1]

    p.open_scope()
    otb = p.sbuf("otb", [128, FC * TOK], BF16)
    stage = p.sbuf("stage", [128, 2 * 512], F32)
    wob = p.sbuf("wob", [128, 4 * FC * 512], BF16)
    sub = p.sbuf("sub", [128, FC * 512], F32)

    for fc in range(FC):
        p.dma("gpsimd", otb[:, fc * TOK:(fc + 1) * TOK],
              otf[fc * 128:(fc + 1) * 128, :], "s_ot")
    p.wait("tensor", "s_ot", p.cnt["s_ot"])

    cast_at = [0, 0]
    panel_ready = []
    for pi in range(4):
        for fc in range(FC):
            sbf = fc % 2
            p.wait("sync", "s_c", cast_at[sbf])
            wcur = p.dma("sync", stage[:, sbf * 512:(sbf + 1) * 512],
                         wo[fc * 128:(fc + 1) * 128, pi * 512:(pi + 1) * 512], f"s_p{sbf}")
            p.wait("gpsimd", f"s_p{sbf}", wcur)
            cc = p.inc("s_c")
            p.em("gpsimd", lambda e, fc=fc, pi=pi, sbf=sbf, s=cc[0]: e.tensor_copy(
                wob[:, pi * FC * 512 + fc * 512: pi * FC * 512 + (fc + 1) * 512],
                stage[:, sbf * 512:(sbf + 1) * 512]).then_inc(s, 1))
            cast_at[sbf] = cc[1]
        panel_ready.append(p.cnt["s_c"])

    for tt in range(2):
        t0 = tt * 512
        for pi in range(4):
            p.wait("tensor", "s_c", panel_ready[pi])
            for el in range(4):
                ec = pi * 4 + el
                bank = ec % 2
                for hdc in range(FC):
                    last = hdc == FC - 1
                    if last:
                        mmv = p.inc("s_m")
                        p.em("tensor", lambda e, hdc=hdc, pi=pi, el=el, t0=t0, bank=bank, s=mmv[0]:
                             e.matmul(ps[bank][:, :],
                                      wob[:, pi * FC * 512 + hdc * 512 + el * 128:
                                          pi * FC * 512 + hdc * 512 + el * 128 + 128],
                                      otb[:, hdc * TOK + t0: hdc * TOK + t0 + 512],
                                      start=(hdc == 0), stop=True).then_inc(s, 1))
                    else:
                        p.em("tensor", lambda e, hdc=hdc, pi=pi, el=el, t0=t0, bank=bank:
                             e.matmul(ps[bank][:, :],
                                      wob[:, pi * FC * 512 + hdc * 512 + el * 128:
                                          pi * FC * 512 + hdc * 512 + el * 128 + 128],
                                      otb[:, hdc * TOK + t0: hdc * TOK + t0 + 512],
                                      start=(hdc == 0), stop=False))
                p.wait("vector", "s_m", p.cnt["s_m"])
                evv = p.inc("s_v")
                p.em("vector", lambda e, ec=ec, bank=bank, s=evv[0]: e.tensor_copy(
                    sub[:, ec * 512:(ec + 1) * 512], ps[bank][:, :]).then_inc(s, 1))

                def mk(e, out, ec=ec):
                    return e.tensor_copy(out, sub[:, ec * 512:(ec + 1) * 512])
                stats_tile(ec, mk)
        emit_ln_rows(p, rows, ps[4], ps[5], ps[6], a_bc, b_bc, ones_r1, D, p.cnt["s_m"])
        for fc in range(FC):
            xb_ = fc % 2
            xcur = p.dma("gpsimd", xin[:, xb_ * 512:(xb_ + 1) * 512],
                         xown[fc * 128:(fc + 1) * 128, t0:t0 + 512], f"s_x{xb_}")
            p.wait("vector", f"s_x{xb_}", xcur)
            p.em("vector", lambda e, fc=fc: e.tensor_add(
                t1[:, :], sub[:, fc * 512:(fc + 1) * 512], b_bc[:, :]))
            p.em("vector", lambda e: e.tensor_mul(t1[:, :], t1[:, :], a_bc[:, :]))
            if p.cnt["s_o1"] > 0:
                p.wait("vector", "s_o1", p.cnt["s_o1"])
            vv = p.inc("s_v")
            p.em("vector", lambda e, fc=fc, xb_=xb_, s=vv[0]: e.tensor_add(
                xout[:, xb_ * 512:(xb_ + 1) * 512], t1[:, :],
                xin[:, xb_ * 512:(xb_ + 1) * 512]).then_inc(s, 1))
            p.wait("gpsimd", "s_v", vv[1])
            p.dma("gpsimd", x1[fc * 128:(fc + 1) * 128, t0:t0 + 512],
                  xout[:, xb_ * 512:(xb_ + 1) * 512], "s_o1")
        p.wait("tensor", "s_v", p.cnt["s_v"])
    p.barrier(dma_sems=("s_ot", "s_p0", "s_p1", "s_x0", "s_x1", "s_o1"))
    p.close_scope()

    p.open_scope()
    xlnb = p.sbuf("xlnb", [128, FC * 512], BF16)
    fuse = p.sbuf("fuse", [128, NI * 512], BF16)
    stageg = p.sbuf("stageg", [128, GSLOT * 1024], WDT)
    stageu = p.sbuf("stageu", [128, GSLOT * 1024], WDT)
    staged = p.sbuf("staged", [128, DSLOT * 2048], WDT)
    wgb = p.sbuf("wgb", [128, 2 * FC * 256], BF16)
    wub = p.sbuf("wub", [128, 2 * FC * 256], BF16)
    wdb = p.sbuf("wdb", [128, DSLOT * 2048], BF16)
    subf = p.sbuf("subf", [128, FC * 512], F32)
    gs = p.sbuf("gs", [128, 2 * 512], F32)
    us = p.sbuf("us", [128, 512], F32)

    stg_grp_at = [(0, 0)] * GSLOT
    panel_mm_at = [0, 0]
    panel_cast_done = [(0, 0), (0, 0)]
    fuse_done_at = {}
    std_cast_at = [None] * DSLOT
    down_mm_at = {}
    xln_at = [0]

    xin_free = [0, 0]

    def ln_from_x1(t0, dst):
        for fc in range(FC):
            xb_ = fc % 2
            p.wait("gpsimd", "s_v", xin_free[xb_])
            xcur = p.dma("gpsimd", xin[:, xb_ * 512:(xb_ + 1) * 512],
                         x1[fc * 128:(fc + 1) * 128, t0:t0 + 512], f"s_x{xb_}")
            p.wait("vector", f"s_x{xb_}", xcur)

            def mk(e, out, xb_=xb_):
                return e.tensor_copy(out, xin[:, xb_ * 512:(xb_ + 1) * 512])
            stats_tile(fc, mk)
            xin_free[xb_] = p.cnt["s_v"]
        emit_ln_rows(p, rows, ps[4], ps[5], ps[6], a_bc, b_bc, ones_r1, D, p.cnt["s_m"])
        for fc in range(FC):
            xb_ = fc % 2
            p.wait("gpsimd", "s_v", xin_free[xb_])
            xcur = p.dma("gpsimd", xin[:, xb_ * 512:(xb_ + 1) * 512],
                         x1[fc * 128:(fc + 1) * 128, t0:t0 + 512], f"s_x{xb_}")
            p.wait("vector", f"s_x{xb_}", xcur)
            p.em("vector", lambda e, xb_=xb_: e.tensor_add(
                t1[:, :], xin[:, xb_ * 512:(xb_ + 1) * 512], b_bc[:, :]))
            vv = p.inc("s_v")
            p.em("vector", lambda e, fc=fc, s=vv[0]: e.tensor_mul(
                dst[:, fc * 512:(fc + 1) * 512], t1[:, :], a_bc[:, :]).then_inc(s, 1))
            xin_free[xb_] = vv[1]
        xln_at[0] = p.cnt["s_v"]

    for tt in range(2):
        t0 = tt * 512
        ln_from_x1(t0, xlnb)
        p.wait("tensor", "s_v", xln_at[0])
        for pi in range(NP):
            pb = pi % 2

            def emit_cast(gr, wait_val):
                sl = gr % GSLOT
                p.wait("scalar", f"s_g{sl}", wait_val)
                p.wait("vector", f"s_g{sl}", wait_val)
                if gr == 0:
                    p.wait("scalar", "s_m", panel_mm_at[pb])
                    p.wait("vector", "s_m", panel_mm_at[pb])
                ccg = p.inc("s_c")
                p.em("scalar", lambda e, gr=gr, pb=pb, sl=sl, s=ccg[0]: e.copy(
                    out=wgb[:, pb * FC * 256 + gr * 1024: pb * FC * 256 + (gr + 1) * 1024],
                    in_=stageg[:, sl * 1024:(sl + 1) * 1024]).then_inc(s, 1))
                cc = p.inc("s_cv")
                p.em("vector", lambda e, gr=gr, pb=pb, sl=sl, s=cc[0]: e.tensor_copy(
                    wub[:, pb * FC * 256 + gr * 1024: pb * FC * 256 + (gr + 1) * 1024],
                    stageu[:, sl * 1024:(sl + 1) * 1024]).then_inc(s, 1))
                stg_grp_at[sl] = (ccg[1], cc[1])

            for gr in range(4):
                sl = gr % GSLOT
                p.wait("sync", "s_c", stg_grp_at[sl][0])
                p.wait("sync", "s_cv", stg_grp_at[sl][1])
                p.dma("sync",
                      stageg[:, sl * 1024:(sl + 1) * 1024].rearrange("p (j c) -> p j c", j=4),
                      wg[gr * 512:(gr + 1) * 512, pi * 256:(pi + 1) * 256].rearrange(
                          "(j p) c -> p j c", p=128), f"s_g{sl}")
                wcur = p.dma("sync",
                             stageu[:, sl * 1024:(sl + 1) * 1024].rearrange("p (j c) -> p j c", j=4),
                             wu[gr * 512:(gr + 1) * 512, pi * 256:(pi + 1) * 256].rearrange(
                                 "(j p) c -> p j c", p=128), f"s_g{sl}")
                emit_cast(gr, wcur)
            panel_cast_done[pb] = (p.cnt["s_c"], p.cnt["s_cv"])
            p.wait("tensor", "s_c", panel_cast_done[pb][0])
            p.wait("tensor", "s_cv", panel_cast_done[pb][1])
            for il in range(2):
                ic = pi * 2 + il
                bk = 2 * (ic % 2)
                if ic >= 2:
                    p.wait("tensor", "s_v", fuse_done_at[ic - 2])
                for fc in range(FC):
                    p.em("tensor", lambda e, fc=fc, il=il, pb=pb, bk=bk: e.matmul(
                        ps[bk][:, :], wgb[:, pb * FC * 256 + fc * 256 + il * 128:
                                          pb * FC * 256 + fc * 256 + il * 128 + 128],
                        xlnb[:, fc * 512:(fc + 1) * 512],
                        start=(fc == 0), stop=(fc == FC - 1)))
                for fc in range(FC):
                    last = fc == FC - 1
                    if last:
                        mmv = p.inc("s_m")
                        p.em("tensor", lambda e, fc=fc, il=il, pb=pb, bk=bk, s=mmv[0]:
                             e.matmul(ps[bk + 1][:, :],
                                      wub[:, pb * FC * 256 + fc * 256 + il * 128:
                                          pb * FC * 256 + fc * 256 + il * 128 + 128],
                                      xlnb[:, fc * 512:(fc + 1) * 512],
                                      start=(fc == 0), stop=True).then_inc(s, 1))
                    else:
                        p.em("tensor", lambda e, fc=fc, il=il, pb=pb, bk=bk: e.matmul(
                            ps[bk + 1][:, :],
                            wub[:, pb * FC * 256 + fc * 256 + il * 128:
                                pb * FC * 256 + fc * 256 + il * 128 + 128],
                            xlnb[:, fc * 512:(fc + 1) * 512],
                            start=(fc == 0), stop=False))
                gu_mm = p.cnt["s_m"]
                sb_ = ic % 2
                p.wait("scalar", "s_m", gu_mm)
                sc = p.inc("s_s")
                p.em("scalar", lambda e, ic=ic, bk=bk, sb_=sb_, s=sc[0]: e.activation(
                    out=gs[:, sb_ * 512:(sb_ + 1) * 512], in_=ps[bk][:, :], func=AF.Silu,
                    scale=sgb[:, ic:ic + 1]).then_inc(s, 1))
                p.wait("vector", "s_m", gu_mm)
                p.em("vector", lambda e, ic=ic, bk=bk: e.tensor_scalar(
                    us[:, :], ps[bk + 1][:, :], sub_s[:, ic:ic + 1], None, ALU.mult))
                p.wait("vector", "s_s", sc[1])
                vv = p.inc("s_v")
                p.em("vector", lambda e, ic=ic, sb_=sb_, s=vv[0]: e.tensor_mul(
                    fuse[:, ic * 512:(ic + 1) * 512], gs[:, sb_ * 512:(sb_ + 1) * 512],
                    us[:, :]).then_inc(s, 1))
                fuse_done_at[ic] = vv[1]
            panel_mm_at[pb] = p.cnt["s_m"]
        fuse_done = p.cnt["s_v"]
        p.wait("tensor", "s_v", fuse_done)

        for hg in range(2):
            for ic in range(NI):
                dbuf = ic % DSLOT
                if std_cast_at[dbuf] is not None:
                    p.wait("sync", std_cast_at[dbuf][0], std_cast_at[dbuf][1])
                wcur = p.dma("sync", staged[:, dbuf * 2048:(dbuf + 1) * 2048],
                             wd[ic * 128:(ic + 1) * 128, :], f"s_d{dbuf}")
                ceng = "vector" if ic % 2 == 0 else "scalar"
                p.wait(ceng, f"s_d{dbuf}", wcur)
                key = (hg, ic - DSLOT)
                if key in down_mm_at:
                    p.wait(ceng, "s_m", down_mm_at[key])
                elif hg == 1 and ic < DSLOT:
                    p.wait(ceng, "s_m", down_mm_at[(0, NI - DSLOT + ic)])
                csem = "s_cv" if ceng == "vector" else "s_c"
                cc = p.inc(csem)
                if ceng == "vector":
                    p.em("vector", lambda e, dbuf=dbuf, s=cc[0]: e.tensor_copy(
                        wdb[:, dbuf * 2048:(dbuf + 1) * 2048],
                        staged[:, dbuf * 2048:(dbuf + 1) * 2048]).then_inc(s, 1))
                else:
                    p.em("scalar", lambda e, dbuf=dbuf, s=cc[0]: e.copy(
                        out=wdb[:, dbuf * 2048:(dbuf + 1) * 2048],
                        in_=staged[:, dbuf * 2048:(dbuf + 1) * 2048]).then_inc(s, 1))
                std_cast_at[dbuf] = (csem, cc[1])
                p.wait("tensor", csem, cc[1])
                for hl in range(8):
                    hc = hg * 8 + hl
                    last = ic == NI - 1
                    if hl == 7:
                        mmv = p.inc("s_m")
                        p.em("tensor", lambda e, dbuf=dbuf, hc=hc, ic=ic, last=last, s=mmv[0]:
                             e.matmul(ps[hl][:, :],
                                      wdb[:, dbuf * 2048 + hc * 128: dbuf * 2048 + hc * 128 + 128],
                                      fuse[:, ic * 512:(ic + 1) * 512],
                                      start=(ic == 0), stop=last).then_inc(s, 1))
                    else:
                        p.em("tensor", lambda e, dbuf=dbuf, hc=hc, hl=hl, ic=ic, last=last:
                             e.matmul(ps[hl][:, :],
                                      wdb[:, dbuf * 2048 + hc * 128: dbuf * 2048 + hc * 128 + 128],
                                      fuse[:, ic * 512:(ic + 1) * 512],
                                      start=(ic == 0), stop=last))
                down_mm_at[(hg, ic)] = p.cnt["s_m"]
            p.wait("scalar", "s_m", p.cnt["s_m"])
            for hl in range(8):
                hc = hg * 8 + hl
                sc = p.inc("s_s")
                p.em("scalar", lambda e, hl=hl, hc=hc, s=sc[0]: e.activation(
                    out=subf[:, hc * 512:(hc + 1) * 512], in_=ps[hl][:, :],
                    func=AF.Copy, scale=sdb[:, hc:hc + 1]).then_inc(s, 1))
            p.wait("tensor", "s_s", p.cnt["s_s"])
        evac_done = p.cnt["s_s"]

        p.wait("vector", "s_s", evac_done)
        for hc in range(FC):
            def mk2(e, out, hc=hc):
                return e.tensor_copy(out, subf[:, hc * 512:(hc + 1) * 512])
            stats_tile(hc, mk2)
        emit_ln_rows(p, rows, ps[4], ps[5], ps[6], a_bc, b_bc, ones_r1, D, p.cnt["s_m"])
        for fc in range(FC):
            xb_ = fc % 2
            p.wait("gpsimd", "s_v", xin_free[xb_])
            xcur = p.dma("gpsimd", xin[:, xb_ * 512:(xb_ + 1) * 512],
                         x1[fc * 128:(fc + 1) * 128, t0:t0 + 512], f"s_x{xb_}")
            p.wait("vector", f"s_x{xb_}", xcur)
            p.em("vector", lambda e, fc=fc: e.tensor_add(
                t1[:, :], subf[:, fc * 512:(fc + 1) * 512], b_bc[:, :]))
            p.em("vector", lambda e: e.tensor_mul(t1[:, :], t1[:, :], a_bc[:, :]))
            vv = p.inc("s_v")
            p.em("vector", lambda e, fc=fc, xb_=xb_, s=vv[0]: e.tensor_add(
                subf[:, fc * 512:(fc + 1) * 512], t1[:, :],
                xin[:, xb_ * 512:(xb_ + 1) * 512]).then_inc(s, 1))
            xin_free[xb_] = vv[1]
            p.wait("gpsimd", "s_v", vv[1])
            p.dma("gpsimd", xo[fc * 128:(fc + 1) * 128, t0:t0 + 512],
                  subf[:, fc * 512:(fc + 1) * 512], "s_o0")
        p.wait("scalar", "s_o0", p.cnt["s_o0"])
        p.wait("vector", "s_m", p.cnt["s_m"])
    p.close_scope()
    p.wait("gpsimd", "s_o0", p.cnt["s_o0"])
    return p.build()



def _run(nc, in_maps):
    return run_bass_kernel_spmd(nc, in_maps, core_ids=list(range(8))).results


def _scale_2d(v, nchunk):
    return np.ascontiguousarray(v.reshape(nchunk, 128).T)


def run_a1(layer, xT, wq_, wk_, wv_, qs_, ks_):
    ropec, ropes = _rope_tables()
    identf = np.eye(128, dtype=np.float32)
    _, masks_ = _attn_sched(layer)
    nc1 = build_a1(layer)
    in_maps = []
    for c in range(8):
        b, r = c // 2, c % 2
        m = {"xT": xT[b],
             "wq": np.ascontiguousarray(wq_[:, r * 1024:(r + 1) * 1024]),
             "wk": np.ascontiguousarray(wk_[:, r * 256:(r + 1) * 256]),
             "wv": np.ascontiguousarray(wv_[:, r * 256:(r + 1) * 256]),
             "ropec": ropec, "ropes": ropes, "masks": masks_, "identf": identf}
        if layer == 0:
            m["qs"] = qs_
            m["ks"] = ks_
        in_maps.append(m)
    return [r_["ot"] for r_ in _run(nc1, in_maps)]


def run_a2(xT, ots, wo_):
    nc2 = build_a2()
    in_maps = []
    for c in range(8):
        b, r = c // 2, c % 2
        otf = np.concatenate([ots[2 * b][:, r * 1024:(r + 1) * 1024],
                              ots[2 * b + 1][:, r * 1024:(r + 1) * 1024]], axis=0)
        in_maps.append({"otf": np.ascontiguousarray(otf), "wo": wo_,
                        "xown": np.ascontiguousarray(xT[b][:, r * 1024:(r + 1) * 1024])})
    xos = [r_["xo"] for r_ in _run(nc2, in_maps)]
    return [np.concatenate([xos[2 * b], xos[2 * b + 1]], axis=1) for b in range(B)]


def attn_block(layer, xT, wq_, wk_, wv_, wo_, qs_, ks_):
    ots = run_a1(layer, xT, wq_, wk_, wv_, qs_, ks_)
    return run_a2(xT, ots, wo_)


def a2f_block(layer, xT, ots, wo_, wg_, wu_, wd_, sg_, su_, sd_):
    nc = build_a2f(layer)
    I = wg_.shape[1]
    in_maps = []
    for c in range(8):
        b, r = c // 2, c % 2
        otf = np.concatenate([ots[2 * b][:, r * 1024:(r + 1) * 1024],
                              ots[2 * b + 1][:, r * 1024:(r + 1) * 1024]], axis=0)
        in_maps.append({"otf": np.ascontiguousarray(otf), "wo": wo_,
                        "xown": np.ascontiguousarray(xT[b][:, r * 1024:(r + 1) * 1024]),
                        "wg": wg_, "wu": wu_, "wd": wd_,
                        "sg": _scale_2d(sg_, I // 128),
                        "su": _scale_2d(su_, I // 128),
                        "sd": _scale_2d(sd_, FC)})
    xos = [r_["xo"] for r_ in _run(nc, in_maps)]
    return [np.concatenate([xos[2 * b], xos[2 * b + 1]], axis=1) for b in range(B)]


def ffn_block(layer, xT, wg_, wu_, wd_, sg_, su_, sd_):
    nc = build_ffn(layer)
    I = wg_.shape[1]
    in_maps = []
    for c in range(8):
        b, r = c // 2, c % 2
        in_maps.append({"xown": np.ascontiguousarray(xT[b][:, r * 1024:(r + 1) * 1024]),
                        "wg": wg_, "wu": wu_, "wd": wd_,
                        "sg": _scale_2d(sg_, I // 128),
                        "su": _scale_2d(su_, I // 128),
                        "sd": _scale_2d(sd_, FC)})
    xos = [r_["xo"] for r_ in _run(nc, in_maps)]
    return [np.concatenate([xos[2 * b], xos[2 * b + 1]], axis=1) for b in range(B)]


def kernel(L0_attn_hidden_states, L0_attn_wq, L0_attn_wk, L0_attn_wv, L0_attn_wo,
           L0_attn_query_scale, L0_attn_key_scale,
           L0_ffn_w_gate, L0_ffn_w_up, L0_ffn_w_down,
           L0_ffn_s_gate, L0_ffn_s_up, L0_ffn_s_down,
           L1_attn_Wq, L1_attn_Wk, L1_attn_Wv, L1_attn_Wo,
           L1_ffn_gate_w, L1_ffn_up_w, L1_ffn_down_w,
           L1_ffn_gate_s, L1_ffn_up_s, L1_ffn_down_s):
    xT = [np.ascontiguousarray(L0_attn_hidden_states[b].T) for b in range(B)]
    ots = run_a1(0, xT, L0_attn_wq, L0_attn_wk, L0_attn_wv,
                 L0_attn_query_scale, L0_attn_key_scale)
    xT = a2f_block(0, xT, ots, L0_attn_wo, L0_ffn_w_gate, L0_ffn_w_up,
                   L0_ffn_w_down, L0_ffn_s_gate, L0_ffn_s_up, L0_ffn_s_down)
    ots = run_a1(1, xT, L1_attn_Wq, L1_attn_Wk, L1_attn_Wv, None, None)
    xT = a2f_block(1, xT, ots, L1_attn_Wo,
                   np.ascontiguousarray(L1_ffn_gate_w.T),
                   np.ascontiguousarray(L1_ffn_up_w.T),
                   np.ascontiguousarray(L1_ffn_down_w.T),
                   L1_ffn_gate_s, L1_ffn_up_s, L1_ffn_down_s)
    return np.stack([xT[b].T for b in range(B)]).astype(np.float32)
